# version 37
# baseline (speedup 1.0000x reference)
import contextlib
import numpy as np
import ml_dtypes
import concourse.bass as bass
import concourse.mybir as mybir
from concourse.bass_utils import run_bass_kernel_spmd


F32 = mybir.dt.float32
BF16 = mybir.dt.bfloat16
I32 = mybir.dt.int32
AF = mybir.ActivationFunctionType
ALU = mybir.AluOpType
AX = mybir.AxisListType

ENGINES = ("tensor", "vector", "scalar", "gpsimd", "sync")

N_SEQS = 2
STAGES = ("l0a", "ffn0", "l1a", "ffn1")
OP_LIMIT = None

NEG = -1.0e30
EPS_LN = 1e-5
EPS_NORM = 1e-7
QK_SCALE = 1.0 / 32.0
DQ = 1.0 / 127.0
A_SCALE = 0.125


class Prog:

    def __init__(self):
        self.ops = []
        self.lw = {}
        self.rd = {}
        self.slot_n = {}

    def op(self, eng, fn, r=(), w=(), slot=None):
        idx = len(self.ops)
        deps = set()
        for k in r:
            if k in self.lw:
                deps.add(self.lw[k])
        for k in w:
            if k in self.lw:
                deps.add(self.lw[k])
            deps |= self.rd.get(k, set())
        deps.discard(idx)
        for d in list(deps):
            pd = self.ops[d]
            if pd["slot"] is not None and pd["slot"] != slot:
                self.rd.setdefault(("slot", pd["slot"]), set()).add(idx)
        if slot is not None:
            deps |= self.rd.get(("slot", slot), set())
            deps.discard(idx)
            self.lw[("slot", slot)] = idx
            self.rd[("slot", slot)] = set()
        for k in r:
            self.rd.setdefault(k, set()).add(idx)
        for k in w:
            self.lw[k] = idx
            self.rd[k] = set()
        cum = None
        if slot is not None:
            self.slot_n[slot] = self.slot_n.get(slot, 0) + 1
            cum = self.slot_n[slot] * 16
        self.ops.append(dict(eng=eng, fn=fn, deps=sorted(deps), slot=slot,
                             cum=cum, idx=idx))
        return idx

    def dma(self, queue, fn, r=(), w=(), slot="d"):
        return self.op(queue, fn, r, w, slot=slot)

    def truncate(self, limit):
        self.ops = self.ops[:limit]
        self.slot_n = {}
        for o in self.ops:
            if o["slot"] is not None:
                self.slot_n[o["slot"]] = self.slot_n.get(o["slot"], 0) + 1
        kept = set(range(len(self.ops)))
        self.lw = {k: v for k, v in self.lw.items() if v in kept}
        self.rd = {k: (v & kept) for k, v in self.rd.items()}

    def build(self, nc, es, final_fences=()):
        ops = self.ops
        n = len(ops)
        need = [False] * n
        for o in ops:
            for d in o["deps"]:
                p = ops[d]
                if p["slot"] is None:
                    if p["eng"] == "tensor" and o["eng"] == "tensor":
                        continue
                    need[d] = True
        semval = [0] * n
        cnt = {e: 0 for e in ENGINES}
        for o in ops:
            if o["slot"] is None and need[o["idx"]]:
                cnt[o["eng"]] += 1
                semval[o["idx"]] = cnt[o["eng"]]
        sems = {}
        for e in ENGINES:
            if cnt.get(e, 0) > 0:
                sems[("e", e)] = es.enter_context(nc.semaphore(f"se_{e}"))
        for s in self.slot_n:
            sems[("s", s)] = es.enter_context(nc.semaphore(f"sd_{s}"))
        waits = [None] * n
        cov = {e: {} for e in ENGINES}
        run_slot = {}
        for o in ops:
            i = o["idx"]
            needc = {}
            for d in o["deps"]:
                p = ops[d]
                if p["slot"] is not None:
                    k = ("s", p["slot"])
                    v = run_slot.get(p["slot"], 0) * 16
                    if needc.get(k, -1) < v:
                        needc[k] = v
                elif need[d]:
                    if p["eng"] == "tensor" and o["eng"] == "tensor":
                        continue
                    k = ("e", p["eng"])
                    if needc.get(k, -1) < semval[d]:
                        needc[k] = semval[d]
            E = o["eng"]
            ww = []
            for k in sorted(needc, key=str):
                v = needc[k]
                if cov[E].get(k, -1) < v:
                    ww.append((k, v))
                    cov[E][k] = v
            waits[i] = ww
            if o["slot"] is not None:
                run_slot[o["slot"]] = run_slot.get(o["slot"], 0) + 1
        sched = {e: [] for e in ENGINES}
        for o in ops:
            sched[o["eng"]].append(o)
        block = es.enter_context(nc.Block())

        def mk(ename):
            def emitter(eng):
                for o in sched[ename]:
                    i = o["idx"]
                    for k, v in waits[i]:
                        eng.wait_ge(sems[k], v)
                    inst = o["fn"](eng)
                    if o["slot"] is not None:
                        inst.then_inc(sems[("s", o["slot"])], 16)
                    elif need[i]:
                        inst.then_inc(sems[("e", o["eng"])], 1)
                for fe, fslot in final_fences:
                    if fe == ename:
                        eng.wait_ge(sems[("s", fslot)],
                                    self.slot_n[fslot] * 16)
            return emitter

        block.tensor(mk("tensor"))
        block.vector(mk("vector"))
        block.scalar(mk("scalar"))
        block.gpsimd(mk("gpsimd"))
        block.sync(mk("sync"))


def segs(pfx, off, size, gran):
    return [(pfx, i) for i in range(off // gran, (off + size + gran - 1) // gran)]


def build_program():
    nc = bass.Bass("TRN2", target_bir_lowering=False)
    p = Prog()

    x_in = nc.dram_tensor("x_in", [4096, 1024], F32, kind="ExternalInput").ap()
    out = nc.dram_tensor("out", [4096, 1024], F32, kind="ExternalOutput").ap()
    wq0 = nc.dram_tensor("wq0", [1024, 1024], F32, kind="ExternalInput").ap()
    wk0 = nc.dram_tensor("wk0", [1024, 1024], F32, kind="ExternalInput").ap()
    wv0 = nc.dram_tensor("wv0", [1024, 1024], F32, kind="ExternalInput").ap()
    g0 = nc.dram_tensor("g0", [3072, 1024], I32, kind="ExternalInput").ap()
    u0 = nc.dram_tensor("u0", [3072, 1024], I32, kind="ExternalInput").ap()
    d0 = nc.dram_tensor("d0", [1024, 3072], I32, kind="ExternalInput").ap()
    wq1 = nc.dram_tensor("wq1", [1024, 1024], F32, kind="ExternalInput").ap()
    wk1 = nc.dram_tensor("wk1", [1024, 1024], F32, kind="ExternalInput").ap()
    wv1 = nc.dram_tensor("wv1", [1024, 1024], F32, kind="ExternalInput").ap()
    wkp1 = nc.dram_tensor("wkp1", [1024, 1024], F32, kind="ExternalInput").ap()
    wo1 = nc.dram_tensor("wo1", [1024, 1024], F32, kind="ExternalInput").ap()
    g1 = nc.dram_tensor("g1", [3072, 1024], I32, kind="ExternalInput").ap()
    u1 = nc.dram_tensor("u1", [3072, 1024], I32, kind="ExternalInput").ap()
    d1 = nc.dram_tensor("d1", [1024, 3072], I32, kind="ExternalInput").ap()
    gamma = nc.dram_tensor("gamma", [128, 1024], F32, kind="ExternalInput").ap()
    beta = nc.dram_tensor("beta", [128, 1024], F32, kind="ExternalInput").ap()
    identd = nc.dram_tensor("identd", [128, 128], BF16, kind="ExternalInput").ap()
    causald = nc.dram_tensor("causald", [128, 128], F32, kind="ExternalInput").ap()
    e1d = nc.dram_tensor("e1d", [32, 1024], BF16, kind="ExternalInput").ap()
    e2d = nc.dram_tensor("e2d", [32, 1024], BF16, kind="ExternalInput").ap()
    maskad = nc.dram_tensor("maskad", [128, 1024], F32, kind="ExternalInput").ap()

    s_att = {}
    for name, src in [("wq0", wq0), ("wk0", wk0), ("wv0", wv0),
                      ("wq1", wq1), ("wk1", wk1), ("wv1", wv1),
                      ("wkp1", wkp1), ("wo1", wo1)]:
        s_att[name] = nc.dram_tensor("s_" + name, [1024, 1024], BF16,
                                     kind="Internal").ap()
    s_ffn = {}
    for name, src in [("g0", g0), ("u0", u0), ("g1", g1), ("u1", u1)]:
        s_ffn[name] = nc.dram_tensor("s_" + name, [3072, 1024], BF16,
                                     kind="Internal").ap()
    for name, src in [("d0", d0), ("d1", d1)]:
        s_ffn[name] = nc.dram_tensor("s_" + name, [1024, 3072], BF16,
                                     kind="Internal").ap()
    wdt = [nc.dram_tensor(f"wdt{l}", [128, 24576], BF16, kind="Internal").ap()
           for l in range(2)]
    wgt = [nc.dram_tensor(f"wgt{l}", [128, 24576], BF16, kind="Internal").ap()
           for l in range(2)]
    wut = [nc.dram_tensor(f"wut{l}", [128, 24576], BF16, kind="Internal").ap()
           for l in range(2)]
    hs = [nc.dram_tensor(f"hs{g}", [128, 12288], BF16, kind="Internal").ap()
          for g in range(4)]

    es = contextlib.ExitStack()
    with es:
        AR_COLS = 55296
        A = es.enter_context(nc.sbuf_tensor("A", [128, AR_COLS], BF16))[:]
        X = es.enter_context(nc.sbuf_tensor("X", [128, 16384], F32))[:]
        FA = es.enter_context(nc.sbuf_tensor("FA", [128, 4096], F32))[:]
        GAM = es.enter_context(nc.sbuf_tensor("GAM", [128, 1024], F32))[:]
        BET = es.enter_context(nc.sbuf_tensor("BET", [128, 1024], F32))[:]
        IDN = es.enter_context(nc.sbuf_tensor("IDN", [128, 128], BF16))[:]
        CAU = es.enter_context(nc.sbuf_tensor("CAU", [128, 128], F32))[:]
        E1 = es.enter_context(nc.sbuf_tensor("E1", [32, 1024], BF16))[:]
        E2 = es.enter_context(nc.sbuf_tensor("E2", [32, 1024], BF16))[:]
        MSK4 = es.enter_context(nc.sbuf_tensor("MSK4", [128, 1024], F32))[:]
        BN1 = es.enter_context(nc.sbuf_tensor("BN1", [128, 12], F32))[:]
        BN2 = es.enter_context(nc.sbuf_tensor("BN2", [128, 12], F32))[:]
        STG = es.enter_context(nc.sbuf_tensor("STG", [128, 32], F32))[:]
        VB = es.enter_context(nc.sbuf_tensor("VB", [128, 64], F32))[:]
        QM = es.enter_context(nc.sbuf_tensor("QM", [128, 16], F32))[:]
        QR = es.enter_context(nc.sbuf_tensor("QR", [128, 16], F32))[:]
        QM2 = es.enter_context(nc.sbuf_tensor("QM2", [128, 16], F32))[:]
        QR2 = es.enter_context(nc.sbuf_tensor("QR2", [128, 16], F32))[:]
        TB = es.enter_context(nc.sbuf_tensor("TB", [128, 24], F32))[:]
        NRM = es.enter_context(nc.sbuf_tensor("NRM", [128, 48], F32))[:]
        RR16 = es.enter_context(nc.sbuf_tensor("RR16", [128, 16], F32))[:]
        PSA = es.enter_context(nc.psum_tensor("PSA", [128, 3072], F32))[:]
        PSB = es.enter_context(nc.psum_tensor("PSB", [128, 512], F32))[:]
        PST = es.enter_context(nc.psum_tensor("PST", [128, 1024], BF16))[:]

        AK = lambda off, size: segs("A", off, size, 512)
        FK = lambda off, size: segs("F", off, size, 512)
        PK = lambda b0, nb=1: [("P", b) for b in range(b0, b0 + nb)]
        TK = lambda c0, nc_=1: segs("T", c0, nc_, 1)
        VK = lambda c0, nc_=1: segs("V", c0, nc_, 1)

        p.dma("scalar", lambda e: e.dma_start(out=IDN, in_=identd[:, :]), w=[("c", "idn")], slot="cst")
        p.dma("scalar", lambda e: e.dma_start(out=CAU, in_=causald[:, :]), w=[("c", "cau")], slot="cst")
        p.dma("scalar", lambda e: e.dma_start(out=E1, in_=e1d[:, :]), w=[("c", "e1")], slot="cst")
        p.dma("scalar", lambda e: e.dma_start(out=E2, in_=e2d[:, :]), w=[("c", "e2")], slot="cst")
        p.dma("scalar", lambda e: e.dma_start(out=MSK4, in_=maskad[:, :]), w=[("c", "msk")], slot="cst")
        p.dma("scalar", lambda e: e.dma_start(out=GAM, in_=gamma[:, :]), w=[("c", "gam")], slot="cst")
        p.dma("scalar", lambda e: e.dma_start(out=BET, in_=beta[:, :]), w=[("c", "bet")], slot="cst")

        def stage(src_ap, dst_ap, srckey, dstkey, sslot):
            p.dma("gpsimd", lambda e, s=src_ap, d=dst_ap: e.dma_start(out=d[:, :], in_=s[:, :]),
                  r=[srckey], w=[dstkey], slot=sslot)

        stage(wk0, s_att["wk0"], ("i", "wk0"), ("w", "wk0"), "sw0")
        stage(wv0, s_att["wv0"], ("i", "wv0"), ("w", "wv0"), "sw0")

        def stage_rest():
            stage(wq0, s_att["wq0"], ("i", "wq0"), ("w", "wq0"), "sw0")
            stage(g0, s_ffn["g0"], ("i", "g0"), ("w", "g0"), "sw1")
            stage(u0, s_ffn["u0"], ("i", "u0"), ("w", "u0"), "sw1")
            stage(d0, s_ffn["d0"], ("i", "d0"), ("w", "d0"), "sw1")
            stage(wq1, s_att["wq1"], ("i", "wq1"), ("w", "wq1"), "sw2")
            stage(wk1, s_att["wk1"], ("i", "wk1"), ("w", "wk1"), "sw2")
            stage(wkp1, s_att["wkp1"], ("i", "wkp1"), ("w", "wkp1"), "sw2")
            stage(wv1, s_att["wv1"], ("i", "wv1"), ("w", "wv1"), "sw2")
            stage(wo1, s_att["wo1"], ("i", "wo1"), ("w", "wo1"), "sw2")
            stage(g1, s_ffn["g1"], ("i", "g1"), ("w", "g1"), "sw3")
            stage(u1, s_ffn["u1"], ("i", "u1"), ("w", "u1"), "sw3")
            stage(d1, s_ffn["d1"], ("i", "d1"), ("w", "d1"), "sw3")

        wslot_state = [0]

        def next_wslot():
            wslot_state[0] ^= 1
            return "wa" if wslot_state[0] else "wb"

        def ln_stats_batch2(src_aps, src_keys, bias_ap, rstd_ap, mkey, rkey,
                            var_scale=1.0, rho=None):
            nt = len(src_aps)
            for i in range(nt):
                sap = src_aps[i]
                bn = BN1 if i % 2 == 0 else BN2
                bkey = ("bn", i % 2)
                p.op("vector", lambda e, a=sap, b=bn: e.bn_stats(b[:, 0:6], a[:, 0:512]),
                     r=src_keys[i], w=[bkey])
                p.op("vector", lambda e, a=sap, b=bn: e.bn_stats(b[:, 6:12], a[:, 512:1024]),
                     r=src_keys[i], w=[bkey])
                p.op("vector", lambda e, b=bn, i=i: e.bn_aggr(STG[:, 2 * i:2 * i + 2], b),
                     r=[bkey], w=[("stg", i)])
            stgk = [("stg", i) for i in range(nt)]
            stg3 = STG[:, 0:2 * nt].rearrange("p (a b) -> p a b", b=2)
            p.op("vector", lambda e, nt=nt, s=stg3: e.tensor_scalar(
                out=VB[:, 0:nt], in0=s[:, :, 1], scalar1=var_scale, scalar2=EPS_LN,
                op0=ALU.mult, op1=ALU.add), r=stgk, w=VK(0, nt))
            p.op("scalar", lambda e, nt=nt: e.activation(
                out=VB[:, 16:16 + nt], in_=VB[:, 0:nt], func=AF.Ln),
                r=VK(0, nt), w=VK(16, nt))
            p.op("scalar", lambda e, nt=nt, rd=rstd_ap: e.activation(
                out=rd[:, 0:nt], in_=VB[:, 16:16 + nt], func=AF.Exp, scale=-0.5),
                r=VK(16, nt), w=[rkey])
            p.op("vector", lambda e, nt=nt, md=bias_ap, rd=rstd_ap, s=stg3: e.scalar_tensor_tensor(
                out=md[:, 0:nt], in0=s[:, :, 0], scalar=-1.0, in1=rd[:, 0:nt],
                op0=ALU.mult, op1=ALU.mult), r=stgk + [rkey], w=[mkey])
            if rho is not None:
                rho_bias, rho_ap, mkey2, rkey2 = rho
                p.op("vector", lambda e, nt=nt, s=stg3: e.tensor_scalar(
                    out=VB[:, 32:32 + nt], in0=s[:, :, 1],
                    scalar1=(1.0 + EPS_LN) * var_scale, scalar2=EPS_LN * EPS_LN,
                    op0=ALU.mult, op1=ALU.add), r=stgk, w=VK(32, nt))
                p.op("scalar", lambda e, nt=nt: e.activation(
                    out=VB[:, 48:48 + nt], in_=VB[:, 32:32 + nt], func=AF.Ln),
                    r=VK(32, nt), w=VK(48, nt))
                p.op("scalar", lambda e, nt=nt, rd=rho_ap: e.activation(
                    out=rd[:, 0:nt], in_=VB[:, 48:48 + nt], func=AF.Exp, scale=-0.5),
                    r=VK(48, nt), w=[rkey2])
                p.op("vector", lambda e, nt=nt, md=rho_bias, rd=rho_ap, s=stg3: e.scalar_tensor_tensor(
                    out=md[:, 0:nt], in0=s[:, :, 0], scalar=-1.0, in1=rd[:, 0:nt],
                    op0=ALU.mult, op1=ALU.mult), r=stgk + [rkey2], w=[mkey2])

        PSTB = PSB.bitcast(BF16)

        def transpose_tile_128(src_ap, src_keys, dst3, dst_keys, pst_base, bank=0,
                               evac="scalar"):
            PB = PST if bank == 0 else PSTB
            bkey = [("PT", 0)] if bank == 0 else [("PB", 0)]
            for cc in range(8):
                p.op("tensor", lambda e, s=src_ap, cc=cc, PB=PB: e.transpose(
                    PB[:, cc * 128:(cc + 1) * 128], s[:, cc * 128:cc * 128 + 128], IDN),
                    r=src_keys + [("c", "idn")], w=bkey)
            src3 = PB[:, 0:1024].rearrange("p (c t) -> p c t", t=128)
            if evac == "scalar":
                p.op("scalar", lambda e, s3=src3, d3=dst3: e.activation(
                    out=d3, in_=s3, func=AF.Copy),
                    r=bkey, w=dst_keys)
            else:
                p.op("vector", lambda e, s3=src3, d3=dst3: e.tensor_copy(d3, s3),
                    r=bkey, w=dst_keys)

        KT0 = 0
        V0 = 16384
        WA0 = 32768
        WB0 = 40960
        XLT0 = 49152
        XST0 = 53248
        XQ0 = 40960
        QTG = 43008
        PT0 = 47104
        P0 = 53248
        FXLT = 0
        FWGH = 16384
        FWUH = 28672
        FHTH = 40960
        FWD = 0
        FHG = 24576
        LW0 = 0
        LW1 = 8192
        LQH = 16384
        LKH = 17408
        LKPH = 18432
        LVH = 19456
        LXQT = 20480
        LQHT = 20736
        LZT = 20992
        LPHT = 21248
        LW2K = 21504
        LW2V = 29696
        LPH = 37888
        LZH = 38912
        LXQB = 39936

        FT0 = 0
        FT1 = 1024
        FT2 = 2048
        FT3 = 3072

        def load_transposed(scr_ap, wkey, dst_off, rows, cc_count, cols_per_cc,
                            slot, extra_r=()):
            for cc in range(cc_count):
                dsl = A[:, dst_off + cc * cols_per_cc: dst_off + (cc + 1) * cols_per_cc]
                src = scr_ap[0:rows, cc * 128:(cc + 1) * 128]
                p.dma("sync", lambda e, d=dsl, s=src: e.dma_start(out=d, in_=s, transpose=True),
                      r=[wkey] + list(extra_r), w=AK(dst_off + cc * cols_per_cc, cols_per_cc),
                      slot=slot)

        for s in range(N_SEQS):
            xoff = s * 2048

            for t in range(16):
                src = x_in[xoff + t * 128: xoff + (t + 1) * 128, :]
                p.dma("sync", lambda e, t=t, src=src: e.dma_start(
                    out=X[:, t * 1024:(t + 1) * 1024], in_=src),
                    r=[("i", "x")], w=[("x", t)], slot="xin")

            do_l0a = "l0a" in STAGES
            sl_k = next_wslot()
            load_transposed(s_att["wk0"], ("w", "wk0"), WA0, 1024, 8, 1024, sl_k)
            sl_v = next_wslot()
            load_transposed(s_att["wv0"], ("w", "wv0"), WB0, 1024, 8, 1024, sl_v)
            if s == 0:
                stage_rest()

            for g in range(4):
                tiles = [4 * g + tt for tt in range(4)]
                ln_stats_batch2(
                    [X[:, t * 1024:(t + 1) * 1024] for t in tiles],
                    [[("x", t)] for t in tiles],
                    QM[:, 4 * g:], QR[:, 4 * g:], ("qm", g), ("qr", g))
                for pair in range(2):
                    sts = []
                    for tt2 in range(2):
                        tt = pair * 2 + tt2
                        t = tiles[tt]
                        xst = A[:, XST0 + tt2 * 1024: XST0 + tt2 * 1024 + 1024]
                        stkey = AK(XST0 + tt2 * 1024, 1024)
                        sts.append((tt, xst, stkey))
                        p.op("scalar", lambda e, t=t, tt=tt, g=g, xst=xst: e.activation(
                            out=xst, in_=X[:, t * 1024:(t + 1) * 1024], func=AF.Identity,
                            scale=QR[:, 4 * g + tt:4 * g + tt + 1],
                            bias=QM[:, 4 * g + tt:4 * g + tt + 1]),
                            r=[("x", t), ("qm", g), ("qr", g)], w=stkey)
                    for tt, xst, stkey in sts:
                        xl = A[:, XLT0:XLT0 + 4096].rearrange("p (c w) -> p c w", w=512)
                        d3 = xl[:, :, tt * 128:tt * 128 + 128]
                        dk = [k for cc in range(8) for k in AK(XLT0 + cc * 512 + tt * 128, 128)]
                        transpose_tile_128(xst, stkey, d3, dk, 0, bank=tt % 2)
                xlk = AK(XLT0, 4096)
                for dc in range(8):
                    bank = dc % 6
                    for cc in range(8):
                        lhs = A[:, WA0 + cc * 1024 + dc * 128: WA0 + cc * 1024 + dc * 128 + 128]
                        rhs = A[:, XLT0 + cc * 512: XLT0 + (cc + 1) * 512]
                        p.op("tensor", lambda e, lhs=lhs, rhs=rhs, b=bank, cc=cc: e.matmul(
                            PSA[:, b * 512:(b + 1) * 512], lhs, rhs,
                            start=(cc == 0), stop=(cc == 7)),
                            r=AK(WA0, 8192) + xlk + ([] if cc else PK(bank)),
                            w=PK(bank))
                    dst = A[:, KT0 + dc * 2048 + g * 512: KT0 + dc * 2048 + (g + 1) * 512]
                    p.op("scalar", lambda e, d=dst, b=bank: e.activation(
                        out=d, in_=PSA[:, b * 512:(b + 1) * 512], func=AF.Copy),
                        r=PK(bank), w=AK(KT0 + dc * 2048 + g * 512, 512))
                for tsub in range(4):
                    t = 4 * g + tsub
                    for dh in range(2):
                        bank = (tsub * 2 + dh) % 6
                        for cc in range(8):
                            lhs = A[:, XLT0 + cc * 512 + tsub * 128: XLT0 + cc * 512 + tsub * 128 + 128]
                            rhs = A[:, WB0 + cc * 1024 + dh * 512: WB0 + cc * 1024 + (dh + 1) * 512]
                            p.op("tensor", lambda e, lhs=lhs, rhs=rhs, b=bank, cc=cc: e.matmul(
                                PSA[:, b * 512:(b + 1) * 512], lhs, rhs,
                                start=(cc == 0), stop=(cc == 7)),
                                r=AK(WB0, 8192) + xlk + ([] if cc else PK(bank)),
                                w=PK(bank))
                        dst = A[:, V0 + t * 1024 + dh * 512: V0 + t * 1024 + (dh + 1) * 512]
                        p.op("scalar", lambda e, d=dst, b=bank: e.activation(
                            out=d, in_=PSA[:, b * 512:(b + 1) * 512], func=AF.Copy),
                            r=PK(bank), w=AK(V0 + t * 1024 + dh * 512, 512))

            sl_q = next_wslot()
            load_transposed(s_att["wq0"], ("w", "wq0"), WA0, 1024, 8, 1024, sl_q)
            def build_qtg(g):
                for pair in range(2):
                    sts = []
                    for tt2 in range(2):
                        tt = pair * 2 + tt2
                        t = 4 * g + tt
                        xst = A[:, XQ0 + tt2 * 1024: XQ0 + tt2 * 1024 + 1024]
                        stkey = AK(XQ0 + tt2 * 1024, 1024)
                        sts.append((tt, xst, stkey))
                        p.op("scalar", lambda e, t=t, xst=xst: e.activation(
                            out=xst, in_=X[:, t * 1024:(t + 1) * 1024], func=AF.Identity,
                            scale=QR[:, t:t + 1], bias=QM[:, t:t + 1]),
                            r=[("x", t), ("qm", t // 4), ("qr", t // 4)], w=stkey)
                    for tt, xst, stkey in sts:
                        xl = A[:, XLT0:XLT0 + 4096].rearrange("p (c w) -> p c w", w=512)
                        d3 = xl[:, :, tt * 128:tt * 128 + 128]
                        dk = [k for cc in range(8) for k in AK(XLT0 + cc * 512 + tt * 128, 128)]
                        transpose_tile_128(xst, stkey, d3, dk, 0, bank=tt % 2)
                xlk = AK(XLT0, 4096)
                for dc in range(8):
                    bank = dc % 4
                    for cc in range(8):
                        lhs = A[:, WA0 + cc * 1024 + dc * 128: WA0 + cc * 1024 + dc * 128 + 128]
                        rhs = A[:, XLT0 + cc * 512: XLT0 + (cc + 1) * 512]
                        p.op("tensor", lambda e, lhs=lhs, rhs=rhs, b=bank, cc=cc: e.matmul(
                            PSA[:, b * 512:(b + 1) * 512], lhs, rhs,
                            start=(cc == 0), stop=(cc == 7)),
                            r=AK(WA0, 8192) + xlk + ([] if cc else PK(bank)),
                            w=PK(bank))
                    dst = A[:, QTG + dc * 512: QTG + (dc + 1) * 512]
                    p.op("scalar", lambda e, d=dst, b=bank: e.activation(
                        out=d, in_=PSA[:, b * 512:(b + 1) * 512], func=AF.Copy),
                        r=PK(bank), w=AK(QTG + dc * 512, 512))

            build_qtg(0)
            for qi in range(16):
                nk = 128 * (qi + 1)
                tt_q = qi % 4
                qtk = AK(QTG, 4096)
                nch = 0
                for c0 in range(0, nk - 128, 512):
                    w_ = min(512, (nk - 128) - c0)
                    bank = c0 // 512
                    for dc in range(8):
                        lhs = A[:, QTG + dc * 512 + tt_q * 128: QTG + dc * 512 + tt_q * 128 + 128]
                        rhs = A[:, KT0 + dc * 2048 + c0: KT0 + dc * 2048 + c0 + w_]
                        p.op("tensor", lambda e, lhs=lhs, rhs=rhs, c0=c0, w_=w_, dc=dc: e.matmul(
                            PSA[:, c0:c0 + w_], lhs, rhs,
                            start=(dc == 0), stop=(dc == 7)),
                            r=qtk + AK(KT0 + 0, 16384) + ([] if dc else PK(bank)),
                            w=PK(bank))
                    p.op("vector", lambda e, c0=c0, w_=w_, i=nch, bank=bank: e.reduce_max(
                        TB[:, 8 + i:9 + i], PSA[:, c0:c0 + w_], axis=AX.X),
                        r=PK(bank), w=TK(8 + nch))
                    nch += 1
                for dc in range(8):
                    lhs = A[:, QTG + dc * 512 + tt_q * 128: QTG + dc * 512 + tt_q * 128 + 128]
                    rhs = A[:, KT0 + dc * 2048 + nk - 128: KT0 + dc * 2048 + nk]
                    p.op("tensor", lambda e, lhs=lhs, rhs=rhs, dc=dc: e.matmul(
                        PSB[:, 0:128], lhs, rhs, start=(dc == 0), stop=(dc == 7)),
                        r=qtk + AK(KT0, 16384) + ([] if dc else [("PB", 0)]),
                        w=[("PB", 0)])
                dbank = (nk - 128) // 512
                p.op("vector", lambda e, nk=nk: e.tensor_tensor(
                    out=PSA[:, nk - 128:nk], in0=PSB[:, 0:128], in1=CAU, op=ALU.add),
                    r=[("PB", 0), ("c", "cau")], w=PK(dbank))
                p.op("vector", lambda e, nk=nk, i=nch, dbank=dbank: e.reduce_max(
                    TB[:, 8 + i:9 + i], PSA[:, nk - 128:nk], axis=AX.X),
                    r=PK(dbank), w=TK(8 + nch))
                nch += 1
                p.op("vector", lambda e, n=nch: e.reduce_max(
                    TB[:, 0:1], TB[:, 8:8 + n], axis=AX.X),
                    r=[k for i in range(nch) for k in TK(8 + i)], w=TK(0))
                p.op("vector", lambda e: e.tensor_scalar(
                    out=TB[:, 1:2], in0=TB[:, 0:1], scalar1=-QK_SCALE, scalar2=None,
                    op0=ALU.mult), r=TK(0), w=TK(1))
                nbanks = (nk + 511) // 512
                for b in range(nbanks):
                    c0 = b * 512
                    w_ = min(512, nk - c0)
                    p.op("scalar", lambda e, c0=c0, w_=w_, b=b: e.activation(
                        out=A[:, P0 + c0:P0 + c0 + w_], in_=PSA[:, c0:c0 + w_],
                        func=AF.Exp, scale=QK_SCALE, bias=TB[:, 1:2],
                        accum_out=TB[:, 12 + b:13 + b]),
                        r=PK(b) + TK(1), w=AK(P0 + c0, w_) + TK(12 + b))
                p.op("vector", lambda e, n=nbanks: e.reduce_sum(
                    TB[:, 2:3], TB[:, 12:12 + n], axis=AX.X),
                    r=[k for b in range(nbanks) for k in TK(12 + b)], w=TK(2))
                p.op("vector", lambda e: e.reciprocal(TB[:, 3:4], TB[:, 2:3]),
                     r=TK(2), w=TK(3))
                if tt_q == 3 and qi < 15:
                    build_qtg(qi // 4 + 1)
                for kc in range(qi + 1):
                    sl = kc % 2
                    PB_ = PST if sl == 0 else PSTB
                    bk_ = [("PT", 0)] if sl == 0 else [("PB", 0)]
                    p.op("tensor", lambda e, kc=kc, sl=sl, PB_=PB_: e.transpose(
                        PB_[:, sl * 128:(sl + 1) * 128],
                        A[:, P0 + kc * 128:P0 + (kc + 1) * 128], IDN),
                        r=AK(P0 + kc * 128, 128) + [("c", "idn")], w=bk_)
                    pts = A[:, PT0 + sl * 128:PT0 + (sl + 1) * 128]
                    if sl == 0:
                        p.op("scalar", lambda e, sl=sl, d=pts, PB_=PB_: e.activation(
                            out=d, in_=PB_[:, sl * 128:(sl + 1) * 128], func=AF.Copy),
                            r=bk_, w=AK(PT0 + sl * 128, 128))
                    else:
                        p.op("vector", lambda e, sl=sl, d=pts, PB_=PB_: e.tensor_copy(
                            d, PB_[:, sl * 128:(sl + 1) * 128]),
                            r=bk_, w=AK(PT0 + sl * 128, 128))
                    for dh in range(2):
                        rhs = A[:, V0 + kc * 1024 + dh * 512: V0 + kc * 1024 + (dh + 1) * 512]
                        p.op("tensor", lambda e, kc=kc, dh=dh, lhs=pts, rhs=rhs, qi=qi, sl=sl: e.matmul(
                            PSA[:, 2048 + dh * 512: 2048 + (dh + 1) * 512], lhs, rhs,
                            start=(kc == 0), stop=(kc == qi)),
                            r=AK(PT0 + sl * 128, 128) + AK(V0 + kc * 1024, 1024) +
                              ([] if kc else PK(4 + dh)),
                            w=PK(4 + dh))
                p.op("vector", lambda e: e.bn_stats(BN1[:, 0:6], PSA[:, 2048:2560]),
                     r=PK(4), w=[("bn", 0)])
                p.op("vector", lambda e: e.bn_stats(BN1[:, 6:12], PSA[:, 2560:3072]),
                     r=PK(5), w=[("bn", 0)])
                p.op("vector", lambda e: e.bn_aggr(TB[:, 4:6], BN1),
                     r=[("bn", 0)], w=TK(4, 2))
                p.op("vector", lambda e: e.tensor_mul(TB[:, 6:7], TB[:, 3:4], TB[:, 3:4]),
                     r=TK(3), w=TK(6))
                p.op("vector", lambda e: e.tensor_scalar(
                    out=TB[:, 16:17], in0=TB[:, 5:6], scalar1=TB[:, 6:7], scalar2=EPS_LN,
                    op0=ALU.mult, op1=ALU.add), r=TK(5) + TK(6), w=TK(16))
                p.op("scalar", lambda e: e.activation(
                    out=TB[:, 17:18], in_=TB[:, 16:17], func=AF.Ln), r=TK(16), w=TK(17))
                p.op("scalar", lambda e: e.activation(
                    out=TB[:, 18:19], in_=TB[:, 17:18], func=AF.Exp, scale=-0.5),
                    r=TK(17), w=TK(18))
                p.op("vector", lambda e: e.tensor_mul(TB[:, 19:20], TB[:, 3:4], TB[:, 18:19]),
                     r=TK(3) + TK(18), w=TK(19))
                p.op("vector", lambda e: e.scalar_tensor_tensor(
                    out=TB[:, 20:21], in0=TB[:, 4:5], scalar=-1.0, in1=TB[:, 19:20],
                    op0=ALU.mult, op1=ALU.mult), r=TK(4) + TK(19), w=TK(20))
                p.op("scalar", lambda e: e.activation(
                    out=FA[:, FT0:FT0 + 1024], in_=PSA[:, 2048:3072],
                    func=AF.Identity, scale=TB[:, 19:20], bias=TB[:, 20:21]),
                    r=PK(4, 2) + TK(19) + TK(20), w=FK(FT0, 1024))
                p.op("gpsimd", lambda e, qi=qi: e.tensor_tensor(
                    out=X[:, qi * 1024:(qi + 1) * 1024],
                    in0=FA[:, FT0:FT0 + 1024],
                    in1=X[:, qi * 1024:(qi + 1) * 1024], op=ALU.add),
                    r=FK(FT0, 1024) + [("x", qi)], w=[("x", qi)])

            def ffn_prep_group(layer, g):
                tiles = [4 * g + tt for tt in range(4)]
                ln_stats_batch2(
                    [X[:, t * 1024:(t + 1) * 1024] for t in tiles],
                    [[("x", t)] for t in tiles],
                    QM[:, 4 * g:], QR[:, 4 * g:], ("qm", g), ("qr", g))
                for pair in range(2):
                    sts = []
                    for tt2 in range(2):
                        t = tiles[pair * 2 + tt2]
                        xst = A[:, XST0 + tt2 * 1024: XST0 + tt2 * 1024 + 1024]
                        stkey = AK(XST0 + tt2 * 1024, 1024)
                        sts.append((t, xst, stkey))
                        p.op("scalar", lambda e, t=t, g=g, xst=xst: e.activation(
                            out=xst, in_=X[:, t * 1024:(t + 1) * 1024], func=AF.Identity,
                            scale=QR[:, t:t + 1], bias=QM[:, t:t + 1]),
                            r=[("x", t), ("qm", g), ("qr", g)], w=stkey)
                    for t, xst, stkey in sts:
                        xl = A[:, FXLT:FXLT + 16384].rearrange("p (c w) -> p c w", w=2048)
                        d3 = xl[:, :, t * 128:t * 128 + 128]
                        dk = [k for cc in range(8) for k in AK(FXLT + cc * 2048 + t * 128, 128)]
                        transpose_tile_128(xst, stkey, d3, dk, 0)

            def ffn_whalf_load(layer, half):
                sgp = s_ffn["g%d" % layer]
                sup = s_ffn["u%d" % layer]
                gk, uk = ("w", "g%d" % layer), ("w", "u%d" % layer)
                slg = next_wslot()
                slu = next_wslot()
                if s == 0:
                    for cc in range(8):
                        dsl = A[:, FWGH + cc * 1536: FWGH + (cc + 1) * 1536]
                        srcq = sgp[half * 1536:(half + 1) * 1536, cc * 128:(cc + 1) * 128]
                        p.dma("sync", lambda e, d=dsl, sr=srcq: e.dma_start(
                            out=d, in_=sr, transpose=True),
                            r=[gk], w=AK(FWGH + cc * 1536, 1536), slot=slg)
                        dsl2 = A[:, FWUH + cc * 1536: FWUH + (cc + 1) * 1536]
                        srcq2 = sup[half * 1536:(half + 1) * 1536, cc * 128:(cc + 1) * 128]
                        p.dma("sync", lambda e, d=dsl2, sr=srcq2: e.dma_start(
                            out=d, in_=sr, transpose=True),
                            r=[uk], w=AK(FWUH + cc * 1536, 1536), slot=slu)
                    p.dma("gpsimd", lambda e, layer=layer, half=half: e.dma_start(
                        out=wgt[layer][:, half * 12288:(half + 1) * 12288],
                        in_=A[:, FWGH:FWGH + 12288]),
                        r=AK(FWGH, 12288), w=[("wgt", layer, half)], slot="wds")
                    p.dma("gpsimd", lambda e, layer=layer, half=half: e.dma_start(
                        out=wut[layer][:, half * 12288:(half + 1) * 12288],
                        in_=A[:, FWUH:FWUH + 12288]),
                        r=AK(FWUH, 12288), w=[("wut", layer, half)], slot="wds")
                else:
                    p.dma("sync", lambda e, layer=layer, half=half: e.dma_start(
                        out=A[:, FWGH:FWGH + 12288],
                        in_=wgt[layer][:, half * 12288:(half + 1) * 12288]),
                        r=[("wgt", layer, half)], w=AK(FWGH, 12288), slot=slg)
                    p.dma("sync", lambda e, layer=layer, half=half: e.dma_start(
                        out=A[:, FWUH:FWUH + 12288],
                        in_=wut[layer][:, half * 12288:(half + 1) * 12288]),
                        r=[("wut", layer, half)], w=AK(FWUH, 12288), slot=slu)

            def ffn_pass1_group(layer, half, g):
                if True:
                    if True:
                        hbase = FHTH + (g % 2) * 6144
                        for hc in range(12):
                            i_ = half * 48 + g * 12 + hc
                            pg = (2 * i_) % 3
                            pu = (2 * i_ + 1) % 3
                            for wsel, wcol, pp in ((0, FWGH, pg), (1, FWUH, pu)):
                                for cc in range(8):
                                    lhs = A[:, wcol + cc * 1536 + hc * 128: wcol + cc * 1536 + hc * 128 + 128]
                                    rhs = A[:, FXLT + cc * 2048 + g * 512: FXLT + cc * 2048 + g * 512 + 512]
                                    p.op("tensor", lambda e, lhs=lhs, rhs=rhs, pp=pp, cc=cc: e.matmul(
                                        PSA[:, pp * 1024: pp * 1024 + 512], lhs, rhs,
                                        start=(cc == 0), stop=(cc == 7)),
                                        r=AK(wcol + cc * 1536 + hc * 128, 128) +
                                          AK(FXLT + cc * 2048 + g * 512, 512) +
                                          ([] if cc else PK(2 * pp)),
                                        w=PK(2 * pp))
                            sgo = FT3 + (hc % 2) * 512
                            p.op("scalar", lambda e, pg=pg, sgo=sgo: e.activation(
                                out=FA[:, sgo:sgo + 512], in_=PSA[:, pg * 1024: pg * 1024 + 512],
                                func=AF.Silu, scale=DQ),
                                r=PK(2 * pg), w=FK(sgo, 512))
                            hdst = A[:, hbase + hc * 512: hbase + (hc + 1) * 512]
                            p.op("vector", lambda e, pu=pu, d=hdst, sgo=sgo: e.scalar_tensor_tensor(
                                out=d, in0=PSA[:, pu * 1024: pu * 1024 + 512],
                                scalar=DQ, in1=FA[:, sgo:sgo + 512],
                                op0=ALU.mult, op1=ALU.mult),
                                r=PK(2 * pu) + FK(sgo, 512),
                                w=AK(hbase + hc * 512, 512))
                        hsrc = A[:, hbase:hbase + 6144]
                        hdst2 = hs[g][:, half * 6144:(half + 1) * 6144]
                        p.dma("gpsimd", lambda e, sr=hsrc, d=hdst2: e.dma_start(out=d, in_=sr),
                              r=AK(hbase, 6144), w=[("hs", g, half)], slot="ho")

            def ffn_layer2(layer, prep_done=False, half0_done=False,
                           post_group=None):
                sdp = s_ffn["d%d" % layer]
                dk_ = ("w", "d%d" % layer)
                if not prep_done:
                    for g in range(4):
                        ffn_prep_group(layer, g)
                if not prep_done:
                    ffn_whalf_load(layer, 0)
                if not half0_done:
                    for g in range(4):
                        ffn_pass1_group(layer, 0, g)
                ffn_whalf_load(layer, 1)
                for g in range(4):
                    ffn_pass1_group(layer, 1, g)
                for g in range(4):
                    hgbase = FHG + (g % 2) * 12288
                    for hv in range(2):
                        sld = ("hiA0", "hiA1", "hiB0", "hiB1")[(g % 2) * 2 + hv]
                        p.dma("sync", lambda e, g=g, hb=hgbase, hv=hv: e.dma_start(
                            out=A[:, hb + hv * 6144: hb + (hv + 1) * 6144],
                            in_=hs[g][:, hv * 6144:(hv + 1) * 6144]),
                            r=[("hs", g, hv)],
                            w=AK(hgbase + hv * 6144, 6144), slot=sld)
                    if g == 0:
                        slw = next_wslot()
                        slw2 = next_wslot()
                        if s == 0:
                            for hc in range(24):
                                dsl = A[:, FWD + hc * 1024: FWD + (hc + 1) * 1024]
                                srcw = sdp[0:1024, hc * 128:(hc + 1) * 128]
                                p.dma("sync", lambda e, d=dsl, sr=srcw: e.dma_start(
                                    out=d, in_=sr, transpose=True),
                                    r=[dk_], w=AK(FWD + hc * 1024, 1024),
                                    slot=(slw if hc % 2 == 0 else slw2))
                            p.dma("gpsimd", lambda e, layer=layer: e.dma_start(
                                out=wdt[layer][:, :], in_=A[:, FWD:FWD + 24576]),
                                r=AK(FWD, 24576), w=[("wdt", layer)], slot="wds")
                        else:
                            for q in range(4):
                                eng_ = "sync" if q < 2 else "gpsimd"
                                sl_ = (slw, slw2, "wdg", "wdg2")[q]
                                p.dma(eng_, lambda e, layer=layer, q=q: e.dma_start(
                                    out=A[:, FWD + q * 6144:FWD + (q + 1) * 6144],
                                    in_=wdt[layer][:, q * 6144:(q + 1) * 6144]),
                                    r=[("wdt", layer)], w=AK(FWD + q * 6144, 6144),
                                    slot=sl_)
                    for tsub in range(4):
                        t = g * 4 + tsub
                        pp = tsub % 3
                        FTd = FT0 if tsub % 2 == 0 else FT1
                        tbb_ = 4 if tsub % 2 == 0 else 12
                        bnn = BN1 if tsub % 2 == 0 else BN2
                        bnk = ("bn", tsub % 2)
                        for dh in range(2):
                            for hc in range(24):
                                lhs = A[:, hgbase + hc * 512 + tsub * 128: hgbase + hc * 512 + tsub * 128 + 128]
                                rhs = A[:, FWD + hc * 1024 + dh * 512: FWD + hc * 1024 + (dh + 1) * 512]
                                p.op("tensor", lambda e, lhs=lhs, rhs=rhs, pp=pp, dh=dh, hc=hc: e.matmul(
                                    PSA[:, pp * 1024 + dh * 512: pp * 1024 + (dh + 1) * 512],
                                    lhs, rhs, start=(hc == 0), stop=(hc == 23)),
                                    r=AK(hgbase + hc * 512, 512) +
                                      AK(FWD + hc * 1024 + dh * 512, 512) +
                                      ([] if hc else PK(2 * pp + dh)),
                                    w=PK(2 * pp + dh))
                        p.op("vector", lambda e, pp=pp, bnn=bnn: e.bn_stats(
                            bnn[:, 0:6], PSA[:, pp * 1024: pp * 1024 + 512]),
                            r=PK(2 * pp), w=[bnk])
                        p.op("vector", lambda e, pp=pp, bnn=bnn: e.bn_stats(
                            bnn[:, 6:12], PSA[:, pp * 1024 + 512: pp * 1024 + 1024]),
                            r=PK(2 * pp + 1), w=[bnk])
                        p.op("vector", lambda e, b=tbb_, bnn=bnn: e.bn_aggr(
                            TB[:, b:b + 2], bnn), r=[bnk], w=TK(tbb_, 2))
                        p.op("vector", lambda e, b=tbb_: e.tensor_scalar(
                            out=TB[:, b + 3:b + 4], in0=TB[:, b + 1:b + 2], scalar1=DQ * DQ,
                            scalar2=EPS_LN, op0=ALU.mult, op1=ALU.add),
                            r=TK(tbb_ + 1), w=TK(tbb_ + 3))
                        p.op("scalar", lambda e, b=tbb_: e.activation(
                            out=TB[:, b + 4:b + 5], in_=TB[:, b + 3:b + 4], func=AF.Ln),
                            r=TK(tbb_ + 3), w=TK(tbb_ + 4))
                        p.op("scalar", lambda e, b=tbb_: e.activation(
                            out=TB[:, b + 5:b + 6], in_=TB[:, b + 4:b + 5], func=AF.Exp, scale=-0.5),
                            r=TK(tbb_ + 4), w=TK(tbb_ + 5))
                        p.op("vector", lambda e, b=tbb_: e.tensor_scalar(
                            out=TB[:, b + 6:b + 7], in0=TB[:, b + 5:b + 6], scalar1=DQ,
                            scalar2=None, op0=ALU.mult), r=TK(tbb_ + 5), w=TK(tbb_ + 6))
                        p.op("vector", lambda e, pp=pp, b=tbb_, FTd=FTd: e.tensor_scalar(
                            out=FA[:, FTd:FTd + 1024], in0=PSA[:, pp * 1024: pp * 1024 + 1024],
                            scalar1=TB[:, b:b + 1], scalar2=TB[:, b + 6:b + 7],
                            op0=ALU.subtract, op1=ALU.mult),
                            r=PK(2 * pp, 2) + TK(tbb_) + TK(tbb_ + 6), w=FK(FTd, 1024))
                        p.op("gpsimd", lambda e, t=t, FTd=FTd: e.tensor_tensor(
                            out=X[:, t * 1024:(t + 1) * 1024],
                            in0=FA[:, FTd:FTd + 1024],
                            in1=X[:, t * 1024:(t + 1) * 1024], op=ALU.add),
                            r=FK(FTd, 1024) + [("x", t)], w=[("x", t)])
                    if post_group is not None:
                        post_group(g)

            def l1a_stats_group(g_):
                tiles = [4 * g_ + tt for tt in range(4)]
                ln_stats_batch2(
                    [X[:, t * 1024:(t + 1) * 1024] for t in tiles],
                    [[("x", t)] for t in tiles],
                    QM[:, 4 * g_:], QR[:, 4 * g_:], ("qm", g_), ("qr", g_),
                    rho=(QM2[:, 4 * g_:], QR2[:, 4 * g_:], ("qm2", g_), ("qr2", g_)))

            ffn_layer2(0, post_group=l1a_stats_group)

            p.op("scalar", lambda e: e.activation(
                out=FA[:, FT2:FT2 + 1024], in_=X[:, 15 * 1024:16 * 1024],
                func=AF.Identity, scale=QR[:, 15:16], bias=QM[:, 15:16]),
                r=[("x", 15), ("qm", 3), ("qr", 3)], w=FK(FT2, 1024))
            p.op("scalar", lambda e: e.activation(
                out=A[96:128, LXQB:LXQB + 1024], in_=FA[96:128, FT2:FT2 + 1024],
                func=AF.Copy), r=FK(FT2, 1024), w=AK(LXQB, 1024))
            for cc in range(8):
                p.op("tensor", lambda e, cc=cc: e.transpose(
                    PST[:, cc * 128:cc * 128 + 32],
                    A[96:128, LXQB + cc * 128:LXQB + (cc + 1) * 128],
                    IDN[96:128, 96:128], tile_position=(96, 0)),
                    r=AK(LXQB, 1024) + [("c", "idn")], w=[("PT", 0)])
            src3q = PST[:, 0:1024].rearrange("p (c t) -> p c t", t=128)[:, :, 0:32]
            dst3q = A[:, LXQT:LXQT + 256].rearrange("p (c t) -> p c t", t=32)
            p.op("scalar", lambda e, s3=src3q, d3=dst3q: e.activation(
                out=d3, in_=s3, func=AF.Copy),
                r=[("PT", 0)], w=AK(LXQT, 256))
            xqtk = AK(LXQT, 256)

            def l1_proj(scr, wkey, wsl_off, dst_off, dst_key_off, l2n):
                slw = next_wslot()
                for cc in range(8):
                    dsl = A[:, wsl_off + cc * 1024: wsl_off + (cc + 1) * 1024]
                    srcw = scr[cc * 128:(cc + 1) * 128, :]
                    p.dma("sync", lambda e, d=dsl, sr=srcw: e.dma_start(out=d, in_=sr),
                          r=[wkey], w=AK(wsl_off + cc * 1024, 1024), slot=slw)
                wk_ = AK(wsl_off, 8192)
                for dh in range(2):
                    for cc in range(8):
                        lhs = A[:, LXQT + cc * 32: LXQT + (cc + 1) * 32]
                        rhs = A[:, wsl_off + cc * 1024 + dh * 512: wsl_off + cc * 1024 + (dh + 1) * 512]
                        p.op("tensor", lambda e, lhs=lhs, rhs=rhs, dh=dh, cc=cc: e.matmul(
                            PSA[0:32, dh * 512:(dh + 1) * 512], lhs, rhs,
                            start=(cc == 0), stop=(cc == 7)),
                            r=wk_ + xqtk + ([] if cc else PK(dh)), w=PK(dh))
                if not l2n:
                    p.op("scalar", lambda e, d=dst_off: e.activation(
                        out=A[0:32, d:d + 1024], in_=PSA[0:32, 0:1024], func=AF.Copy),
                        r=PK(0, 2), w=AK(dst_off, 1024))
                    return
                p.op("vector", lambda e: e.tensor_copy(FA[0:32, FT0:FT0 + 1024],
                                                       PSA[0:32, 0:1024]),
                     r=PK(0, 2), w=FK(FT0, 1024))
                p.op("scalar", lambda e: e.activation(
                    out=FA[0:32, FT1:FT1 + 1024], in_=FA[0:32, FT0:FT0 + 1024],
                    func=AF.Square), r=FK(FT0, 1024), w=FK(FT1, 1024))
                sq3 = FA[0:32, FT1:FT1 + 1024].rearrange("p (h d) -> p h d", d=64)
                p.op("vector", lambda e, s3=sq3: e.reduce_sum(
                    NRM[0:32, 0:16], s3, axis=AX.X), r=FK(FT1, 1024), w=[("nrm", 0)])
                p.op("vector", lambda e: e.tensor_scalar(
                    out=NRM[0:32, 0:16], in0=NRM[0:32, 0:16], scalar1=1e-30,
                    scalar2=None, op0=ALU.add), r=[("nrm", 0)], w=[("nrm", 0)])
                p.op("scalar", lambda e: e.activation(
                    out=NRM[0:32, 16:32], in_=NRM[0:32, 0:16], func=AF.Ln),
                    r=[("nrm", 0)], w=[("nrm", 1)])
                p.op("scalar", lambda e: e.activation(
                    out=NRM[0:32, 16:32], in_=NRM[0:32, 16:32], func=AF.Exp, scale=0.5),
                    r=[("nrm", 1)], w=[("nrm", 1)])
                p.op("vector", lambda e: e.tensor_scalar(
                    out=NRM[0:32, 16:32], in0=NRM[0:32, 16:32], scalar1=EPS_NORM,
                    scalar2=None, op0=ALU.add), r=[("nrm", 1)], w=[("nrm", 1)])
                p.op("vector", lambda e: e.reciprocal(NRM[0:32, 32:48], NRM[0:32, 16:32]),
                     r=[("nrm", 1)], w=[("nrm", 2)])
                for hh in range(16):
                    p.op("vector", lambda e, hh=hh, d=dst_off: e.tensor_scalar(
                        out=A[0:32, d + hh * 64: d + (hh + 1) * 64],
                        in0=FA[0:32, FT0 + hh * 64: FT0 + (hh + 1) * 64],
                        scalar1=NRM[0:32, 32 + hh:33 + hh], scalar2=None,
                        op0=ALU.mult),
                        r=FK(FT0, 1024) + [("nrm", 2)], w=AK(dst_off + hh * 64, 64))

            l1_proj(s_att["wq1"], ("w", "wq1"), LW0, LQH, 0, True)
            l1_proj(s_att["wk1"], ("w", "wk1"), LW1, LKH, 0, True)
            l1_proj(s_att["wkp1"], ("w", "wkp1"), LW0, LKPH, 0, True)
            l1_proj(s_att["wv1"], ("w", "wv1"), LW1, LVH, 0, False)

            for cc in range(8):
                p.op("tensor", lambda e, cc=cc: e.transpose(
                    PST[:, cc * 128:cc * 128 + 32],
                    A[0:32, LQH + cc * 128: LQH + (cc + 1) * 128],
                    IDN[0:32, 0:32]),
                    r=AK(LQH, 1024) + [("c", "idn")], w=[("PT", 0)])
            src3q = PST[:, 0:1024].rearrange("p (c t) -> p c t", t=128)[:, :, 0:32]
            dst3q = A[:, LQHT:LQHT + 256].rearrange("p (c t) -> p c t", t=32)
            p.op("scalar", lambda e, s3=src3q, d3=dst3q: e.activation(
                out=d3, in_=s3, func=AF.Copy),
                r=[("PT", 0)], w=AK(LQHT, 256))

            for cc in range(8):
                lhs1 = A[0:32, LKH + cc * 128: LKH + (cc + 1) * 128]
                for dh in range(2):
                    p.op("tensor", lambda e, lhs=lhs1, dh=dh: e.matmul(
                        PSA[:, dh * 512:(dh + 1) * 512], lhs,
                        E1[:, dh * 512:(dh + 1) * 512], start=True, stop=True),
                        r=AK(LKH, 1024) + [("c", "e1")] + PK(dh), w=PK(dh))
                lhs2 = A[0:32, LKPH + cc * 128: LKPH + (cc + 1) * 128]
                for dh in range(2):
                    p.op("tensor", lambda e, lhs=lhs2, dh=dh: e.matmul(
                        PSA[:, 1024 + dh * 512: 1024 + (dh + 1) * 512], lhs,
                        E2[:, dh * 512:(dh + 1) * 512], start=True, stop=True),
                        r=AK(LKPH, 1024) + [("c", "e2")] + PK(2 + dh), w=PK(2 + dh))
                rep1s = A[:, XST0:XST0 + 1024]
                p.op("scalar", lambda e, d=rep1s: e.activation(
                    out=d, in_=PSA[:, 0:1024], func=AF.Copy),
                    r=PK(0, 2), w=AK(XST0, 1024))
                dst = A[:, LW2K + cc * 1024: LW2K + (cc + 1) * 1024]
                p.op("vector", lambda e, d=dst, r1=rep1s: e.tensor_tensor(
                    out=d, in0=PSA[:, 1024:2048], in1=r1, op=ALU.mult),
                    r=PK(2, 2) + AK(XST0, 1024), w=AK(LW2K + cc * 1024, 1024))
            for cc in range(8):
                lhs1 = E1[:, cc * 128:(cc + 1) * 128]
                for dh in range(2):
                    p.op("tensor", lambda e, lhs=lhs1, dh=dh: e.matmul(
                        PSA[:, dh * 512:(dh + 1) * 512], lhs,
                        A[0:32, LVH:LVH + 1024][:, dh * 512:(dh + 1) * 512],
                        start=True, stop=True),
                        r=AK(LVH, 1024) + [("c", "e1")] + PK(dh), w=PK(dh))
                lhs2 = E2[:, cc * 128:(cc + 1) * 128]
                for dh in range(2):
                    p.op("tensor", lambda e, lhs=lhs2, dh=dh: e.matmul(
                        PSA[:, 1024 + dh * 512: 1024 + (dh + 1) * 512], lhs,
                        A[0:32, LVH:LVH + 1024][:, dh * 512:(dh + 1) * 512],
                        start=True, stop=True),
                        r=AK(LVH, 1024) + [("c", "e2")] + PK(2 + dh), w=PK(2 + dh))
                rep1s = A[:, XST0:XST0 + 1024]
                p.op("scalar", lambda e, d=rep1s: e.activation(
                    out=d, in_=PSA[:, 0:1024], func=AF.Copy),
                    r=PK(0, 2), w=AK(XST0, 1024))
                dst = A[:, LW2V + cc * 1024: LW2V + (cc + 1) * 1024]
                p.op("vector", lambda e, d=dst, r1=rep1s: e.tensor_tensor(
                    out=d, in0=PSA[:, 1024:2048], in1=r1, op=ALU.mult),
                    r=PK(2, 2) + AK(XST0, 1024), w=AK(LW2V + cc * 1024, 1024))

            LP4S = 40960
            for b in range(4):
                for j in range(4):
                    hh = 4 * b + j
                    cc = hh // 2
                    po = (hh % 2) * 64
                    lhs = A[:, LQHT + cc * 32: LQHT + (cc + 1) * 32][po:po + 64, :]
                    for dh in range(2):
                        rhs = A[:, LW2K + cc * 1024 + dh * 512: LW2K + cc * 1024 + (dh + 1) * 512][po:po + 64, :]
                        p.op("tensor", lambda e, lhs=lhs, rhs=rhs, dh=dh, j=j, po=po: e.matmul(
                            PSA[32 * j:32 * j + 32, dh * 512:(dh + 1) * 512], lhs, rhs,
                            start=True, stop=True, tile_position=(po, 32 * j)),
                            r=AK(LQHT, 256) + AK(LW2K, 8192) + PK(dh), w=PK(dh))
                p.op("vector", lambda e: e.tensor_tensor(
                    out=PSA[:, 1024:2048], in0=PSA[:, 0:1024], in1=MSK4,
                    op=ALU.add), r=PK(0, 2) + [("c", "msk")], w=PK(2, 2))
                p.op("vector", lambda e: e.reduce_max(
                    TB[:, 0:1], PSA[:, 1024:2048], axis=AX.X),
                    r=PK(2, 2), w=TK(0))
                p.op("vector", lambda e: e.tensor_scalar(
                    out=TB[:, 1:2], in0=TB[:, 0:1], scalar1=-A_SCALE,
                    scalar2=None, op0=ALU.mult), r=TK(0), w=TK(1))
                p.op("scalar", lambda e: e.activation(
                    out=A[:, LPH:LPH + 1024], in_=PSA[:, 1024:2048],
                    func=AF.Exp, scale=A_SCALE, bias=TB[:, 1:2],
                    accum_out=TB[:, 2:3]),
                    r=PK(2, 2) + TK(1), w=AK(LPH, 1024) + TK(2))
                p.op("vector", lambda e: e.reciprocal(
                    TB[:, 3:4], TB[:, 2:3]), r=TK(2), w=TK(3))
                p.op("vector", lambda e: e.tensor_scalar(
                    out=A[:, LP4S:LP4S + 1024], in0=A[:, LPH:LPH + 1024],
                    scalar1=TB[:, 3:4], scalar2=None, op0=ALU.mult),
                    r=AK(LPH, 1024) + TK(3), w=AK(LP4S, 1024))
                for j in range(4):
                    hh = 4 * b + j
                    for c2 in range(8):
                        p.op("tensor", lambda e, c2=c2, j=j: e.transpose(
                            PST[:, c2 * 128:c2 * 128 + 32],
                            A[32 * j:32 * j + 32, LP4S + c2 * 128: LP4S + (c2 + 1) * 128],
                            IDN[32 * j:32 * j + 32, 32 * j:32 * j + 32],
                            tile_position=(32 * j, 0)),
                            r=AK(LP4S, 1024) + [("c", "idn")], w=[("PT", 0)])
                    src3p = PST[:, 0:1024].rearrange("p (c t) -> p c t", t=128)[:, :, 0:32]
                    dst3p = A[:, LPHT:LPHT + 256].rearrange("p (c t) -> p c t", t=32)
                    p.op("scalar", lambda e, s3=src3p, d3=dst3p: e.activation(
                        out=d3, in_=s3, func=AF.Copy),
                        r=[("PT", 0)], w=AK(LPHT, 256))
                    zc = (hh % 8) * 64
                    for c2 in range(8):
                        lhs = A[:, LPHT + c2 * 32: LPHT + (c2 + 1) * 32]
                        rhs = A[:, LW2V + c2 * 1024 + hh * 64: LW2V + c2 * 1024 + (hh + 1) * 64]
                        p.op("tensor", lambda e, lhs=lhs, rhs=rhs, zc=zc, c2=c2: e.matmul(
                            PSB[0:32, zc:zc + 64], lhs, rhs,
                            start=(c2 == 0), stop=(c2 == 7)),
                            r=AK(LPHT, 256) + AK(LW2V, 8192) + ([] if c2 else [("PB", 0)]),
                            w=[("PB", 0)])
                if b % 2 == 1:
                    zb = (b // 2) * 512
                    p.op("scalar", lambda e, zb=zb: e.activation(
                        out=A[0:32, LZH + zb: LZH + zb + 512], in_=PSB[0:32, 0:512],
                        func=AF.Copy),
                        r=[("PB", 0)], w=AK(LZH + zb, 512))

            for cc in range(8):
                p.op("tensor", lambda e, cc=cc: e.transpose(
                    PST[:, cc * 128:cc * 128 + 32],
                    A[0:32, LZH + cc * 128: LZH + (cc + 1) * 128],
                    IDN[0:32, 0:32]),
                    r=AK(LZH, 1024) + [("c", "idn")], w=[("PT", 0)])
            src3z = PST[:, 0:1024].rearrange("p (c t) -> p c t", t=128)[:, :, 0:32]
            dst3z = A[:, LZT:LZT + 256].rearrange("p (c t) -> p c t", t=32)
            p.op("scalar", lambda e, s3=src3z, d3=dst3z: e.activation(
                out=d3, in_=s3, func=AF.Copy),
                r=[("PT", 0)], w=AK(LZT, 256))
            slw = next_wslot()
            for cc in range(8):
                dsl = A[:, LW0 + cc * 1024: LW0 + (cc + 1) * 1024]
                srcw = s_att["wo1"][cc * 128:(cc + 1) * 128, :]
                p.dma("sync", lambda e, d=dsl, sr=srcw: e.dma_start(out=d, in_=sr),
                      r=[("w", "wo1")], w=AK(LW0 + cc * 1024, 1024), slot=slw)
            for dh in range(2):
                for cc in range(8):
                    lhs = A[:, LZT + cc * 32: LZT + (cc + 1) * 32]
                    rhs = A[:, LW0 + cc * 1024 + dh * 512: LW0 + cc * 1024 + (dh + 1) * 512]
                    p.op("tensor", lambda e, lhs=lhs, rhs=rhs, dh=dh, cc=cc: e.matmul(
                        PSA[96:128, 2048 + dh * 512: 2048 + (dh + 1) * 512], lhs, rhs,
                        start=(cc == 0), stop=(cc == 7), tile_position=(0, 96)),
                        r=AK(LZT, 256) + AK(LW0, 8192) + ([] if cc else PK(4 + dh)),
                        w=PK(4 + dh))
            p.op("vector", lambda e: e.tensor_tensor(
                out=FA[96:128, FT2:FT2 + 1024], in0=PSA[96:128, 2048:3072],
                in1=FA[96:128, FT2:FT2 + 1024], op=ALU.add),
                r=PK(4, 2) + FK(FT2, 1024), w=FK(FT2, 1024))
            ffn_whalf_load(1, 0)

            chain_tiles = [15] + list(range(15))

            def chain_bufs(pos):
                par = pos % 2
                S = FT2 if par == 0 else FT3
                W = FT0 if par == 0 else FT1
                tb0 = 4 if par == 0 else 12
                bn = BN1 if par == 0 else BN2
                return par, S, W, tb0, bn

            def chain_rstd(tb0):
                tbv, tbl, tbr, tbb = tb0 + 3, tb0 + 4, tb0 + 5, tb0 + 7
                p.op("vector", lambda e: e.tensor_scalar(
                    out=TB[:, tbv:tbv + 1], in0=TB[:, tb0 + 1:tb0 + 2],
                    scalar1=1.0, scalar2=EPS_LN,
                    op0=ALU.mult, op1=ALU.add), r=TK(tb0 + 1), w=TK(tbv))
                p.op("scalar", lambda e: e.activation(
                    out=TB[:, tbl:tbl + 1], in_=TB[:, tbv:tbv + 1], func=AF.Ln),
                    r=TK(tbv), w=TK(tbl))
                p.op("scalar", lambda e: e.activation(
                    out=TB[:, tbr:tbr + 1], in_=TB[:, tbl:tbl + 1], func=AF.Exp,
                    scale=-0.5), r=TK(tbl), w=TK(tbr))
                p.op("vector", lambda e: e.scalar_tensor_tensor(
                    out=TB[:, tbb:tbb + 1], in0=TB[:, tb0:tb0 + 1], scalar=-1.0,
                    in1=TB[:, tbr:tbr + 1],
                    op0=ALU.mult, op1=ALU.mult), r=TK(tb0) + TK(tbr), w=TK(tbb))

            def chain_stageA(pos):
                t = chain_tiles[pos]
                par, S, W, tb0, bn = chain_bufs(pos)
                tbr, tbb = tb0 + 5, tb0 + 7
                Sk, Wk_ = FK(S, 1024), FK(W, 1024)
                bkey = ("bn", par)
                if t == 15:
                    p.op("vector", lambda e, S=S, b=bn: e.bn_stats(
                        b[:, 0:6], FA[:, S:S + 512]), r=Sk, w=[bkey])
                    p.op("vector", lambda e, S=S, b=bn: e.bn_stats(
                        b[:, 6:12], FA[:, S + 512:S + 1024]), r=Sk, w=[bkey])
                    p.op("vector", lambda e, b=bn, tb0=tb0: e.bn_aggr(
                        TB[:, tb0:tb0 + 2], b), r=[bkey], w=TK(tb0, 2))
                    chain_rstd(tb0)
                    p.op("scalar", lambda e, S=S, W=W, tbr=tbr, tbb=tbb: e.activation(
                        out=FA[:, W:W + 1024], in_=FA[:, S:S + 1024],
                        func=AF.Identity, scale=TB[:, tbr:tbr + 1],
                        bias=TB[:, tbb:tbb + 1]),
                        r=Sk + TK(tbr) + TK(tbb), w=Wk_)
                else:
                    p.op("scalar", lambda e, t=t, W=W: e.activation(
                        out=FA[:, W:W + 1024], in_=X[:, t * 1024:(t + 1) * 1024],
                        func=AF.Identity, scale=QR2[:, t:t + 1], bias=QM2[:, t:t + 1]),
                        r=[("x", t), ("qm2", t // 4), ("qr2", t // 4)], w=Wk_)
                p.op("vector", lambda e, S=S, W=W: e.tensor_mul(
                    FA[:, S:S + 1024], FA[:, W:W + 1024], GAM),
                    r=Wk_ + [("c", "gam")], w=Sk)
                p.op("vector", lambda e, S=S, W=W: e.tensor_tensor(
                    out=FA[:, W:W + 1024], in0=FA[:, S:S + 1024],
                    in1=BET, op=ALU.add),
                    r=Sk + [("c", "bet")], w=Wk_)
                p.op("vector", lambda e, W=W, b=bn: e.bn_stats(
                    b[:, 0:6], FA[:, W:W + 512]), r=Wk_, w=[bkey])
                p.op("vector", lambda e, W=W, b=bn: e.bn_stats(
                    b[:, 6:12], FA[:, W + 512:W + 1024]), r=Wk_, w=[bkey])
                p.op("vector", lambda e, b=bn, tb0=tb0: e.bn_aggr(
                    TB[:, tb0:tb0 + 2], b), r=[bkey], w=TK(tb0, 2))
                chain_rstd(tb0)

            def chain_stageB(pos):
                t = chain_tiles[pos]
                par, S, W, tb0, bn = chain_bufs(pos)
                tbr, tbb = tb0 + 5, tb0 + 7
                Sk, Wk_ = FK(S, 1024), FK(W, 1024)
                p.op("scalar", lambda e, S=S, W=W, tbr=tbr, tbb=tbb: e.activation(
                    out=FA[:, S:S + 1024], in_=FA[:, W:W + 1024],
                    func=AF.Identity, scale=TB[:, tbr:tbr + 1], bias=TB[:, tbb:tbb + 1]),
                    r=Wk_ + TK(tbr) + TK(tbb), w=Sk)
                p.op("gpsimd", lambda e, t=t, S=S: e.tensor_tensor(
                    out=X[:, t * 1024:(t + 1) * 1024], in0=FA[:, S:S + 1024],
                    in1=X[:, t * 1024:(t + 1) * 1024], op=ALU.add),
                    r=Sk + [("x", t)], w=[("x", t)])
                if t in (3, 7, 11, 14):
                    g_done = 3 if t == 14 else t // 4
                    ffn_prep_group(1, g_done)
                p.op("tensor", lambda e: e.matmul(
                    PSB[:, 0:128], IDN, IDN, start=True, stop=True),
                    r=[("c", "idn")], w=[("PB", 0)])

            CHAIN_PIPE = True
            if CHAIN_PIPE:
                chain_stageA(0)
                for pos in range(16):
                    if pos + 1 < 16:
                        chain_stageA(pos + 1)
                    chain_stageB(pos)
            else:
                for pos in range(16):
                    chain_stageA(pos)
                    chain_stageB(pos)

            def store_group(g_):
                for t in range(4 * g_, 4 * g_ + 4):
                    dst = out[xoff + t * 128: xoff + (t + 1) * 128, :]
                    p.dma("gpsimd", lambda e, t=t, dst=dst: e.dma_start(
                        out=dst, in_=X[:, t * 1024:(t + 1) * 1024]),
                        r=[("x", t)], slot="out")

            ffn_layer2(1, prep_done=True, post_group=store_group)

        if OP_LIMIT is not None:
            p.truncate(OP_LIMIT)
            for t in range(16):
                dst = out[t * 128:(t + 1) * 128, :]
                p.dma("gpsimd", lambda e, t=t, dst=dst: e.dma_start(
                    out=dst, in_=X[:, t * 1024:(t + 1) * 1024]), r=(), slot="out")
        p.build(nc, es, final_fences=[("gpsimd", "out")])
    return nc


_CACHE = {}


def kernel(**inputs):
    if "nc" not in _CACHE:
        _CACHE["nc"] = build_program()
    nc = _CACHE["nc"]
    x = np.ascontiguousarray(np.asarray(inputs["L0_attn_x"], dtype=np.float32))
    gam = np.ascontiguousarray(np.broadcast_to(
        np.asarray(inputs["L1_attn_ln_gamma"], dtype=np.float32), (128, 1024)))
    bet = np.ascontiguousarray(np.broadcast_to(
        np.asarray(inputs["L1_attn_ln_beta"], dtype=np.float32), (128, 1024)))
    ident = np.eye(128, dtype=ml_dtypes.bfloat16)
    causal = (np.triu(np.ones((128, 128), np.float32), k=1) * NEG).astype(np.float32)
    jj = np.arange(1024) // 32
    kk = np.arange(1024) % 32
    ss = np.arange(32)[:, None]
    maska = np.where((jj[None, :] > ss) | (kk[None, :] > ss), NEG, 0.0).astype(np.float32)
    maska = np.ascontiguousarray(np.tile(maska, (4, 1)))
    e1 = (jj[None, :] == np.arange(32)[:, None]).astype(ml_dtypes.bfloat16)
    e2 = (kk[None, :] == np.arange(32)[:, None]).astype(ml_dtypes.bfloat16)

    common = {
        "wq0": np.asarray(inputs["L0_attn_Wq"], np.float32),
        "wk0": np.asarray(inputs["L0_attn_Wk"], np.float32),
        "wv0": np.asarray(inputs["L0_attn_Wv"], np.float32),
        "g0": np.asarray(inputs["L0_ffn_w_gate"], np.int32),
        "u0": np.asarray(inputs["L0_ffn_w_up"], np.int32),
        "d0": np.asarray(inputs["L0_ffn_w_down"], np.int32),
        "wq1": np.asarray(inputs["L1_attn_W_Q"], np.float32),
        "wk1": np.asarray(inputs["L1_attn_W_K"], np.float32),
        "wv1": np.asarray(inputs["L1_attn_W_V"], np.float32),
        "wkp1": np.asarray(inputs["L1_attn_W_Kp"], np.float32),
        "wo1": np.asarray(inputs["L1_attn_W_O"], np.float32),
        "g1": np.asarray(inputs["L1_ffn_w_gate"], np.int32),
        "u1": np.asarray(inputs["L1_ffn_w_up"], np.int32),
        "d1": np.asarray(inputs["L1_ffn_w_down"], np.int32),
        "gamma": gam, "beta": bet, "identd": ident, "causald": causal,
        "e1d": e1, "e2d": e2, "maskad": maska,
    }
    in_maps = []
    for c in range(8):
        m = dict(common)
        m["x_in"] = np.ascontiguousarray(x[2 * c:2 * c + 2].reshape(4096, 1024))
        in_maps.append(m)
    res = run_bass_kernel_spmd(nc, in_maps, core_ids=list(range(8)))
    outs = [res.results[c]["out"].reshape(2, 2048, 1024) for c in range(8)]
    return np.concatenate(outs, axis=0)


# version 38
# speedup vs baseline: 1.0429x; 1.0429x over previous
import contextlib
import numpy as np
import ml_dtypes
import concourse.bass as bass
import concourse.mybir as mybir
from concourse.bass_utils import run_bass_kernel_spmd


F32 = mybir.dt.float32
BF16 = mybir.dt.bfloat16
I32 = mybir.dt.int32
AF = mybir.ActivationFunctionType
ALU = mybir.AluOpType
AX = mybir.AxisListType

ENGINES = ("tensor", "vector", "scalar", "gpsimd", "sync")

N_SEQS = 2
STAGES = ("l0a", "ffn0", "l1a", "ffn1")
OP_LIMIT = None

NEG = -1.0e30
EPS_LN = 1e-5
EPS_NORM = 1e-7
QK_SCALE = 1.0 / 32.0
DQ = 1.0 / 127.0
A_SCALE = 0.125


class Prog:

    def __init__(self):
        self.ops = []
        self.lw = {}
        self.rd = {}
        self.slot_n = {}

    def op(self, eng, fn, r=(), w=(), slot=None):
        idx = len(self.ops)
        deps = set()
        for k in r:
            if k in self.lw:
                deps.add(self.lw[k])
        for k in w:
            if k in self.lw:
                deps.add(self.lw[k])
            deps |= self.rd.get(k, set())
        deps.discard(idx)
        for d in list(deps):
            pd = self.ops[d]
            if pd["slot"] is not None and pd["slot"] != slot:
                self.rd.setdefault(("slot", pd["slot"]), set()).add(idx)
        if slot is not None:
            deps |= self.rd.get(("slot", slot), set())
            deps.discard(idx)
            self.lw[("slot", slot)] = idx
            self.rd[("slot", slot)] = set()
        for k in r:
            self.rd.setdefault(k, set()).add(idx)
        for k in w:
            self.lw[k] = idx
            self.rd[k] = set()
        cum = None
        if slot is not None:
            self.slot_n[slot] = self.slot_n.get(slot, 0) + 1
            cum = self.slot_n[slot] * 16
        self.ops.append(dict(eng=eng, fn=fn, deps=sorted(deps), slot=slot,
                             cum=cum, idx=idx))
        return idx

    def dma(self, queue, fn, r=(), w=(), slot="d"):
        return self.op(queue, fn, r, w, slot=slot)

    def truncate(self, limit):
        self.ops = self.ops[:limit]
        self.slot_n = {}
        for o in self.ops:
            if o["slot"] is not None:
                self.slot_n[o["slot"]] = self.slot_n.get(o["slot"], 0) + 1
        kept = set(range(len(self.ops)))
        self.lw = {k: v for k, v in self.lw.items() if v in kept}
        self.rd = {k: (v & kept) for k, v in self.rd.items()}

    def build(self, nc, es, final_fences=()):
        ops = self.ops
        n = len(ops)
        need = [False] * n
        for o in ops:
            for d in o["deps"]:
                p = ops[d]
                if p["slot"] is None:
                    if p["eng"] == "tensor" and o["eng"] == "tensor":
                        continue
                    need[d] = True
        semval = [0] * n
        cnt = {e: 0 for e in ENGINES}
        for o in ops:
            if o["slot"] is None and need[o["idx"]]:
                cnt[o["eng"]] += 1
                semval[o["idx"]] = cnt[o["eng"]]
        sems = {}
        for e in ENGINES:
            if cnt.get(e, 0) > 0:
                sems[("e", e)] = es.enter_context(nc.semaphore(f"se_{e}"))
        for s in self.slot_n:
            sems[("s", s)] = es.enter_context(nc.semaphore(f"sd_{s}"))
        waits = [None] * n
        cov = {e: {} for e in ENGINES}
        run_slot = {}
        for o in ops:
            i = o["idx"]
            needc = {}
            for d in o["deps"]:
                p = ops[d]
                if p["slot"] is not None:
                    k = ("s", p["slot"])
                    v = run_slot.get(p["slot"], 0) * 16
                    if needc.get(k, -1) < v:
                        needc[k] = v
                elif need[d]:
                    if p["eng"] == "tensor" and o["eng"] == "tensor":
                        continue
                    k = ("e", p["eng"])
                    if needc.get(k, -1) < semval[d]:
                        needc[k] = semval[d]
            E = o["eng"]
            ww = []
            for k in sorted(needc, key=str):
                v = needc[k]
                if cov[E].get(k, -1) < v:
                    ww.append((k, v))
                    cov[E][k] = v
            waits[i] = ww
            if o["slot"] is not None:
                run_slot[o["slot"]] = run_slot.get(o["slot"], 0) + 1
        sched = {e: [] for e in ENGINES}
        for o in ops:
            sched[o["eng"]].append(o)
        block = es.enter_context(nc.Block())

        def mk(ename):
            def emitter(eng):
                for o in sched[ename]:
                    i = o["idx"]
                    for k, v in waits[i]:
                        eng.wait_ge(sems[k], v)
                    inst = o["fn"](eng)
                    if o["slot"] is not None:
                        inst.then_inc(sems[("s", o["slot"])], 16)
                    elif need[i]:
                        inst.then_inc(sems[("e", o["eng"])], 1)
                for fe, fslot in final_fences:
                    if fe == ename:
                        eng.wait_ge(sems[("s", fslot)],
                                    self.slot_n[fslot] * 16)
            return emitter

        block.tensor(mk("tensor"))
        block.vector(mk("vector"))
        block.scalar(mk("scalar"))
        block.gpsimd(mk("gpsimd"))
        block.sync(mk("sync"))


def segs(pfx, off, size, gran):
    return [(pfx, i) for i in range(off // gran, (off + size + gran - 1) // gran)]


def build_program():
    nc = bass.Bass("TRN2", target_bir_lowering=False)
    p = Prog()

    x_in = nc.dram_tensor("x_in", [4096, 1024], F32, kind="ExternalInput").ap()
    out = nc.dram_tensor("out", [4096, 1024], F32, kind="ExternalOutput").ap()
    wq0 = nc.dram_tensor("wq0", [1024, 1024], F32, kind="ExternalInput").ap()
    wk0 = nc.dram_tensor("wk0", [1024, 1024], F32, kind="ExternalInput").ap()
    wv0 = nc.dram_tensor("wv0", [1024, 1024], F32, kind="ExternalInput").ap()
    g0 = nc.dram_tensor("g0", [3072, 1024], I32, kind="ExternalInput").ap()
    u0 = nc.dram_tensor("u0", [3072, 1024], I32, kind="ExternalInput").ap()
    d0 = nc.dram_tensor("d0", [1024, 3072], I32, kind="ExternalInput").ap()
    wq1 = nc.dram_tensor("wq1", [1024, 1024], F32, kind="ExternalInput").ap()
    wk1 = nc.dram_tensor("wk1", [1024, 1024], F32, kind="ExternalInput").ap()
    wv1 = nc.dram_tensor("wv1", [1024, 1024], F32, kind="ExternalInput").ap()
    wkp1 = nc.dram_tensor("wkp1", [1024, 1024], F32, kind="ExternalInput").ap()
    wo1 = nc.dram_tensor("wo1", [1024, 1024], F32, kind="ExternalInput").ap()
    g1 = nc.dram_tensor("g1", [3072, 1024], I32, kind="ExternalInput").ap()
    u1 = nc.dram_tensor("u1", [3072, 1024], I32, kind="ExternalInput").ap()
    d1 = nc.dram_tensor("d1", [1024, 3072], I32, kind="ExternalInput").ap()
    gamma = nc.dram_tensor("gamma", [128, 1024], F32, kind="ExternalInput").ap()
    beta = nc.dram_tensor("beta", [128, 1024], F32, kind="ExternalInput").ap()
    identd = nc.dram_tensor("identd", [128, 128], BF16, kind="ExternalInput").ap()
    causald = nc.dram_tensor("causald", [128, 128], F32, kind="ExternalInput").ap()
    e1d = nc.dram_tensor("e1d", [32, 1024], BF16, kind="ExternalInput").ap()
    e2d = nc.dram_tensor("e2d", [32, 1024], BF16, kind="ExternalInput").ap()
    maskad = nc.dram_tensor("maskad", [128, 1024], F32, kind="ExternalInput").ap()

    s_att = {}
    for name, src in [("wq0", wq0), ("wk0", wk0), ("wv0", wv0),
                      ("wq1", wq1), ("wk1", wk1), ("wv1", wv1),
                      ("wkp1", wkp1), ("wo1", wo1)]:
        s_att[name] = nc.dram_tensor("s_" + name, [1024, 1024], BF16,
                                     kind="Internal").ap()
    s_ffn = {}
    for name, src in [("g0", g0), ("u0", u0), ("g1", g1), ("u1", u1)]:
        s_ffn[name] = nc.dram_tensor("s_" + name, [3072, 1024], BF16,
                                     kind="Internal").ap()
    for name, src in [("d0", d0), ("d1", d1)]:
        s_ffn[name] = nc.dram_tensor("s_" + name, [1024, 3072], BF16,
                                     kind="Internal").ap()
    wdt = [nc.dram_tensor(f"wdt{l}", [128, 24576], BF16, kind="Internal").ap()
           for l in range(2)]
    wgt = [nc.dram_tensor(f"wgt{l}", [128, 24576], BF16, kind="Internal").ap()
           for l in range(2)]
    wut = [nc.dram_tensor(f"wut{l}", [128, 24576], BF16, kind="Internal").ap()
           for l in range(2)]
    hs = [nc.dram_tensor(f"hs{g}", [128, 12288], BF16, kind="Internal").ap()
          for g in range(4)]

    es = contextlib.ExitStack()
    with es:
        AR_COLS = 55296
        A = es.enter_context(nc.sbuf_tensor("A", [128, AR_COLS], BF16))[:]
        X = es.enter_context(nc.sbuf_tensor("X", [128, 16384], F32))[:]
        FA = es.enter_context(nc.sbuf_tensor("FA", [128, 4096], F32))[:]
        GAM = es.enter_context(nc.sbuf_tensor("GAM", [128, 1024], F32))[:]
        BET = es.enter_context(nc.sbuf_tensor("BET", [128, 1024], F32))[:]
        IDN = es.enter_context(nc.sbuf_tensor("IDN", [128, 128], BF16))[:]
        CAU = es.enter_context(nc.sbuf_tensor("CAU", [128, 128], F32))[:]
        E1 = es.enter_context(nc.sbuf_tensor("E1", [32, 1024], BF16))[:]
        E2 = es.enter_context(nc.sbuf_tensor("E2", [32, 1024], BF16))[:]
        MSK4 = es.enter_context(nc.sbuf_tensor("MSK4", [128, 1024], F32))[:]
        BN1 = es.enter_context(nc.sbuf_tensor("BN1", [128, 12], F32))[:]
        BN2 = es.enter_context(nc.sbuf_tensor("BN2", [128, 12], F32))[:]
        STG = es.enter_context(nc.sbuf_tensor("STG", [128, 32], F32))[:]
        VB = es.enter_context(nc.sbuf_tensor("VB", [128, 64], F32))[:]
        QM = es.enter_context(nc.sbuf_tensor("QM", [128, 16], F32))[:]
        QR = es.enter_context(nc.sbuf_tensor("QR", [128, 16], F32))[:]
        QM2 = es.enter_context(nc.sbuf_tensor("QM2", [128, 16], F32))[:]
        QR2 = es.enter_context(nc.sbuf_tensor("QR2", [128, 16], F32))[:]
        TB = es.enter_context(nc.sbuf_tensor("TB", [128, 24], F32))[:]
        NRM = es.enter_context(nc.sbuf_tensor("NRM", [128, 48], F32))[:]
        RR16 = es.enter_context(nc.sbuf_tensor("RR16", [128, 16], F32))[:]
        PSA = es.enter_context(nc.psum_tensor("PSA", [128, 3072], F32))[:]
        PSB = es.enter_context(nc.psum_tensor("PSB", [128, 512], F32))[:]
        PST = es.enter_context(nc.psum_tensor("PST", [128, 1024], BF16))[:]

        AK = lambda off, size: segs("A", off, size, 512)
        FK = lambda off, size: segs("F", off, size, 512)
        PK = lambda b0, nb=1: [("P", b) for b in range(b0, b0 + nb)]
        TK = lambda c0, nc_=1: segs("T", c0, nc_, 1)
        VK = lambda c0, nc_=1: segs("V", c0, nc_, 1)

        p.dma("scalar", lambda e: e.dma_start(out=IDN, in_=identd[:, :]), w=[("c", "idn")], slot="cst")
        p.dma("scalar", lambda e: e.dma_start(out=CAU, in_=causald[:, :]), w=[("c", "cau")], slot="cst")
        p.dma("scalar", lambda e: e.dma_start(out=E1, in_=e1d[:, :]), w=[("c", "e1")], slot="cst")
        p.dma("scalar", lambda e: e.dma_start(out=E2, in_=e2d[:, :]), w=[("c", "e2")], slot="cst")
        p.dma("scalar", lambda e: e.dma_start(out=MSK4, in_=maskad[:, :]), w=[("c", "msk")], slot="cst")
        p.dma("scalar", lambda e: e.dma_start(out=GAM, in_=gamma[:, :]), w=[("c", "gam")], slot="cst")
        p.dma("scalar", lambda e: e.dma_start(out=BET, in_=beta[:, :]), w=[("c", "bet")], slot="cst")

        def stage(src_ap, dst_ap, srckey, dstkey, sslot):
            p.dma("gpsimd", lambda e, s=src_ap, d=dst_ap: e.dma_start(out=d[:, :], in_=s[:, :]),
                  r=[srckey], w=[dstkey], slot=sslot)

        stage(wk0, s_att["wk0"], ("i", "wk0"), ("w", "wk0"), "sw0")
        stage(wv0, s_att["wv0"], ("i", "wv0"), ("w", "wv0"), "sw0")

        def stage_rest():
            stage(wq0, s_att["wq0"], ("i", "wq0"), ("w", "wq0"), "sw0")
            stage(g0, s_ffn["g0"], ("i", "g0"), ("w", "g0"), "sw1")
            stage(u0, s_ffn["u0"], ("i", "u0"), ("w", "u0"), "sw1")
            stage(d0, s_ffn["d0"], ("i", "d0"), ("w", "d0"), "sw1")
            stage(wq1, s_att["wq1"], ("i", "wq1"), ("w", "wq1"), "sw2")
            stage(wk1, s_att["wk1"], ("i", "wk1"), ("w", "wk1"), "sw2")
            stage(wkp1, s_att["wkp1"], ("i", "wkp1"), ("w", "wkp1"), "sw2")
            stage(wv1, s_att["wv1"], ("i", "wv1"), ("w", "wv1"), "sw2")
            stage(wo1, s_att["wo1"], ("i", "wo1"), ("w", "wo1"), "sw2")
            stage(g1, s_ffn["g1"], ("i", "g1"), ("w", "g1"), "sw3")
            stage(u1, s_ffn["u1"], ("i", "u1"), ("w", "u1"), "sw3")
            stage(d1, s_ffn["d1"], ("i", "d1"), ("w", "d1"), "sw3")

        wslot_state = [0]

        def next_wslot():
            wslot_state[0] ^= 1
            return "wa" if wslot_state[0] else "wb"

        def ln_stats_batch2(src_aps, src_keys, bias_ap, rstd_ap, mkey, rkey,
                            var_scale=1.0, rho=None):
            nt = len(src_aps)
            for i in range(nt):
                sap = src_aps[i]
                bn = BN1 if i % 2 == 0 else BN2
                bkey = ("bn", i % 2)
                p.op("vector", lambda e, a=sap, b=bn: e.bn_stats(b[:, 0:6], a[:, 0:512]),
                     r=src_keys[i], w=[bkey])
                p.op("vector", lambda e, a=sap, b=bn: e.bn_stats(b[:, 6:12], a[:, 512:1024]),
                     r=src_keys[i], w=[bkey])
                p.op("vector", lambda e, b=bn, i=i: e.bn_aggr(STG[:, 2 * i:2 * i + 2], b),
                     r=[bkey], w=[("stg", i)])
            stgk = [("stg", i) for i in range(nt)]
            stg3 = STG[:, 0:2 * nt].rearrange("p (a b) -> p a b", b=2)
            p.op("vector", lambda e, nt=nt, s=stg3: e.tensor_scalar(
                out=VB[:, 0:nt], in0=s[:, :, 1], scalar1=var_scale, scalar2=EPS_LN,
                op0=ALU.mult, op1=ALU.add), r=stgk, w=VK(0, nt))
            p.op("scalar", lambda e, nt=nt: e.activation(
                out=VB[:, 16:16 + nt], in_=VB[:, 0:nt], func=AF.Ln),
                r=VK(0, nt), w=VK(16, nt))
            p.op("scalar", lambda e, nt=nt, rd=rstd_ap: e.activation(
                out=rd[:, 0:nt], in_=VB[:, 16:16 + nt], func=AF.Exp, scale=-0.5),
                r=VK(16, nt), w=[rkey])
            p.op("vector", lambda e, nt=nt, md=bias_ap, rd=rstd_ap, s=stg3: e.scalar_tensor_tensor(
                out=md[:, 0:nt], in0=s[:, :, 0], scalar=-1.0, in1=rd[:, 0:nt],
                op0=ALU.mult, op1=ALU.mult), r=stgk + [rkey], w=[mkey])
            if rho is not None:
                rho_bias, rho_ap, mkey2, rkey2 = rho
                p.op("vector", lambda e, nt=nt, s=stg3: e.tensor_scalar(
                    out=VB[:, 32:32 + nt], in0=s[:, :, 1],
                    scalar1=(1.0 + EPS_LN) * var_scale, scalar2=EPS_LN * EPS_LN,
                    op0=ALU.mult, op1=ALU.add), r=stgk, w=VK(32, nt))
                p.op("scalar", lambda e, nt=nt: e.activation(
                    out=VB[:, 48:48 + nt], in_=VB[:, 32:32 + nt], func=AF.Ln),
                    r=VK(32, nt), w=VK(48, nt))
                p.op("scalar", lambda e, nt=nt, rd=rho_ap: e.activation(
                    out=rd[:, 0:nt], in_=VB[:, 48:48 + nt], func=AF.Exp, scale=-0.5),
                    r=VK(48, nt), w=[rkey2])
                p.op("vector", lambda e, nt=nt, md=rho_bias, rd=rho_ap, s=stg3: e.scalar_tensor_tensor(
                    out=md[:, 0:nt], in0=s[:, :, 0], scalar=-1.0, in1=rd[:, 0:nt],
                    op0=ALU.mult, op1=ALU.mult), r=stgk + [rkey2], w=[mkey2])

        PSTB = PSB.bitcast(BF16)

        def transpose_tile_128(src_ap, src_keys, dst3, dst_keys, pst_base, bank=0,
                               evac="scalar"):
            PB = PST if bank == 0 else PSTB
            bkey = [("PT", 0)] if bank == 0 else [("PB", 0)]
            for cc in range(8):
                p.op("tensor", lambda e, s=src_ap, cc=cc, PB=PB: e.transpose(
                    PB[:, cc * 128:(cc + 1) * 128], s[:, cc * 128:cc * 128 + 128], IDN),
                    r=src_keys + [("c", "idn")], w=bkey)
            src3 = PB[:, 0:1024].rearrange("p (c t) -> p c t", t=128)
            if evac == "scalar":
                p.op("scalar", lambda e, s3=src3, d3=dst3: e.activation(
                    out=d3, in_=s3, func=AF.Copy),
                    r=bkey, w=dst_keys)
            else:
                p.op("vector", lambda e, s3=src3, d3=dst3: e.tensor_copy(d3, s3),
                    r=bkey, w=dst_keys)

        KT0 = 0
        V0 = 16384
        WA0 = 32768
        WB0 = 40960
        XLT0 = 49152
        XST0 = 53248
        XQ0 = 40960
        QTG = 43008
        PT0 = 47104
        P0 = 53248
        FXLT = 0
        FWGH = 16384
        FWUH = 28672
        FHTH = 40960
        FWD = 0
        FHG = 24576
        LW0 = 0
        LW1 = 8192
        LQH = 16384
        LKH = 17408
        LKPH = 18432
        LVH = 19456
        LXQT = 20480
        LQHT = 20736
        LZT = 20992
        LPHT = 21248
        LW2K = 21504
        LW2V = 29696
        LPH = 37888
        LZH = 38912
        LXQB = 39936

        FT0 = 0
        FT1 = 1024
        FT2 = 2048
        FT3 = 3072

        def load_transposed(scr_ap, wkey, dst_off, rows, cc_count, cols_per_cc,
                            slot, extra_r=()):
            for cc in range(cc_count):
                dsl = A[:, dst_off + cc * cols_per_cc: dst_off + (cc + 1) * cols_per_cc]
                src = scr_ap[0:rows, cc * 128:(cc + 1) * 128]
                p.dma("sync", lambda e, d=dsl, s=src: e.dma_start(out=d, in_=s, transpose=True),
                      r=[wkey] + list(extra_r), w=AK(dst_off + cc * cols_per_cc, cols_per_cc),
                      slot=slot)

        for s in range(N_SEQS):
            xoff = s * 2048

            for t in range(16):
                src = x_in[xoff + t * 128: xoff + (t + 1) * 128, :]
                p.dma("sync", lambda e, t=t, src=src: e.dma_start(
                    out=X[:, t * 1024:(t + 1) * 1024], in_=src),
                    r=[("i", "x")], w=[("x", t)], slot="xin")

            do_l0a = "l0a" in STAGES
            sl_k = next_wslot()
            load_transposed(s_att["wk0"], ("w", "wk0"), WA0, 1024, 8, 1024, sl_k)
            sl_v = next_wslot()
            load_transposed(s_att["wv0"], ("w", "wv0"), WB0, 1024, 8, 1024, sl_v)
            if s == 0:
                stage_rest()

            for g in range(4):
                tiles = [4 * g + tt for tt in range(4)]
                ln_stats_batch2(
                    [X[:, t * 1024:(t + 1) * 1024] for t in tiles],
                    [[("x", t)] for t in tiles],
                    QM[:, 4 * g:], QR[:, 4 * g:], ("qm", g), ("qr", g))
                for pair in range(2):
                    sts = []
                    for tt2 in range(2):
                        tt = pair * 2 + tt2
                        t = tiles[tt]
                        xst = A[:, XST0 + tt2 * 1024: XST0 + tt2 * 1024 + 1024]
                        stkey = AK(XST0 + tt2 * 1024, 1024)
                        sts.append((tt, xst, stkey))
                        p.op("scalar", lambda e, t=t, tt=tt, g=g, xst=xst: e.activation(
                            out=xst, in_=X[:, t * 1024:(t + 1) * 1024], func=AF.Identity,
                            scale=QR[:, 4 * g + tt:4 * g + tt + 1],
                            bias=QM[:, 4 * g + tt:4 * g + tt + 1]),
                            r=[("x", t), ("qm", g), ("qr", g)], w=stkey)
                    for tt, xst, stkey in sts:
                        xl = A[:, XLT0:XLT0 + 4096].rearrange("p (c w) -> p c w", w=512)
                        d3 = xl[:, :, tt * 128:tt * 128 + 128]
                        dk = [k for cc in range(8) for k in AK(XLT0 + cc * 512 + tt * 128, 128)]
                        transpose_tile_128(xst, stkey, d3, dk, 0, bank=tt % 2)
                xlk = AK(XLT0, 4096)
                for dc in range(8):
                    bank = dc % 6
                    for cc in range(8):
                        lhs = A[:, WA0 + cc * 1024 + dc * 128: WA0 + cc * 1024 + dc * 128 + 128]
                        rhs = A[:, XLT0 + cc * 512: XLT0 + (cc + 1) * 512]
                        p.op("tensor", lambda e, lhs=lhs, rhs=rhs, b=bank, cc=cc: e.matmul(
                            PSA[:, b * 512:(b + 1) * 512], lhs, rhs,
                            start=(cc == 0), stop=(cc == 7)),
                            r=AK(WA0, 8192) + xlk + ([] if cc else PK(bank)),
                            w=PK(bank))
                    dst = A[:, KT0 + dc * 2048 + g * 512: KT0 + dc * 2048 + (g + 1) * 512]
                    p.op("scalar", lambda e, d=dst, b=bank: e.activation(
                        out=d, in_=PSA[:, b * 512:(b + 1) * 512], func=AF.Copy),
                        r=PK(bank), w=AK(KT0 + dc * 2048 + g * 512, 512))
                for tsub in range(4):
                    t = 4 * g + tsub
                    for dh in range(2):
                        bank = (tsub * 2 + dh) % 6
                        for cc in range(8):
                            lhs = A[:, XLT0 + cc * 512 + tsub * 128: XLT0 + cc * 512 + tsub * 128 + 128]
                            rhs = A[:, WB0 + cc * 1024 + dh * 512: WB0 + cc * 1024 + (dh + 1) * 512]
                            p.op("tensor", lambda e, lhs=lhs, rhs=rhs, b=bank, cc=cc: e.matmul(
                                PSA[:, b * 512:(b + 1) * 512], lhs, rhs,
                                start=(cc == 0), stop=(cc == 7)),
                                r=AK(WB0, 8192) + xlk + ([] if cc else PK(bank)),
                                w=PK(bank))
                        dst = A[:, V0 + t * 1024 + dh * 512: V0 + t * 1024 + (dh + 1) * 512]
                        p.op("scalar", lambda e, d=dst, b=bank: e.activation(
                            out=d, in_=PSA[:, b * 512:(b + 1) * 512], func=AF.Copy),
                            r=PK(bank), w=AK(V0 + t * 1024 + dh * 512, 512))

            sl_q = next_wslot()
            load_transposed(s_att["wq0"], ("w", "wq0"), WA0, 1024, 8, 1024, sl_q)
            def build_qtg(g):
                for pair in range(2):
                    sts = []
                    for tt2 in range(2):
                        tt = pair * 2 + tt2
                        t = 4 * g + tt
                        xst = A[:, XQ0 + tt2 * 1024: XQ0 + tt2 * 1024 + 1024]
                        stkey = AK(XQ0 + tt2 * 1024, 1024)
                        sts.append((tt, xst, stkey))
                        p.op("scalar", lambda e, t=t, xst=xst: e.activation(
                            out=xst, in_=X[:, t * 1024:(t + 1) * 1024], func=AF.Identity,
                            scale=QR[:, t:t + 1], bias=QM[:, t:t + 1]),
                            r=[("x", t), ("qm", t // 4), ("qr", t // 4)], w=stkey)
                    for tt, xst, stkey in sts:
                        xl = A[:, XLT0:XLT0 + 4096].rearrange("p (c w) -> p c w", w=512)
                        d3 = xl[:, :, tt * 128:tt * 128 + 128]
                        dk = [k for cc in range(8) for k in AK(XLT0 + cc * 512 + tt * 128, 128)]
                        transpose_tile_128(xst, stkey, d3, dk, 0, bank=tt % 2)
                xlk = AK(XLT0, 4096)
                for dc in range(8):
                    bank = dc % 4
                    for cc in range(8):
                        lhs = A[:, WA0 + cc * 1024 + dc * 128: WA0 + cc * 1024 + dc * 128 + 128]
                        rhs = A[:, XLT0 + cc * 512: XLT0 + (cc + 1) * 512]
                        p.op("tensor", lambda e, lhs=lhs, rhs=rhs, b=bank, cc=cc: e.matmul(
                            PSA[:, b * 512:(b + 1) * 512], lhs, rhs,
                            start=(cc == 0), stop=(cc == 7)),
                            r=AK(WA0, 8192) + xlk + ([] if cc else PK(bank)),
                            w=PK(bank))
                    dst = A[:, QTG + dc * 512: QTG + (dc + 1) * 512]
                    p.op("scalar", lambda e, d=dst, b=bank: e.activation(
                        out=d, in_=PSA[:, b * 512:(b + 1) * 512], func=AF.Copy),
                        r=PK(bank), w=AK(QTG + dc * 512, 512))

            build_qtg(0)
            for qi in range(16):
                nk = 128 * (qi + 1)
                tt_q = qi % 4
                qtk = AK(QTG, 4096)
                nch = 0
                for c0 in range(0, nk, 512):
                    w_ = min(512, nk - c0)
                    bank = c0 // 512
                    for dc in range(8):
                        lhs = A[:, QTG + dc * 512 + tt_q * 128: QTG + dc * 512 + tt_q * 128 + 128]
                        rhs = A[:, KT0 + dc * 2048 + c0: KT0 + dc * 2048 + c0 + w_]
                        p.op("tensor", lambda e, lhs=lhs, rhs=rhs, c0=c0, w_=w_, dc=dc: e.matmul(
                            PSA[:, c0:c0 + w_], lhs, rhs,
                            start=(dc == 0), stop=(dc == 7)),
                            r=qtk + AK(KT0 + 0, 16384) + ([] if dc else PK(bank)),
                            w=PK(bank))
                    if c0 + w_ == nk:
                        p.op("vector", lambda e, nk=nk: e.tensor_tensor(
                            out=PSA[:, nk - 128:nk], in0=PSA[:, nk - 128:nk],
                            in1=CAU, op=ALU.add),
                            r=PK(bank) + [("c", "cau")], w=PK(bank))
                    p.op("vector", lambda e, c0=c0, w_=w_, i=nch, bank=bank: e.reduce_max(
                        TB[:, 8 + i:9 + i], PSA[:, c0:c0 + w_], axis=AX.X),
                        r=PK(bank), w=TK(8 + nch))
                    nch += 1
                p.op("vector", lambda e, n=nch: e.reduce_max(
                    TB[:, 0:1], TB[:, 8:8 + n], axis=AX.X),
                    r=[k for i in range(nch) for k in TK(8 + i)], w=TK(0))
                p.op("vector", lambda e: e.tensor_scalar(
                    out=TB[:, 1:2], in0=TB[:, 0:1], scalar1=-QK_SCALE, scalar2=None,
                    op0=ALU.mult), r=TK(0), w=TK(1))
                nbanks = (nk + 511) // 512
                for b in range(nbanks):
                    c0 = b * 512
                    w_ = min(512, nk - c0)
                    p.op("scalar", lambda e, c0=c0, w_=w_, b=b: e.activation(
                        out=A[:, P0 + c0:P0 + c0 + w_], in_=PSA[:, c0:c0 + w_],
                        func=AF.Exp, scale=QK_SCALE, bias=TB[:, 1:2],
                        accum_out=TB[:, 12 + b:13 + b]),
                        r=PK(b) + TK(1), w=AK(P0 + c0, w_) + TK(12 + b))
                p.op("vector", lambda e, n=nbanks: e.reduce_sum(
                    TB[:, 2:3], TB[:, 12:12 + n], axis=AX.X),
                    r=[k for b in range(nbanks) for k in TK(12 + b)], w=TK(2))
                p.op("vector", lambda e: e.reciprocal(TB[:, 3:4], TB[:, 2:3]),
                     r=TK(2), w=TK(3))
                if tt_q == 3 and qi < 15:
                    build_qtg(qi // 4 + 1)
                for kc in range(qi + 1):
                    sl = kc % 2
                    PB_ = PST if sl == 0 else PSTB
                    bk_ = [("PT", 0)] if sl == 0 else [("PB", 0)]
                    p.op("tensor", lambda e, kc=kc, sl=sl, PB_=PB_: e.transpose(
                        PB_[:, sl * 128:(sl + 1) * 128],
                        A[:, P0 + kc * 128:P0 + (kc + 1) * 128], IDN),
                        r=AK(P0 + kc * 128, 128) + [("c", "idn")], w=bk_)
                    pts = A[:, PT0 + sl * 128:PT0 + (sl + 1) * 128]
                    if sl == 0:
                        p.op("scalar", lambda e, sl=sl, d=pts, PB_=PB_: e.activation(
                            out=d, in_=PB_[:, sl * 128:(sl + 1) * 128], func=AF.Copy),
                            r=bk_, w=AK(PT0 + sl * 128, 128))
                    else:
                        p.op("vector", lambda e, sl=sl, d=pts, PB_=PB_: e.tensor_copy(
                            d, PB_[:, sl * 128:(sl + 1) * 128]),
                            r=bk_, w=AK(PT0 + sl * 128, 128))
                    for dh in range(2):
                        rhs = A[:, V0 + kc * 1024 + dh * 512: V0 + kc * 1024 + (dh + 1) * 512]
                        p.op("tensor", lambda e, kc=kc, dh=dh, lhs=pts, rhs=rhs, qi=qi, sl=sl: e.matmul(
                            PSA[:, 2048 + dh * 512: 2048 + (dh + 1) * 512], lhs, rhs,
                            start=(kc == 0), stop=(kc == qi)),
                            r=AK(PT0 + sl * 128, 128) + AK(V0 + kc * 1024, 1024) +
                              ([] if kc else PK(4 + dh)),
                            w=PK(4 + dh))
                p.op("vector", lambda e: e.bn_stats(BN1[:, 0:6], PSA[:, 2048:2560]),
                     r=PK(4), w=[("bn", 0)])
                p.op("vector", lambda e: e.bn_stats(BN1[:, 6:12], PSA[:, 2560:3072]),
                     r=PK(5), w=[("bn", 0)])
                p.op("vector", lambda e: e.bn_aggr(TB[:, 4:6], BN1),
                     r=[("bn", 0)], w=TK(4, 2))
                p.op("vector", lambda e: e.tensor_mul(TB[:, 6:7], TB[:, 3:4], TB[:, 3:4]),
                     r=TK(3), w=TK(6))
                p.op("vector", lambda e: e.tensor_scalar(
                    out=TB[:, 16:17], in0=TB[:, 5:6], scalar1=TB[:, 6:7], scalar2=EPS_LN,
                    op0=ALU.mult, op1=ALU.add), r=TK(5) + TK(6), w=TK(16))
                p.op("scalar", lambda e: e.activation(
                    out=TB[:, 17:18], in_=TB[:, 16:17], func=AF.Ln), r=TK(16), w=TK(17))
                p.op("scalar", lambda e: e.activation(
                    out=TB[:, 18:19], in_=TB[:, 17:18], func=AF.Exp, scale=-0.5),
                    r=TK(17), w=TK(18))
                p.op("vector", lambda e: e.tensor_mul(TB[:, 19:20], TB[:, 3:4], TB[:, 18:19]),
                     r=TK(3) + TK(18), w=TK(19))
                p.op("vector", lambda e: e.scalar_tensor_tensor(
                    out=TB[:, 20:21], in0=TB[:, 4:5], scalar=-1.0, in1=TB[:, 19:20],
                    op0=ALU.mult, op1=ALU.mult), r=TK(4) + TK(19), w=TK(20))
                p.op("scalar", lambda e: e.activation(
                    out=FA[:, FT0:FT0 + 1024], in_=PSA[:, 2048:3072],
                    func=AF.Identity, scale=TB[:, 19:20], bias=TB[:, 20:21]),
                    r=PK(4, 2) + TK(19) + TK(20), w=FK(FT0, 1024))
                p.op("gpsimd", lambda e, qi=qi: e.tensor_tensor(
                    out=X[:, qi * 1024:(qi + 1) * 1024],
                    in0=FA[:, FT0:FT0 + 1024],
                    in1=X[:, qi * 1024:(qi + 1) * 1024], op=ALU.add),
                    r=FK(FT0, 1024) + [("x", qi)], w=[("x", qi)])

            def ffn_prep_group(layer, g):
                tiles = [4 * g + tt for tt in range(4)]
                ln_stats_batch2(
                    [X[:, t * 1024:(t + 1) * 1024] for t in tiles],
                    [[("x", t)] for t in tiles],
                    QM[:, 4 * g:], QR[:, 4 * g:], ("qm", g), ("qr", g))
                for pair in range(2):
                    sts = []
                    for tt2 in range(2):
                        t = tiles[pair * 2 + tt2]
                        xst = A[:, XST0 + tt2 * 1024: XST0 + tt2 * 1024 + 1024]
                        stkey = AK(XST0 + tt2 * 1024, 1024)
                        sts.append((t, xst, stkey))
                        p.op("scalar", lambda e, t=t, g=g, xst=xst: e.activation(
                            out=xst, in_=X[:, t * 1024:(t + 1) * 1024], func=AF.Identity,
                            scale=QR[:, t:t + 1], bias=QM[:, t:t + 1]),
                            r=[("x", t), ("qm", g), ("qr", g)], w=stkey)
                    for t, xst, stkey in sts:
                        xl = A[:, FXLT:FXLT + 16384].rearrange("p (c w) -> p c w", w=2048)
                        d3 = xl[:, :, t * 128:t * 128 + 128]
                        dk = [k for cc in range(8) for k in AK(FXLT + cc * 2048 + t * 128, 128)]
                        transpose_tile_128(xst, stkey, d3, dk, 0)

            def ffn_whalf_load(layer, half):
                sgp = s_ffn["g%d" % layer]
                sup = s_ffn["u%d" % layer]
                gk, uk = ("w", "g%d" % layer), ("w", "u%d" % layer)
                slg = next_wslot()
                slu = next_wslot()
                if s == 0:
                    for cc in range(8):
                        dsl = A[:, FWGH + cc * 1536: FWGH + (cc + 1) * 1536]
                        srcq = sgp[half * 1536:(half + 1) * 1536, cc * 128:(cc + 1) * 128]
                        p.dma("sync", lambda e, d=dsl, sr=srcq: e.dma_start(
                            out=d, in_=sr, transpose=True),
                            r=[gk], w=AK(FWGH + cc * 1536, 1536), slot=slg)
                        dsl2 = A[:, FWUH + cc * 1536: FWUH + (cc + 1) * 1536]
                        srcq2 = sup[half * 1536:(half + 1) * 1536, cc * 128:(cc + 1) * 128]
                        p.dma("sync", lambda e, d=dsl2, sr=srcq2: e.dma_start(
                            out=d, in_=sr, transpose=True),
                            r=[uk], w=AK(FWUH + cc * 1536, 1536), slot=slu)
                    p.dma("gpsimd", lambda e, layer=layer, half=half: e.dma_start(
                        out=wgt[layer][:, half * 12288:(half + 1) * 12288],
                        in_=A[:, FWGH:FWGH + 12288]),
                        r=AK(FWGH, 12288), w=[("wgt", layer, half)], slot="wds")
                    p.dma("gpsimd", lambda e, layer=layer, half=half: e.dma_start(
                        out=wut[layer][:, half * 12288:(half + 1) * 12288],
                        in_=A[:, FWUH:FWUH + 12288]),
                        r=AK(FWUH, 12288), w=[("wut", layer, half)], slot="wds")
                else:
                    p.dma("sync", lambda e, layer=layer, half=half: e.dma_start(
                        out=A[:, FWGH:FWGH + 12288],
                        in_=wgt[layer][:, half * 12288:(half + 1) * 12288]),
                        r=[("wgt", layer, half)], w=AK(FWGH, 12288), slot=slg)
                    p.dma("sync", lambda e, layer=layer, half=half: e.dma_start(
                        out=A[:, FWUH:FWUH + 12288],
                        in_=wut[layer][:, half * 12288:(half + 1) * 12288]),
                        r=[("wut", layer, half)], w=AK(FWUH, 12288), slot=slu)

            def ffn_pass1_group(layer, half, g):
                if True:
                    if True:
                        hbase = FHTH + (g % 2) * 6144
                        for hc in range(12):
                            i_ = half * 48 + g * 12 + hc
                            pg = (2 * i_) % 3
                            pu = (2 * i_ + 1) % 3
                            for wsel, wcol, pp in ((0, FWGH, pg), (1, FWUH, pu)):
                                for cc in range(8):
                                    lhs = A[:, wcol + cc * 1536 + hc * 128: wcol + cc * 1536 + hc * 128 + 128]
                                    rhs = A[:, FXLT + cc * 2048 + g * 512: FXLT + cc * 2048 + g * 512 + 512]
                                    p.op("tensor", lambda e, lhs=lhs, rhs=rhs, pp=pp, cc=cc: e.matmul(
                                        PSA[:, pp * 1024: pp * 1024 + 512], lhs, rhs,
                                        start=(cc == 0), stop=(cc == 7)),
                                        r=AK(wcol + cc * 1536 + hc * 128, 128) +
                                          AK(FXLT + cc * 2048 + g * 512, 512) +
                                          ([] if cc else PK(2 * pp)),
                                        w=PK(2 * pp))
                            sgo = FT3 + (hc % 2) * 512
                            p.op("scalar", lambda e, pg=pg, sgo=sgo: e.activation(
                                out=FA[:, sgo:sgo + 512], in_=PSA[:, pg * 1024: pg * 1024 + 512],
                                func=AF.Silu, scale=DQ),
                                r=PK(2 * pg), w=FK(sgo, 512))
                            hdst = A[:, hbase + hc * 512: hbase + (hc + 1) * 512]
                            p.op("vector", lambda e, pu=pu, d=hdst, sgo=sgo: e.scalar_tensor_tensor(
                                out=d, in0=PSA[:, pu * 1024: pu * 1024 + 512],
                                scalar=DQ, in1=FA[:, sgo:sgo + 512],
                                op0=ALU.mult, op1=ALU.mult),
                                r=PK(2 * pu) + FK(sgo, 512),
                                w=AK(hbase + hc * 512, 512))
                        hsrc = A[:, hbase:hbase + 6144]
                        hdst2 = hs[g][:, half * 6144:(half + 1) * 6144]
                        p.dma("gpsimd", lambda e, sr=hsrc, d=hdst2: e.dma_start(out=d, in_=sr),
                              r=AK(hbase, 6144), w=[("hs", g, half)], slot="ho")

            def ffn_layer2(layer, prep_done=False, half0_done=False,
                           post_group=None):
                sdp = s_ffn["d%d" % layer]
                dk_ = ("w", "d%d" % layer)
                if not prep_done:
                    for g in range(4):
                        ffn_prep_group(layer, g)
                if not prep_done:
                    ffn_whalf_load(layer, 0)
                if not half0_done:
                    for g in range(4):
                        ffn_pass1_group(layer, 0, g)
                ffn_whalf_load(layer, 1)
                for g in range(4):
                    ffn_pass1_group(layer, 1, g)
                for g in range(4):
                    hgbase = FHG + (g % 2) * 12288
                    for hv in range(2):
                        sld = ("hiA0", "hiA1", "hiB0", "hiB1")[(g % 2) * 2 + hv]
                        p.dma("sync", lambda e, g=g, hb=hgbase, hv=hv: e.dma_start(
                            out=A[:, hb + hv * 6144: hb + (hv + 1) * 6144],
                            in_=hs[g][:, hv * 6144:(hv + 1) * 6144]),
                            r=[("hs", g, hv)],
                            w=AK(hgbase + hv * 6144, 6144), slot=sld)
                    if g == 0:
                        slw = next_wslot()
                        slw2 = next_wslot()
                        if s == 0:
                            for hc in range(24):
                                dsl = A[:, FWD + hc * 1024: FWD + (hc + 1) * 1024]
                                srcw = sdp[0:1024, hc * 128:(hc + 1) * 128]
                                p.dma("sync", lambda e, d=dsl, sr=srcw: e.dma_start(
                                    out=d, in_=sr, transpose=True),
                                    r=[dk_], w=AK(FWD + hc * 1024, 1024),
                                    slot=(slw if hc % 2 == 0 else slw2))
                            p.dma("gpsimd", lambda e, layer=layer: e.dma_start(
                                out=wdt[layer][:, :], in_=A[:, FWD:FWD + 24576]),
                                r=AK(FWD, 24576), w=[("wdt", layer)], slot="wds")
                        else:
                            for q in range(4):
                                eng_ = "sync" if q < 2 else "gpsimd"
                                sl_ = (slw, slw2, "wdg", "wdg2")[q]
                                p.dma(eng_, lambda e, layer=layer, q=q: e.dma_start(
                                    out=A[:, FWD + q * 6144:FWD + (q + 1) * 6144],
                                    in_=wdt[layer][:, q * 6144:(q + 1) * 6144]),
                                    r=[("wdt", layer)], w=AK(FWD + q * 6144, 6144),
                                    slot=sl_)
                    for tsub in range(4):
                        t = g * 4 + tsub
                        pp = tsub % 3
                        FTd = FT0 if tsub % 2 == 0 else FT1
                        tbb_ = 4 if tsub % 2 == 0 else 12
                        bnn = BN1 if tsub % 2 == 0 else BN2
                        bnk = ("bn", tsub % 2)
                        for dh in range(2):
                            for hc in range(24):
                                lhs = A[:, hgbase + hc * 512 + tsub * 128: hgbase + hc * 512 + tsub * 128 + 128]
                                rhs = A[:, FWD + hc * 1024 + dh * 512: FWD + hc * 1024 + (dh + 1) * 512]
                                p.op("tensor", lambda e, lhs=lhs, rhs=rhs, pp=pp, dh=dh, hc=hc: e.matmul(
                                    PSA[:, pp * 1024 + dh * 512: pp * 1024 + (dh + 1) * 512],
                                    lhs, rhs, start=(hc == 0), stop=(hc == 23)),
                                    r=AK(hgbase + hc * 512, 512) +
                                      AK(FWD + hc * 1024 + dh * 512, 512) +
                                      ([] if hc else PK(2 * pp + dh)),
                                    w=PK(2 * pp + dh))
                        p.op("vector", lambda e, pp=pp, bnn=bnn: e.bn_stats(
                            bnn[:, 0:6], PSA[:, pp * 1024: pp * 1024 + 512]),
                            r=PK(2 * pp), w=[bnk])
                        p.op("vector", lambda e, pp=pp, bnn=bnn: e.bn_stats(
                            bnn[:, 6:12], PSA[:, pp * 1024 + 512: pp * 1024 + 1024]),
                            r=PK(2 * pp + 1), w=[bnk])
                        p.op("vector", lambda e, b=tbb_, bnn=bnn: e.bn_aggr(
                            TB[:, b:b + 2], bnn), r=[bnk], w=TK(tbb_, 2))
                        p.op("vector", lambda e, b=tbb_: e.tensor_scalar(
                            out=TB[:, b + 3:b + 4], in0=TB[:, b + 1:b + 2], scalar1=DQ * DQ,
                            scalar2=EPS_LN, op0=ALU.mult, op1=ALU.add),
                            r=TK(tbb_ + 1), w=TK(tbb_ + 3))
                        p.op("scalar", lambda e, b=tbb_: e.activation(
                            out=TB[:, b + 4:b + 5], in_=TB[:, b + 3:b + 4], func=AF.Ln),
                            r=TK(tbb_ + 3), w=TK(tbb_ + 4))
                        p.op("scalar", lambda e, b=tbb_: e.activation(
                            out=TB[:, b + 5:b + 6], in_=TB[:, b + 4:b + 5], func=AF.Exp, scale=-0.5),
                            r=TK(tbb_ + 4), w=TK(tbb_ + 5))
                        p.op("vector", lambda e, b=tbb_: e.tensor_scalar(
                            out=TB[:, b + 6:b + 7], in0=TB[:, b + 5:b + 6], scalar1=DQ,
                            scalar2=None, op0=ALU.mult), r=TK(tbb_ + 5), w=TK(tbb_ + 6))
                        p.op("vector", lambda e, pp=pp, b=tbb_, FTd=FTd: e.tensor_scalar(
                            out=FA[:, FTd:FTd + 1024], in0=PSA[:, pp * 1024: pp * 1024 + 1024],
                            scalar1=TB[:, b:b + 1], scalar2=TB[:, b + 6:b + 7],
                            op0=ALU.subtract, op1=ALU.mult),
                            r=PK(2 * pp, 2) + TK(tbb_) + TK(tbb_ + 6), w=FK(FTd, 1024))
                        p.op("gpsimd", lambda e, t=t, FTd=FTd: e.tensor_tensor(
                            out=X[:, t * 1024:(t + 1) * 1024],
                            in0=FA[:, FTd:FTd + 1024],
                            in1=X[:, t * 1024:(t + 1) * 1024], op=ALU.add),
                            r=FK(FTd, 1024) + [("x", t)], w=[("x", t)])
                    if post_group is not None:
                        post_group(g)

            def l1a_stats_group(g_):
                tiles = [4 * g_ + tt for tt in range(4)]
                ln_stats_batch2(
                    [X[:, t * 1024:(t + 1) * 1024] for t in tiles],
                    [[("x", t)] for t in tiles],
                    QM[:, 4 * g_:], QR[:, 4 * g_:], ("qm", g_), ("qr", g_),
                    rho=(QM2[:, 4 * g_:], QR2[:, 4 * g_:], ("qm2", g_), ("qr2", g_)))

            ffn_layer2(0, post_group=l1a_stats_group)

            p.op("scalar", lambda e: e.activation(
                out=FA[:, FT2:FT2 + 1024], in_=X[:, 15 * 1024:16 * 1024],
                func=AF.Identity, scale=QR[:, 15:16], bias=QM[:, 15:16]),
                r=[("x", 15), ("qm", 3), ("qr", 3)], w=FK(FT2, 1024))
            p.op("scalar", lambda e: e.activation(
                out=A[96:128, LXQB:LXQB + 1024], in_=FA[96:128, FT2:FT2 + 1024],
                func=AF.Copy), r=FK(FT2, 1024), w=AK(LXQB, 1024))
            for cc in range(8):
                p.op("tensor", lambda e, cc=cc: e.transpose(
                    PST[:, cc * 128:cc * 128 + 32],
                    A[96:128, LXQB + cc * 128:LXQB + (cc + 1) * 128],
                    IDN[96:128, 96:128], tile_position=(96, 0)),
                    r=AK(LXQB, 1024) + [("c", "idn")], w=[("PT", 0)])
            src3q = PST[:, 0:1024].rearrange("p (c t) -> p c t", t=128)[:, :, 0:32]
            dst3q = A[:, LXQT:LXQT + 256].rearrange("p (c t) -> p c t", t=32)
            p.op("scalar", lambda e, s3=src3q, d3=dst3q: e.activation(
                out=d3, in_=s3, func=AF.Copy),
                r=[("PT", 0)], w=AK(LXQT, 256))
            xqtk = AK(LXQT, 256)

            def l1_proj(scr, wkey, wsl_off, dst_off, dst_key_off, l2n):
                slw = next_wslot()
                for cc in range(8):
                    dsl = A[:, wsl_off + cc * 1024: wsl_off + (cc + 1) * 1024]
                    srcw = scr[cc * 128:(cc + 1) * 128, :]
                    p.dma("sync", lambda e, d=dsl, sr=srcw: e.dma_start(out=d, in_=sr),
                          r=[wkey], w=AK(wsl_off + cc * 1024, 1024), slot=slw)
                wk_ = AK(wsl_off, 8192)
                for dh in range(2):
                    for cc in range(8):
                        lhs = A[:, LXQT + cc * 32: LXQT + (cc + 1) * 32]
                        rhs = A[:, wsl_off + cc * 1024 + dh * 512: wsl_off + cc * 1024 + (dh + 1) * 512]
                        p.op("tensor", lambda e, lhs=lhs, rhs=rhs, dh=dh, cc=cc: e.matmul(
                            PSA[0:32, dh * 512:(dh + 1) * 512], lhs, rhs,
                            start=(cc == 0), stop=(cc == 7)),
                            r=wk_ + xqtk + ([] if cc else PK(dh)), w=PK(dh))
                if not l2n:
                    p.op("scalar", lambda e, d=dst_off: e.activation(
                        out=A[0:32, d:d + 1024], in_=PSA[0:32, 0:1024], func=AF.Copy),
                        r=PK(0, 2), w=AK(dst_off, 1024))
                    return
                p.op("vector", lambda e: e.tensor_copy(FA[0:32, FT0:FT0 + 1024],
                                                       PSA[0:32, 0:1024]),
                     r=PK(0, 2), w=FK(FT0, 1024))
                p.op("scalar", lambda e: e.activation(
                    out=FA[0:32, FT1:FT1 + 1024], in_=FA[0:32, FT0:FT0 + 1024],
                    func=AF.Square), r=FK(FT0, 1024), w=FK(FT1, 1024))
                sq3 = FA[0:32, FT1:FT1 + 1024].rearrange("p (h d) -> p h d", d=64)
                p.op("vector", lambda e, s3=sq3: e.reduce_sum(
                    NRM[0:32, 0:16], s3, axis=AX.X), r=FK(FT1, 1024), w=[("nrm", 0)])
                p.op("vector", lambda e: e.tensor_scalar(
                    out=NRM[0:32, 0:16], in0=NRM[0:32, 0:16], scalar1=1e-30,
                    scalar2=None, op0=ALU.add), r=[("nrm", 0)], w=[("nrm", 0)])
                p.op("scalar", lambda e: e.activation(
                    out=NRM[0:32, 16:32], in_=NRM[0:32, 0:16], func=AF.Ln),
                    r=[("nrm", 0)], w=[("nrm", 1)])
                p.op("scalar", lambda e: e.activation(
                    out=NRM[0:32, 16:32], in_=NRM[0:32, 16:32], func=AF.Exp, scale=0.5),
                    r=[("nrm", 1)], w=[("nrm", 1)])
                p.op("vector", lambda e: e.tensor_scalar(
                    out=NRM[0:32, 16:32], in0=NRM[0:32, 16:32], scalar1=EPS_NORM,
                    scalar2=None, op0=ALU.add), r=[("nrm", 1)], w=[("nrm", 1)])
                p.op("vector", lambda e: e.reciprocal(NRM[0:32, 32:48], NRM[0:32, 16:32]),
                     r=[("nrm", 1)], w=[("nrm", 2)])
                for hh in range(16):
                    p.op("vector", lambda e, hh=hh, d=dst_off: e.tensor_scalar(
                        out=A[0:32, d + hh * 64: d + (hh + 1) * 64],
                        in0=FA[0:32, FT0 + hh * 64: FT0 + (hh + 1) * 64],
                        scalar1=NRM[0:32, 32 + hh:33 + hh], scalar2=None,
                        op0=ALU.mult),
                        r=FK(FT0, 1024) + [("nrm", 2)], w=AK(dst_off + hh * 64, 64))

            l1_proj(s_att["wq1"], ("w", "wq1"), LW0, LQH, 0, True)
            l1_proj(s_att["wk1"], ("w", "wk1"), LW1, LKH, 0, True)
            l1_proj(s_att["wkp1"], ("w", "wkp1"), LW0, LKPH, 0, True)
            l1_proj(s_att["wv1"], ("w", "wv1"), LW1, LVH, 0, False)

            for cc in range(8):
                p.op("tensor", lambda e, cc=cc: e.transpose(
                    PST[:, cc * 128:cc * 128 + 32],
                    A[0:32, LQH + cc * 128: LQH + (cc + 1) * 128],
                    IDN[0:32, 0:32]),
                    r=AK(LQH, 1024) + [("c", "idn")], w=[("PT", 0)])
            src3q = PST[:, 0:1024].rearrange("p (c t) -> p c t", t=128)[:, :, 0:32]
            dst3q = A[:, LQHT:LQHT + 256].rearrange("p (c t) -> p c t", t=32)
            p.op("scalar", lambda e, s3=src3q, d3=dst3q: e.activation(
                out=d3, in_=s3, func=AF.Copy),
                r=[("PT", 0)], w=AK(LQHT, 256))

            for cc in range(8):
                lhs1 = A[0:32, LKH + cc * 128: LKH + (cc + 1) * 128]
                for dh in range(2):
                    p.op("tensor", lambda e, lhs=lhs1, dh=dh: e.matmul(
                        PSA[:, dh * 512:(dh + 1) * 512], lhs,
                        E1[:, dh * 512:(dh + 1) * 512], start=True, stop=True),
                        r=AK(LKH, 1024) + [("c", "e1")] + PK(dh), w=PK(dh))
                lhs2 = A[0:32, LKPH + cc * 128: LKPH + (cc + 1) * 128]
                for dh in range(2):
                    p.op("tensor", lambda e, lhs=lhs2, dh=dh: e.matmul(
                        PSA[:, 1024 + dh * 512: 1024 + (dh + 1) * 512], lhs,
                        E2[:, dh * 512:(dh + 1) * 512], start=True, stop=True),
                        r=AK(LKPH, 1024) + [("c", "e2")] + PK(2 + dh), w=PK(2 + dh))
                rep1s = A[:, XST0:XST0 + 1024]
                p.op("scalar", lambda e, d=rep1s: e.activation(
                    out=d, in_=PSA[:, 0:1024], func=AF.Copy),
                    r=PK(0, 2), w=AK(XST0, 1024))
                dst = A[:, LW2K + cc * 1024: LW2K + (cc + 1) * 1024]
                p.op("vector", lambda e, d=dst, r1=rep1s: e.tensor_tensor(
                    out=d, in0=PSA[:, 1024:2048], in1=r1, op=ALU.mult),
                    r=PK(2, 2) + AK(XST0, 1024), w=AK(LW2K + cc * 1024, 1024))
            for cc in range(8):
                lhs1 = E1[:, cc * 128:(cc + 1) * 128]
                for dh in range(2):
                    p.op("tensor", lambda e, lhs=lhs1, dh=dh: e.matmul(
                        PSA[:, dh * 512:(dh + 1) * 512], lhs,
                        A[0:32, LVH:LVH + 1024][:, dh * 512:(dh + 1) * 512],
                        start=True, stop=True),
                        r=AK(LVH, 1024) + [("c", "e1")] + PK(dh), w=PK(dh))
                lhs2 = E2[:, cc * 128:(cc + 1) * 128]
                for dh in range(2):
                    p.op("tensor", lambda e, lhs=lhs2, dh=dh: e.matmul(
                        PSA[:, 1024 + dh * 512: 1024 + (dh + 1) * 512], lhs,
                        A[0:32, LVH:LVH + 1024][:, dh * 512:(dh + 1) * 512],
                        start=True, stop=True),
                        r=AK(LVH, 1024) + [("c", "e2")] + PK(2 + dh), w=PK(2 + dh))
                rep1s = A[:, XST0:XST0 + 1024]
                p.op("scalar", lambda e, d=rep1s: e.activation(
                    out=d, in_=PSA[:, 0:1024], func=AF.Copy),
                    r=PK(0, 2), w=AK(XST0, 1024))
                dst = A[:, LW2V + cc * 1024: LW2V + (cc + 1) * 1024]
                p.op("vector", lambda e, d=dst, r1=rep1s: e.tensor_tensor(
                    out=d, in0=PSA[:, 1024:2048], in1=r1, op=ALU.mult),
                    r=PK(2, 2) + AK(XST0, 1024), w=AK(LW2V + cc * 1024, 1024))

            LP4S = 40960
            for b in range(4):
                for j in range(4):
                    hh = 4 * b + j
                    cc = hh // 2
                    po = (hh % 2) * 64
                    lhs = A[:, LQHT + cc * 32: LQHT + (cc + 1) * 32][po:po + 64, :]
                    for dh in range(2):
                        rhs = A[:, LW2K + cc * 1024 + dh * 512: LW2K + cc * 1024 + (dh + 1) * 512][po:po + 64, :]
                        p.op("tensor", lambda e, lhs=lhs, rhs=rhs, dh=dh, j=j, po=po: e.matmul(
                            PSA[32 * j:32 * j + 32, dh * 512:(dh + 1) * 512], lhs, rhs,
                            start=True, stop=True, tile_position=(po, 32 * j)),
                            r=AK(LQHT, 256) + AK(LW2K, 8192) + PK(dh), w=PK(dh))
                p.op("vector", lambda e: e.tensor_tensor(
                    out=PSA[:, 1024:2048], in0=PSA[:, 0:1024], in1=MSK4,
                    op=ALU.add), r=PK(0, 2) + [("c", "msk")], w=PK(2, 2))
                p.op("vector", lambda e: e.reduce_max(
                    TB[:, 0:1], PSA[:, 1024:2048], axis=AX.X),
                    r=PK(2, 2), w=TK(0))
                p.op("vector", lambda e: e.tensor_scalar(
                    out=TB[:, 1:2], in0=TB[:, 0:1], scalar1=-A_SCALE,
                    scalar2=None, op0=ALU.mult), r=TK(0), w=TK(1))
                p.op("scalar", lambda e: e.activation(
                    out=A[:, LPH:LPH + 1024], in_=PSA[:, 1024:2048],
                    func=AF.Exp, scale=A_SCALE, bias=TB[:, 1:2],
                    accum_out=TB[:, 2:3]),
                    r=PK(2, 2) + TK(1), w=AK(LPH, 1024) + TK(2))
                p.op("vector", lambda e: e.reciprocal(
                    TB[:, 3:4], TB[:, 2:3]), r=TK(2), w=TK(3))
                p.op("vector", lambda e: e.tensor_scalar(
                    out=A[:, LP4S:LP4S + 1024], in0=A[:, LPH:LPH + 1024],
                    scalar1=TB[:, 3:4], scalar2=None, op0=ALU.mult),
                    r=AK(LPH, 1024) + TK(3), w=AK(LP4S, 1024))
                for j in range(4):
                    hh = 4 * b + j
                    for c2 in range(8):
                        p.op("tensor", lambda e, c2=c2, j=j: e.transpose(
                            PST[:, c2 * 128:c2 * 128 + 32],
                            A[32 * j:32 * j + 32, LP4S + c2 * 128: LP4S + (c2 + 1) * 128],
                            IDN[32 * j:32 * j + 32, 32 * j:32 * j + 32],
                            tile_position=(32 * j, 0)),
                            r=AK(LP4S, 1024) + [("c", "idn")], w=[("PT", 0)])
                    src3p = PST[:, 0:1024].rearrange("p (c t) -> p c t", t=128)[:, :, 0:32]
                    dst3p = A[:, LPHT:LPHT + 256].rearrange("p (c t) -> p c t", t=32)
                    p.op("scalar", lambda e, s3=src3p, d3=dst3p: e.activation(
                        out=d3, in_=s3, func=AF.Copy),
                        r=[("PT", 0)], w=AK(LPHT, 256))
                    zc = (hh % 8) * 64
                    for c2 in range(8):
                        lhs = A[:, LPHT + c2 * 32: LPHT + (c2 + 1) * 32]
                        rhs = A[:, LW2V + c2 * 1024 + hh * 64: LW2V + c2 * 1024 + (hh + 1) * 64]
                        p.op("tensor", lambda e, lhs=lhs, rhs=rhs, zc=zc, c2=c2: e.matmul(
                            PSB[0:32, zc:zc + 64], lhs, rhs,
                            start=(c2 == 0), stop=(c2 == 7)),
                            r=AK(LPHT, 256) + AK(LW2V, 8192) + ([] if c2 else [("PB", 0)]),
                            w=[("PB", 0)])
                if b % 2 == 1:
                    zb = (b // 2) * 512
                    p.op("scalar", lambda e, zb=zb: e.activation(
                        out=A[0:32, LZH + zb: LZH + zb + 512], in_=PSB[0:32, 0:512],
                        func=AF.Copy),
                        r=[("PB", 0)], w=AK(LZH + zb, 512))

            for cc in range(8):
                p.op("tensor", lambda e, cc=cc: e.transpose(
                    PST[:, cc * 128:cc * 128 + 32],
                    A[0:32, LZH + cc * 128: LZH + (cc + 1) * 128],
                    IDN[0:32, 0:32]),
                    r=AK(LZH, 1024) + [("c", "idn")], w=[("PT", 0)])
            src3z = PST[:, 0:1024].rearrange("p (c t) -> p c t", t=128)[:, :, 0:32]
            dst3z = A[:, LZT:LZT + 256].rearrange("p (c t) -> p c t", t=32)
            p.op("scalar", lambda e, s3=src3z, d3=dst3z: e.activation(
                out=d3, in_=s3, func=AF.Copy),
                r=[("PT", 0)], w=AK(LZT, 256))
            slw = next_wslot()
            for cc in range(8):
                dsl = A[:, LW0 + cc * 1024: LW0 + (cc + 1) * 1024]
                srcw = s_att["wo1"][cc * 128:(cc + 1) * 128, :]
                p.dma("sync", lambda e, d=dsl, sr=srcw: e.dma_start(out=d, in_=sr),
                      r=[("w", "wo1")], w=AK(LW0 + cc * 1024, 1024), slot=slw)
            for dh in range(2):
                for cc in range(8):
                    lhs = A[:, LZT + cc * 32: LZT + (cc + 1) * 32]
                    rhs = A[:, LW0 + cc * 1024 + dh * 512: LW0 + cc * 1024 + (dh + 1) * 512]
                    p.op("tensor", lambda e, lhs=lhs, rhs=rhs, dh=dh, cc=cc: e.matmul(
                        PSA[96:128, 2048 + dh * 512: 2048 + (dh + 1) * 512], lhs, rhs,
                        start=(cc == 0), stop=(cc == 7), tile_position=(0, 96)),
                        r=AK(LZT, 256) + AK(LW0, 8192) + ([] if cc else PK(4 + dh)),
                        w=PK(4 + dh))
            p.op("vector", lambda e: e.tensor_tensor(
                out=FA[96:128, FT2:FT2 + 1024], in0=PSA[96:128, 2048:3072],
                in1=FA[96:128, FT2:FT2 + 1024], op=ALU.add),
                r=PK(4, 2) + FK(FT2, 1024), w=FK(FT2, 1024))
            ffn_whalf_load(1, 0)

            chain_tiles = [15] + list(range(15))

            def chain_bufs(pos):
                par = pos % 2
                S = FT2 if par == 0 else FT3
                W = FT0 if par == 0 else FT1
                tb0 = 4 if par == 0 else 12
                bn = BN1 if par == 0 else BN2
                return par, S, W, tb0, bn

            def chain_rstd(tb0):
                tbv, tbl, tbr, tbb = tb0 + 3, tb0 + 4, tb0 + 5, tb0 + 7
                p.op("vector", lambda e: e.tensor_scalar(
                    out=TB[:, tbv:tbv + 1], in0=TB[:, tb0 + 1:tb0 + 2],
                    scalar1=1.0, scalar2=EPS_LN,
                    op0=ALU.mult, op1=ALU.add), r=TK(tb0 + 1), w=TK(tbv))
                p.op("scalar", lambda e: e.activation(
                    out=TB[:, tbl:tbl + 1], in_=TB[:, tbv:tbv + 1], func=AF.Ln),
                    r=TK(tbv), w=TK(tbl))
                p.op("scalar", lambda e: e.activation(
                    out=TB[:, tbr:tbr + 1], in_=TB[:, tbl:tbl + 1], func=AF.Exp,
                    scale=-0.5), r=TK(tbl), w=TK(tbr))
                p.op("vector", lambda e: e.scalar_tensor_tensor(
                    out=TB[:, tbb:tbb + 1], in0=TB[:, tb0:tb0 + 1], scalar=-1.0,
                    in1=TB[:, tbr:tbr + 1],
                    op0=ALU.mult, op1=ALU.mult), r=TK(tb0) + TK(tbr), w=TK(tbb))

            def chain_stageA(pos):
                t = chain_tiles[pos]
                par, S, W, tb0, bn = chain_bufs(pos)
                tbr, tbb = tb0 + 5, tb0 + 7
                Sk, Wk_ = FK(S, 1024), FK(W, 1024)
                bkey = ("bn", par)
                if t == 15:
                    p.op("vector", lambda e, S=S, b=bn: e.bn_stats(
                        b[:, 0:6], FA[:, S:S + 512]), r=Sk, w=[bkey])
                    p.op("vector", lambda e, S=S, b=bn: e.bn_stats(
                        b[:, 6:12], FA[:, S + 512:S + 1024]), r=Sk, w=[bkey])
                    p.op("vector", lambda e, b=bn, tb0=tb0: e.bn_aggr(
                        TB[:, tb0:tb0 + 2], b), r=[bkey], w=TK(tb0, 2))
                    chain_rstd(tb0)
                    p.op("scalar", lambda e, S=S, W=W, tbr=tbr, tbb=tbb: e.activation(
                        out=FA[:, W:W + 1024], in_=FA[:, S:S + 1024],
                        func=AF.Identity, scale=TB[:, tbr:tbr + 1],
                        bias=TB[:, tbb:tbb + 1]),
                        r=Sk + TK(tbr) + TK(tbb), w=Wk_)
                else:
                    p.op("scalar", lambda e, t=t, W=W: e.activation(
                        out=FA[:, W:W + 1024], in_=X[:, t * 1024:(t + 1) * 1024],
                        func=AF.Identity, scale=QR2[:, t:t + 1], bias=QM2[:, t:t + 1]),
                        r=[("x", t), ("qm2", t // 4), ("qr2", t // 4)], w=Wk_)
                p.op("vector", lambda e, S=S, W=W: e.tensor_mul(
                    FA[:, S:S + 1024], FA[:, W:W + 1024], GAM),
                    r=Wk_ + [("c", "gam")], w=Sk)
                p.op("vector", lambda e, S=S, W=W: e.tensor_tensor(
                    out=FA[:, W:W + 1024], in0=FA[:, S:S + 1024],
                    in1=BET, op=ALU.add),
                    r=Sk + [("c", "bet")], w=Wk_)
                p.op("vector", lambda e, W=W, b=bn: e.bn_stats(
                    b[:, 0:6], FA[:, W:W + 512]), r=Wk_, w=[bkey])
                p.op("vector", lambda e, W=W, b=bn: e.bn_stats(
                    b[:, 6:12], FA[:, W + 512:W + 1024]), r=Wk_, w=[bkey])
                p.op("vector", lambda e, b=bn, tb0=tb0: e.bn_aggr(
                    TB[:, tb0:tb0 + 2], b), r=[bkey], w=TK(tb0, 2))
                chain_rstd(tb0)

            def chain_stageB(pos):
                t = chain_tiles[pos]
                par, S, W, tb0, bn = chain_bufs(pos)
                tbr, tbb = tb0 + 5, tb0 + 7
                Sk, Wk_ = FK(S, 1024), FK(W, 1024)
                p.op("scalar", lambda e, S=S, W=W, tbr=tbr, tbb=tbb: e.activation(
                    out=FA[:, S:S + 1024], in_=FA[:, W:W + 1024],
                    func=AF.Identity, scale=TB[:, tbr:tbr + 1], bias=TB[:, tbb:tbb + 1]),
                    r=Wk_ + TK(tbr) + TK(tbb), w=Sk)
                p.op("vector", lambda e, t=t, S=S: e.tensor_tensor(
                    out=X[:, t * 1024:(t + 1) * 1024], in0=FA[:, S:S + 1024],
                    in1=X[:, t * 1024:(t + 1) * 1024], op=ALU.add),
                    r=Sk + [("x", t)], w=[("x", t)])
                if t in (3, 7, 11, 14):
                    g_done = 3 if t == 14 else t // 4
                    ffn_prep_group(1, g_done)
                p.op("tensor", lambda e: e.matmul(
                    PSB[:, 0:128], IDN, IDN, start=True, stop=True),
                    r=[("c", "idn")], w=[("PB", 0)])

            CHAIN_PIPE = True
            if CHAIN_PIPE:
                chain_stageA(0)
                for pos in range(16):
                    if pos + 1 < 16:
                        chain_stageA(pos + 1)
                    chain_stageB(pos)
            else:
                for pos in range(16):
                    chain_stageA(pos)
                    chain_stageB(pos)

            def store_group(g_):
                for t in range(4 * g_, 4 * g_ + 4):
                    dst = out[xoff + t * 128: xoff + (t + 1) * 128, :]
                    p.dma("gpsimd", lambda e, t=t, dst=dst: e.dma_start(
                        out=dst, in_=X[:, t * 1024:(t + 1) * 1024]),
                        r=[("x", t)], slot="out")

            ffn_layer2(1, prep_done=True, post_group=store_group)

        if OP_LIMIT is not None:
            p.truncate(OP_LIMIT)
            for t in range(16):
                dst = out[t * 128:(t + 1) * 128, :]
                p.dma("gpsimd", lambda e, t=t, dst=dst: e.dma_start(
                    out=dst, in_=X[:, t * 1024:(t + 1) * 1024]), r=(), slot="out")
        p.build(nc, es, final_fences=[("gpsimd", "out")])
    return nc


_CACHE = {}


def kernel(**inputs):
    if "nc" not in _CACHE:
        _CACHE["nc"] = build_program()
    nc = _CACHE["nc"]
    x = np.ascontiguousarray(np.asarray(inputs["L0_attn_x"], dtype=np.float32))
    gam = np.ascontiguousarray(np.broadcast_to(
        np.asarray(inputs["L1_attn_ln_gamma"], dtype=np.float32), (128, 1024)))
    bet = np.ascontiguousarray(np.broadcast_to(
        np.asarray(inputs["L1_attn_ln_beta"], dtype=np.float32), (128, 1024)))
    ident = np.eye(128, dtype=ml_dtypes.bfloat16)
    causal = (np.triu(np.ones((128, 128), np.float32), k=1) * NEG).astype(np.float32)
    jj = np.arange(1024) // 32
    kk = np.arange(1024) % 32
    ss = np.arange(32)[:, None]
    maska = np.where((jj[None, :] > ss) | (kk[None, :] > ss), NEG, 0.0).astype(np.float32)
    maska = np.ascontiguousarray(np.tile(maska, (4, 1)))
    e1 = (jj[None, :] == np.arange(32)[:, None]).astype(ml_dtypes.bfloat16)
    e2 = (kk[None, :] == np.arange(32)[:, None]).astype(ml_dtypes.bfloat16)

    common = {
        "wq0": np.asarray(inputs["L0_attn_Wq"], np.float32),
        "wk0": np.asarray(inputs["L0_attn_Wk"], np.float32),
        "wv0": np.asarray(inputs["L0_attn_Wv"], np.float32),
        "g0": np.asarray(inputs["L0_ffn_w_gate"], np.int32),
        "u0": np.asarray(inputs["L0_ffn_w_up"], np.int32),
        "d0": np.asarray(inputs["L0_ffn_w_down"], np.int32),
        "wq1": np.asarray(inputs["L1_attn_W_Q"], np.float32),
        "wk1": np.asarray(inputs["L1_attn_W_K"], np.float32),
        "wv1": np.asarray(inputs["L1_attn_W_V"], np.float32),
        "wkp1": np.asarray(inputs["L1_attn_W_Kp"], np.float32),
        "wo1": np.asarray(inputs["L1_attn_W_O"], np.float32),
        "g1": np.asarray(inputs["L1_ffn_w_gate"], np.int32),
        "u1": np.asarray(inputs["L1_ffn_w_up"], np.int32),
        "d1": np.asarray(inputs["L1_ffn_w_down"], np.int32),
        "gamma": gam, "beta": bet, "identd": ident, "causald": causal,
        "e1d": e1, "e2d": e2, "maskad": maska,
    }
    in_maps = []
    for c in range(8):
        m = dict(common)
        m["x_in"] = np.ascontiguousarray(x[2 * c:2 * c + 2].reshape(4096, 1024))
        in_maps.append(m)
    res = run_bass_kernel_spmd(nc, in_maps, core_ids=list(range(8)))
    outs = [res.results[c]["out"].reshape(2, 2048, 1024) for c in range(8)]
    return np.concatenate(outs, axis=0)


# version 39 (final)
# speedup vs baseline: 1.0841x; 1.0394x over previous
import contextlib
import numpy as np
import ml_dtypes
import concourse.bass as bass
import concourse.mybir as mybir
from concourse.bass_utils import run_bass_kernel_spmd


F32 = mybir.dt.float32
BF16 = mybir.dt.bfloat16
I32 = mybir.dt.int32
AF = mybir.ActivationFunctionType
ALU = mybir.AluOpType
AX = mybir.AxisListType

ENGINES = ("tensor", "vector", "scalar", "gpsimd", "sync")

N_SEQS = 2
STAGES = ("l0a", "ffn0", "l1a", "ffn1")
OP_LIMIT = None

NEG = -1.0e30
EPS_LN = 1e-5
EPS_NORM = 1e-7
QK_SCALE = 1.0 / 32.0
DQ = 1.0 / 127.0
A_SCALE = 0.125


class Prog:

    def __init__(self):
        self.ops = []
        self.lw = {}
        self.rd = {}
        self.slot_n = {}

    def op(self, eng, fn, r=(), w=(), slot=None):
        idx = len(self.ops)
        deps = set()
        for k in r:
            if k in self.lw:
                deps.add(self.lw[k])
        for k in w:
            if k in self.lw:
                deps.add(self.lw[k])
            deps |= self.rd.get(k, set())
        deps.discard(idx)
        for d in list(deps):
            pd = self.ops[d]
            if pd["slot"] is not None and pd["slot"] != slot:
                self.rd.setdefault(("slot", pd["slot"]), set()).add(idx)
        if slot is not None:
            deps |= self.rd.get(("slot", slot), set())
            deps.discard(idx)
            self.lw[("slot", slot)] = idx
            self.rd[("slot", slot)] = set()
        for k in r:
            self.rd.setdefault(k, set()).add(idx)
        for k in w:
            self.lw[k] = idx
            self.rd[k] = set()
        cum = None
        if slot is not None:
            self.slot_n[slot] = self.slot_n.get(slot, 0) + 1
            cum = self.slot_n[slot] * 16
        self.ops.append(dict(eng=eng, fn=fn, deps=sorted(deps), slot=slot,
                             cum=cum, idx=idx))
        return idx

    def dma(self, queue, fn, r=(), w=(), slot="d"):
        return self.op(queue, fn, r, w, slot=slot)

    def truncate(self, limit):
        self.ops = self.ops[:limit]
        self.slot_n = {}
        for o in self.ops:
            if o["slot"] is not None:
                self.slot_n[o["slot"]] = self.slot_n.get(o["slot"], 0) + 1
        kept = set(range(len(self.ops)))
        self.lw = {k: v for k, v in self.lw.items() if v in kept}
        self.rd = {k: (v & kept) for k, v in self.rd.items()}

    def build(self, nc, es, final_fences=()):
        ops = self.ops
        n = len(ops)
        need = [False] * n
        for o in ops:
            for d in o["deps"]:
                p = ops[d]
                if p["slot"] is None:
                    if p["eng"] == "tensor" and o["eng"] == "tensor":
                        continue
                    need[d] = True
        semval = [0] * n
        cnt = {e: 0 for e in ENGINES}
        for o in ops:
            if o["slot"] is None and need[o["idx"]]:
                cnt[o["eng"]] += 1
                semval[o["idx"]] = cnt[o["eng"]]
        sems = {}
        for e in ENGINES:
            if cnt.get(e, 0) > 0:
                sems[("e", e)] = es.enter_context(nc.semaphore(f"se_{e}"))
        for s in self.slot_n:
            sems[("s", s)] = es.enter_context(nc.semaphore(f"sd_{s}"))
        waits = [None] * n
        cov = {e: {} for e in ENGINES}
        run_slot = {}
        for o in ops:
            i = o["idx"]
            needc = {}
            for d in o["deps"]:
                p = ops[d]
                if p["slot"] is not None:
                    k = ("s", p["slot"])
                    v = run_slot.get(p["slot"], 0) * 16
                    if needc.get(k, -1) < v:
                        needc[k] = v
                elif need[d]:
                    if p["eng"] == "tensor" and o["eng"] == "tensor":
                        continue
                    k = ("e", p["eng"])
                    if needc.get(k, -1) < semval[d]:
                        needc[k] = semval[d]
            E = o["eng"]
            ww = []
            for k in sorted(needc, key=str):
                v = needc[k]
                if cov[E].get(k, -1) < v:
                    ww.append((k, v))
                    cov[E][k] = v
            waits[i] = ww
            if o["slot"] is not None:
                run_slot[o["slot"]] = run_slot.get(o["slot"], 0) + 1
        sched = {e: [] for e in ENGINES}
        for o in ops:
            sched[o["eng"]].append(o)
        block = es.enter_context(nc.Block())

        def mk(ename):
            def emitter(eng):
                for o in sched[ename]:
                    i = o["idx"]
                    for k, v in waits[i]:
                        eng.wait_ge(sems[k], v)
                    inst = o["fn"](eng)
                    if o["slot"] is not None:
                        inst.then_inc(sems[("s", o["slot"])], 16)
                    elif need[i]:
                        inst.then_inc(sems[("e", o["eng"])], 1)
                for fe, fslot in final_fences:
                    if fe == ename:
                        eng.wait_ge(sems[("s", fslot)],
                                    self.slot_n[fslot] * 16)
            return emitter

        block.tensor(mk("tensor"))
        block.vector(mk("vector"))
        block.scalar(mk("scalar"))
        block.gpsimd(mk("gpsimd"))
        block.sync(mk("sync"))


def segs(pfx, off, size, gran):
    return [(pfx, i) for i in range(off // gran, (off + size + gran - 1) // gran)]


def build_program():
    nc = bass.Bass("TRN2", target_bir_lowering=False)
    p = Prog()

    x_in = nc.dram_tensor("x_in", [4096, 1024], F32, kind="ExternalInput").ap()
    out = nc.dram_tensor("out", [4096, 1024], F32, kind="ExternalOutput").ap()
    wq0 = nc.dram_tensor("wq0", [1024, 1024], F32, kind="ExternalInput").ap()
    wk0 = nc.dram_tensor("wk0", [1024, 1024], F32, kind="ExternalInput").ap()
    wv0 = nc.dram_tensor("wv0", [1024, 1024], F32, kind="ExternalInput").ap()
    g0 = nc.dram_tensor("g0", [3072, 1024], I32, kind="ExternalInput").ap()
    u0 = nc.dram_tensor("u0", [3072, 1024], I32, kind="ExternalInput").ap()
    d0 = nc.dram_tensor("d0", [1024, 3072], I32, kind="ExternalInput").ap()
    wq1 = nc.dram_tensor("wq1", [1024, 1024], F32, kind="ExternalInput").ap()
    wk1 = nc.dram_tensor("wk1", [1024, 1024], F32, kind="ExternalInput").ap()
    wv1 = nc.dram_tensor("wv1", [1024, 1024], F32, kind="ExternalInput").ap()
    wkp1 = nc.dram_tensor("wkp1", [1024, 1024], F32, kind="ExternalInput").ap()
    wo1 = nc.dram_tensor("wo1", [1024, 1024], F32, kind="ExternalInput").ap()
    g1 = nc.dram_tensor("g1", [3072, 1024], I32, kind="ExternalInput").ap()
    u1 = nc.dram_tensor("u1", [3072, 1024], I32, kind="ExternalInput").ap()
    d1 = nc.dram_tensor("d1", [1024, 3072], I32, kind="ExternalInput").ap()
    gamma = nc.dram_tensor("gamma", [128, 1024], F32, kind="ExternalInput").ap()
    beta = nc.dram_tensor("beta", [128, 1024], F32, kind="ExternalInput").ap()
    identd = nc.dram_tensor("identd", [128, 128], BF16, kind="ExternalInput").ap()
    causald = nc.dram_tensor("causald", [128, 128], F32, kind="ExternalInput").ap()
    e1d = nc.dram_tensor("e1d", [32, 1024], BF16, kind="ExternalInput").ap()
    e2d = nc.dram_tensor("e2d", [32, 1024], BF16, kind="ExternalInput").ap()
    maskad = nc.dram_tensor("maskad", [128, 1024], F32, kind="ExternalInput").ap()

    s_att = {}
    for name, src in [("wq0", wq0), ("wk0", wk0), ("wv0", wv0),
                      ("wq1", wq1), ("wk1", wk1), ("wv1", wv1),
                      ("wkp1", wkp1), ("wo1", wo1)]:
        s_att[name] = nc.dram_tensor("s_" + name, [1024, 1024], BF16,
                                     kind="Internal").ap()
    s_ffn = {}
    for name, src in [("g0", g0), ("u0", u0), ("g1", g1), ("u1", u1)]:
        s_ffn[name] = nc.dram_tensor("s_" + name, [3072, 1024], BF16,
                                     kind="Internal").ap()
    for name, src in [("d0", d0), ("d1", d1)]:
        s_ffn[name] = nc.dram_tensor("s_" + name, [1024, 3072], BF16,
                                     kind="Internal").ap()
    wdt = [nc.dram_tensor(f"wdt{l}", [128, 24576], BF16, kind="Internal").ap()
           for l in range(2)]
    wgt = [nc.dram_tensor(f"wgt{l}", [128, 24576], BF16, kind="Internal").ap()
           for l in range(2)]
    wut = [nc.dram_tensor(f"wut{l}", [128, 24576], BF16, kind="Internal").ap()
           for l in range(2)]
    hs = [nc.dram_tensor(f"hs{g}", [128, 12288], BF16, kind="Internal").ap()
          for g in range(4)]

    es = contextlib.ExitStack()
    with es:
        AR_COLS = 55296
        A = es.enter_context(nc.sbuf_tensor("A", [128, AR_COLS], BF16))[:]
        X = es.enter_context(nc.sbuf_tensor("X", [128, 16384], F32))[:]
        FA = es.enter_context(nc.sbuf_tensor("FA", [128, 4096], F32))[:]
        GAM = es.enter_context(nc.sbuf_tensor("GAM", [128, 1024], F32))[:]
        BET = es.enter_context(nc.sbuf_tensor("BET", [128, 1024], F32))[:]
        IDN = es.enter_context(nc.sbuf_tensor("IDN", [128, 128], BF16))[:]
        CAU = es.enter_context(nc.sbuf_tensor("CAU", [128, 128], F32))[:]
        E1 = es.enter_context(nc.sbuf_tensor("E1", [32, 1024], BF16))[:]
        E2 = es.enter_context(nc.sbuf_tensor("E2", [32, 1024], BF16))[:]
        MSK4 = es.enter_context(nc.sbuf_tensor("MSK4", [128, 1024], F32))[:]
        BN1 = es.enter_context(nc.sbuf_tensor("BN1", [128, 12], F32))[:]
        BN2 = es.enter_context(nc.sbuf_tensor("BN2", [128, 12], F32))[:]
        STG = es.enter_context(nc.sbuf_tensor("STG", [128, 32], F32))[:]
        VB = es.enter_context(nc.sbuf_tensor("VB", [128, 64], F32))[:]
        QM = es.enter_context(nc.sbuf_tensor("QM", [128, 16], F32))[:]
        QR = es.enter_context(nc.sbuf_tensor("QR", [128, 16], F32))[:]
        QM2 = es.enter_context(nc.sbuf_tensor("QM2", [128, 16], F32))[:]
        QR2 = es.enter_context(nc.sbuf_tensor("QR2", [128, 16], F32))[:]
        TB = es.enter_context(nc.sbuf_tensor("TB", [128, 24], F32))[:]
        NRM = es.enter_context(nc.sbuf_tensor("NRM", [128, 48], F32))[:]
        RR16 = es.enter_context(nc.sbuf_tensor("RR16", [128, 16], F32))[:]
        PSA = es.enter_context(nc.psum_tensor("PSA", [128, 3072], F32))[:]
        PSB = es.enter_context(nc.psum_tensor("PSB", [128, 512], F32))[:]
        PST = es.enter_context(nc.psum_tensor("PST", [128, 1024], BF16))[:]

        AK = lambda off, size: segs("A", off, size, 512)
        FK = lambda off, size: segs("F", off, size, 512)
        PK = lambda b0, nb=1: [("P", b) for b in range(b0, b0 + nb)]
        TK = lambda c0, nc_=1: segs("T", c0, nc_, 1)
        VK = lambda c0, nc_=1: segs("V", c0, nc_, 1)

        p.dma("scalar", lambda e: e.dma_start(out=IDN, in_=identd[:, :]), w=[("c", "idn")], slot="cst")
        p.dma("scalar", lambda e: e.dma_start(out=CAU, in_=causald[:, :]), w=[("c", "cau")], slot="cst")
        p.dma("scalar", lambda e: e.dma_start(out=E1, in_=e1d[:, :]), w=[("c", "e1")], slot="cst")
        p.dma("scalar", lambda e: e.dma_start(out=E2, in_=e2d[:, :]), w=[("c", "e2")], slot="cst")
        p.dma("scalar", lambda e: e.dma_start(out=MSK4, in_=maskad[:, :]), w=[("c", "msk")], slot="cst")
        p.dma("scalar", lambda e: e.dma_start(out=GAM, in_=gamma[:, :]), w=[("c", "gam")], slot="cst")
        p.dma("scalar", lambda e: e.dma_start(out=BET, in_=beta[:, :]), w=[("c", "bet")], slot="cst")

        def stage(src_ap, dst_ap, srckey, dstkey, sslot):
            p.dma("gpsimd", lambda e, s=src_ap, d=dst_ap: e.dma_start(out=d[:, :], in_=s[:, :]),
                  r=[srckey], w=[dstkey], slot=sslot)

        stage(wk0, s_att["wk0"], ("i", "wk0"), ("w", "wk0"), "sw0")
        stage(wv0, s_att["wv0"], ("i", "wv0"), ("w", "wv0"), "sw0")

        def stage_rest():
            stage(wq0, s_att["wq0"], ("i", "wq0"), ("w", "wq0"), "sw0")
            stage(g0, s_ffn["g0"], ("i", "g0"), ("w", "g0"), "sw1")
            stage(u0, s_ffn["u0"], ("i", "u0"), ("w", "u0"), "sw1")
            stage(d0, s_ffn["d0"], ("i", "d0"), ("w", "d0"), "sw1")
            stage(wq1, s_att["wq1"], ("i", "wq1"), ("w", "wq1"), "sw2")
            stage(wk1, s_att["wk1"], ("i", "wk1"), ("w", "wk1"), "sw2")
            stage(wkp1, s_att["wkp1"], ("i", "wkp1"), ("w", "wkp1"), "sw2")
            stage(wv1, s_att["wv1"], ("i", "wv1"), ("w", "wv1"), "sw2")
            stage(wo1, s_att["wo1"], ("i", "wo1"), ("w", "wo1"), "sw2")
            stage(g1, s_ffn["g1"], ("i", "g1"), ("w", "g1"), "sw3")
            stage(u1, s_ffn["u1"], ("i", "u1"), ("w", "u1"), "sw3")
            stage(d1, s_ffn["d1"], ("i", "d1"), ("w", "d1"), "sw3")

        wslot_state = [0]

        def next_wslot():
            wslot_state[0] ^= 1
            return "wa" if wslot_state[0] else "wb"

        def ln_stats_batch2(src_aps, src_keys, bias_ap, rstd_ap, mkey, rkey,
                            var_scale=1.0, rho=None):
            nt = len(src_aps)
            for i in range(nt):
                sap = src_aps[i]
                bn = BN1 if i % 2 == 0 else BN2
                bkey = ("bn", i % 2)
                p.op("vector", lambda e, a=sap, b=bn: e.bn_stats(b[:, 0:6], a[:, 0:512]),
                     r=src_keys[i], w=[bkey])
                p.op("vector", lambda e, a=sap, b=bn: e.bn_stats(b[:, 6:12], a[:, 512:1024]),
                     r=src_keys[i], w=[bkey])
                p.op("vector", lambda e, b=bn, i=i: e.bn_aggr(STG[:, 2 * i:2 * i + 2], b),
                     r=[bkey], w=[("stg", i)])
            stgk = [("stg", i) for i in range(nt)]
            stg3 = STG[:, 0:2 * nt].rearrange("p (a b) -> p a b", b=2)
            p.op("vector", lambda e, nt=nt, s=stg3: e.tensor_scalar(
                out=VB[:, 0:nt], in0=s[:, :, 1], scalar1=var_scale, scalar2=EPS_LN,
                op0=ALU.mult, op1=ALU.add), r=stgk, w=VK(0, nt))
            p.op("scalar", lambda e, nt=nt: e.activation(
                out=VB[:, 16:16 + nt], in_=VB[:, 0:nt], func=AF.Ln),
                r=VK(0, nt), w=VK(16, nt))
            p.op("scalar", lambda e, nt=nt, rd=rstd_ap: e.activation(
                out=rd[:, 0:nt], in_=VB[:, 16:16 + nt], func=AF.Exp, scale=-0.5),
                r=VK(16, nt), w=[rkey])
            p.op("vector", lambda e, nt=nt, md=bias_ap, rd=rstd_ap, s=stg3: e.scalar_tensor_tensor(
                out=md[:, 0:nt], in0=s[:, :, 0], scalar=-1.0, in1=rd[:, 0:nt],
                op0=ALU.mult, op1=ALU.mult), r=stgk + [rkey], w=[mkey])
            if rho is not None:
                rho_bias, rho_ap, mkey2, rkey2 = rho
                p.op("vector", lambda e, nt=nt, s=stg3: e.tensor_scalar(
                    out=VB[:, 32:32 + nt], in0=s[:, :, 1],
                    scalar1=(1.0 + EPS_LN) * var_scale, scalar2=EPS_LN * EPS_LN,
                    op0=ALU.mult, op1=ALU.add), r=stgk, w=VK(32, nt))
                p.op("scalar", lambda e, nt=nt: e.activation(
                    out=VB[:, 48:48 + nt], in_=VB[:, 32:32 + nt], func=AF.Ln),
                    r=VK(32, nt), w=VK(48, nt))
                p.op("scalar", lambda e, nt=nt, rd=rho_ap: e.activation(
                    out=rd[:, 0:nt], in_=VB[:, 48:48 + nt], func=AF.Exp, scale=-0.5),
                    r=VK(48, nt), w=[rkey2])
                p.op("vector", lambda e, nt=nt, md=rho_bias, rd=rho_ap, s=stg3: e.scalar_tensor_tensor(
                    out=md[:, 0:nt], in0=s[:, :, 0], scalar=-1.0, in1=rd[:, 0:nt],
                    op0=ALU.mult, op1=ALU.mult), r=stgk + [rkey2], w=[mkey2])

        PSTB = PSB.bitcast(BF16)

        def transpose_tile_128(src_ap, src_keys, dst3, dst_keys, pst_base, bank=0,
                               evac="scalar"):
            PB = PST if bank == 0 else PSTB
            bkey = [("PT", 0)] if bank == 0 else [("PB", 0)]
            for cc in range(8):
                p.op("tensor", lambda e, s=src_ap, cc=cc, PB=PB: e.transpose(
                    PB[:, cc * 128:(cc + 1) * 128], s[:, cc * 128:cc * 128 + 128], IDN),
                    r=src_keys + [("c", "idn")], w=bkey)
            src3 = PB[:, 0:1024].rearrange("p (c t) -> p c t", t=128)
            if evac == "scalar":
                p.op("scalar", lambda e, s3=src3, d3=dst3: e.activation(
                    out=d3, in_=s3, func=AF.Copy),
                    r=bkey, w=dst_keys)
            else:
                p.op("vector", lambda e, s3=src3, d3=dst3: e.tensor_copy(d3, s3),
                    r=bkey, w=dst_keys)

        KT0 = 0
        V0 = 16384
        WA0 = 32768
        WB0 = 40960
        XLT0 = 49152
        XST0 = 53248
        XQ0 = 40960
        QTG = 43008
        PT0 = 47104
        P0 = 53248
        FXLT = 0
        FWGH = 16384
        FWUH = 28672
        FHTH = 40960
        FWD = 0
        FHG = 24576
        LW0 = 0
        LW1 = 8192
        LQH = 16384
        LKH = 17408
        LKPH = 18432
        LVH = 19456
        LXQT = 20480
        LQHT = 20736
        LZT = 20992
        LPHT = 21248
        LW2K = 21504
        LW2V = 29696
        LPH = 37888
        LZH = 38912
        LXQB = 39936

        FT0 = 0
        FT1 = 1024
        FT2 = 2048
        FT3 = 3072

        def load_transposed(scr_ap, wkey, dst_off, rows, cc_count, cols_per_cc,
                            slot, extra_r=()):
            for cc in range(cc_count):
                dsl = A[:, dst_off + cc * cols_per_cc: dst_off + (cc + 1) * cols_per_cc]
                src = scr_ap[0:rows, cc * 128:(cc + 1) * 128]
                p.dma("sync", lambda e, d=dsl, s=src: e.dma_start(out=d, in_=s, transpose=True),
                      r=[wkey] + list(extra_r), w=AK(dst_off + cc * cols_per_cc, cols_per_cc),
                      slot=slot)

        for s in range(N_SEQS):
            xoff = s * 2048

            for t in range(16):
                src = x_in[xoff + t * 128: xoff + (t + 1) * 128, :]
                p.dma("sync", lambda e, t=t, src=src: e.dma_start(
                    out=X[:, t * 1024:(t + 1) * 1024], in_=src),
                    r=[("i", "x")], w=[("x", t)], slot="xin")

            do_l0a = "l0a" in STAGES
            sl_k = next_wslot()
            load_transposed(s_att["wk0"], ("w", "wk0"), WA0, 1024, 8, 1024, sl_k)
            sl_v = next_wslot()
            load_transposed(s_att["wv0"], ("w", "wv0"), WB0, 1024, 8, 1024, sl_v)
            if s == 0:
                stage_rest()

            for g in range(4):
                tiles = [4 * g + tt for tt in range(4)]
                ln_stats_batch2(
                    [X[:, t * 1024:(t + 1) * 1024] for t in tiles],
                    [[("x", t)] for t in tiles],
                    QM[:, 4 * g:], QR[:, 4 * g:], ("qm", g), ("qr", g))
                for pair in range(2):
                    sts = []
                    for tt2 in range(2):
                        tt = pair * 2 + tt2
                        t = tiles[tt]
                        xst = A[:, XST0 + tt2 * 1024: XST0 + tt2 * 1024 + 1024]
                        stkey = AK(XST0 + tt2 * 1024, 1024)
                        sts.append((tt, xst, stkey))
                        p.op("scalar", lambda e, t=t, tt=tt, g=g, xst=xst: e.activation(
                            out=xst, in_=X[:, t * 1024:(t + 1) * 1024], func=AF.Identity,
                            scale=QR[:, 4 * g + tt:4 * g + tt + 1],
                            bias=QM[:, 4 * g + tt:4 * g + tt + 1]),
                            r=[("x", t), ("qm", g), ("qr", g)], w=stkey)
                    for tt, xst, stkey in sts:
                        xl = A[:, XLT0:XLT0 + 4096].rearrange("p (c w) -> p c w", w=512)
                        d3 = xl[:, :, tt * 128:tt * 128 + 128]
                        dk = [k for cc in range(8) for k in AK(XLT0 + cc * 512 + tt * 128, 128)]
                        transpose_tile_128(xst, stkey, d3, dk, 0, bank=tt % 2)
                xlk = AK(XLT0, 4096)
                for dc in range(8):
                    bank = dc % 6
                    for cc in range(8):
                        lhs = A[:, WA0 + cc * 1024 + dc * 128: WA0 + cc * 1024 + dc * 128 + 128]
                        rhs = A[:, XLT0 + cc * 512: XLT0 + (cc + 1) * 512]
                        p.op("tensor", lambda e, lhs=lhs, rhs=rhs, b=bank, cc=cc: e.matmul(
                            PSA[:, b * 512:(b + 1) * 512], lhs, rhs,
                            start=(cc == 0), stop=(cc == 7)),
                            r=AK(WA0, 8192) + xlk + ([] if cc else PK(bank)),
                            w=PK(bank))
                    dst = A[:, KT0 + dc * 2048 + g * 512: KT0 + dc * 2048 + (g + 1) * 512]
                    p.op("scalar", lambda e, d=dst, b=bank: e.activation(
                        out=d, in_=PSA[:, b * 512:(b + 1) * 512], func=AF.Copy),
                        r=PK(bank), w=AK(KT0 + dc * 2048 + g * 512, 512))
                for tsub in range(4):
                    t = 4 * g + tsub
                    for dh in range(2):
                        bank = (tsub * 2 + dh) % 6
                        for cc in range(8):
                            lhs = A[:, XLT0 + cc * 512 + tsub * 128: XLT0 + cc * 512 + tsub * 128 + 128]
                            rhs = A[:, WB0 + cc * 1024 + dh * 512: WB0 + cc * 1024 + (dh + 1) * 512]
                            p.op("tensor", lambda e, lhs=lhs, rhs=rhs, b=bank, cc=cc: e.matmul(
                                PSA[:, b * 512:(b + 1) * 512], lhs, rhs,
                                start=(cc == 0), stop=(cc == 7)),
                                r=AK(WB0, 8192) + xlk + ([] if cc else PK(bank)),
                                w=PK(bank))
                        dst = A[:, V0 + t * 1024 + dh * 512: V0 + t * 1024 + (dh + 1) * 512]
                        p.op("scalar", lambda e, d=dst, b=bank: e.activation(
                            out=d, in_=PSA[:, b * 512:(b + 1) * 512], func=AF.Copy),
                            r=PK(bank), w=AK(V0 + t * 1024 + dh * 512, 512))

            sl_q = next_wslot()
            load_transposed(s_att["wq0"], ("w", "wq0"), WA0, 1024, 8, 1024, sl_q)
            def build_qtg(g):
                for pair in range(2):
                    sts = []
                    for tt2 in range(2):
                        tt = pair * 2 + tt2
                        t = 4 * g + tt
                        xst = A[:, XQ0 + tt2 * 1024: XQ0 + tt2 * 1024 + 1024]
                        stkey = AK(XQ0 + tt2 * 1024, 1024)
                        sts.append((tt, xst, stkey))
                        p.op("scalar", lambda e, t=t, xst=xst: e.activation(
                            out=xst, in_=X[:, t * 1024:(t + 1) * 1024], func=AF.Identity,
                            scale=QR[:, t:t + 1], bias=QM[:, t:t + 1]),
                            r=[("x", t), ("qm", t // 4), ("qr", t // 4)], w=stkey)
                    for tt, xst, stkey in sts:
                        xl = A[:, XLT0:XLT0 + 4096].rearrange("p (c w) -> p c w", w=512)
                        d3 = xl[:, :, tt * 128:tt * 128 + 128]
                        dk = [k for cc in range(8) for k in AK(XLT0 + cc * 512 + tt * 128, 128)]
                        transpose_tile_128(xst, stkey, d3, dk, 0, bank=tt % 2)
                xlk = AK(XLT0, 4096)
                for dc in range(8):
                    bank = dc % 4
                    for cc in range(8):
                        lhs = A[:, WA0 + cc * 1024 + dc * 128: WA0 + cc * 1024 + dc * 128 + 128]
                        rhs = A[:, XLT0 + cc * 512: XLT0 + (cc + 1) * 512]
                        p.op("tensor", lambda e, lhs=lhs, rhs=rhs, b=bank, cc=cc: e.matmul(
                            PSA[:, b * 512:(b + 1) * 512], lhs, rhs,
                            start=(cc == 0), stop=(cc == 7)),
                            r=AK(WA0, 8192) + xlk + ([] if cc else PK(bank)),
                            w=PK(bank))
                    dst = A[:, QTG + dc * 512: QTG + (dc + 1) * 512]
                    p.op("scalar", lambda e, d=dst, b=bank: e.activation(
                        out=d, in_=PSA[:, b * 512:(b + 1) * 512], func=AF.Copy),
                        r=PK(bank), w=AK(QTG + dc * 512, 512))

            build_qtg(0)
            for qi in range(16):
                nk = 128 * (qi + 1)
                tt_q = qi % 4
                qtk = AK(QTG, 4096)
                nch = 0
                for c0 in range(0, nk, 512):
                    w_ = min(512, nk - c0)
                    bank = c0 // 512
                    for dc in range(8):
                        lhs = A[:, QTG + dc * 512 + tt_q * 128: QTG + dc * 512 + tt_q * 128 + 128]
                        rhs = A[:, KT0 + dc * 2048 + c0: KT0 + dc * 2048 + c0 + w_]
                        p.op("tensor", lambda e, lhs=lhs, rhs=rhs, c0=c0, w_=w_, dc=dc: e.matmul(
                            PSA[:, c0:c0 + w_], lhs, rhs,
                            start=(dc == 0), stop=(dc == 7)),
                            r=qtk + AK(KT0 + 0, 16384) + ([] if dc else PK(bank)),
                            w=PK(bank))
                    if c0 + w_ == nk:
                        p.op("vector", lambda e, nk=nk: e.tensor_tensor(
                            out=PSA[:, nk - 128:nk], in0=PSA[:, nk - 128:nk],
                            in1=CAU, op=ALU.add),
                            r=PK(bank) + [("c", "cau")], w=PK(bank))
                    p.op("vector", lambda e, c0=c0, w_=w_, i=nch, bank=bank: e.reduce_max(
                        TB[:, 8 + i:9 + i], PSA[:, c0:c0 + w_], axis=AX.X),
                        r=PK(bank), w=TK(8 + nch))
                    nch += 1
                p.op("vector", lambda e, n=nch: e.reduce_max(
                    TB[:, 0:1], TB[:, 8:8 + n], axis=AX.X),
                    r=[k for i in range(nch) for k in TK(8 + i)], w=TK(0))
                p.op("vector", lambda e: e.tensor_scalar(
                    out=TB[:, 1:2], in0=TB[:, 0:1], scalar1=-QK_SCALE, scalar2=None,
                    op0=ALU.mult), r=TK(0), w=TK(1))
                nbanks = (nk + 511) // 512
                for b in range(nbanks):
                    c0 = b * 512
                    w_ = min(512, nk - c0)
                    p.op("scalar", lambda e, c0=c0, w_=w_, b=b: e.activation(
                        out=A[:, P0 + c0:P0 + c0 + w_], in_=PSA[:, c0:c0 + w_],
                        func=AF.Exp, scale=QK_SCALE, bias=TB[:, 1:2],
                        accum_out=TB[:, 12 + b:13 + b]),
                        r=PK(b) + TK(1), w=AK(P0 + c0, w_) + TK(12 + b))
                p.op("vector", lambda e, n=nbanks: e.reduce_sum(
                    TB[:, 2:3], TB[:, 12:12 + n], axis=AX.X),
                    r=[k for b in range(nbanks) for k in TK(12 + b)], w=TK(2))
                p.op("vector", lambda e: e.reciprocal(TB[:, 3:4], TB[:, 2:3]),
                     r=TK(2), w=TK(3))
                if tt_q == 3 and qi < 15:
                    build_qtg(qi // 4 + 1)
                for base in range(0, qi + 1, 4):
                    grp = list(range(base, min(base + 4, qi + 1)))
                    gi = (base // 4) % 2
                    PB_ = PST if gi == 0 else PSTB
                    bk_ = [("PT", 0)] if gi == 0 else [("PB", 0)]
                    for j, kc in enumerate(grp):
                        p.op("tensor", lambda e, kc=kc, j=j, PB_=PB_: e.transpose(
                            PB_[:, j * 128:(j + 1) * 128],
                            A[:, P0 + kc * 128:P0 + (kc + 1) * 128], IDN),
                            r=AK(P0 + kc * 128, 128) + [("c", "idn")], w=bk_)
                    gw = len(grp) * 128
                    pdst = A[:, PT0 + gi * 512:PT0 + gi * 512 + gw]
                    if gi == 0:
                        p.op("scalar", lambda e, d=pdst, PB_=PB_, gw=gw: e.activation(
                            out=d, in_=PB_[:, 0:gw], func=AF.Copy),
                            r=bk_, w=AK(PT0 + gi * 512, gw))
                    else:
                        p.op("vector", lambda e, d=pdst, PB_=PB_, gw=gw: e.tensor_copy(
                            d, PB_[:, 0:gw]),
                            r=bk_, w=AK(PT0 + gi * 512, gw))
                    for j, kc in enumerate(grp):
                        pts = A[:, PT0 + gi * 512 + j * 128:PT0 + gi * 512 + (j + 1) * 128]
                        for dh in range(2):
                            rhs = A[:, V0 + kc * 1024 + dh * 512: V0 + kc * 1024 + (dh + 1) * 512]
                            p.op("tensor", lambda e, kc=kc, dh=dh, lhs=pts, rhs=rhs, qi=qi: e.matmul(
                                PSA[:, 2048 + dh * 512: 2048 + (dh + 1) * 512], lhs, rhs,
                                start=(kc == 0), stop=(kc == qi)),
                                r=AK(PT0 + gi * 512 + j * 128, 128) +
                                  AK(V0 + kc * 1024, 1024) +
                                  ([] if kc else PK(4 + dh)),
                                w=PK(4 + dh))
                p.op("vector", lambda e: e.bn_stats(BN1[:, 0:6], PSA[:, 2048:2560]),
                     r=PK(4), w=[("bn", 0)])
                p.op("vector", lambda e: e.bn_stats(BN1[:, 6:12], PSA[:, 2560:3072]),
                     r=PK(5), w=[("bn", 0)])
                p.op("vector", lambda e: e.bn_aggr(TB[:, 4:6], BN1),
                     r=[("bn", 0)], w=TK(4, 2))
                p.op("vector", lambda e: e.tensor_mul(TB[:, 6:7], TB[:, 3:4], TB[:, 3:4]),
                     r=TK(3), w=TK(6))
                p.op("vector", lambda e: e.tensor_scalar(
                    out=TB[:, 16:17], in0=TB[:, 5:6], scalar1=TB[:, 6:7], scalar2=EPS_LN,
                    op0=ALU.mult, op1=ALU.add), r=TK(5) + TK(6), w=TK(16))
                p.op("scalar", lambda e: e.activation(
                    out=TB[:, 17:18], in_=TB[:, 16:17], func=AF.Ln), r=TK(16), w=TK(17))
                p.op("scalar", lambda e: e.activation(
                    out=TB[:, 18:19], in_=TB[:, 17:18], func=AF.Exp, scale=-0.5),
                    r=TK(17), w=TK(18))
                p.op("vector", lambda e: e.tensor_mul(TB[:, 19:20], TB[:, 3:4], TB[:, 18:19]),
                     r=TK(3) + TK(18), w=TK(19))
                p.op("vector", lambda e: e.scalar_tensor_tensor(
                    out=TB[:, 20:21], in0=TB[:, 4:5], scalar=-1.0, in1=TB[:, 19:20],
                    op0=ALU.mult, op1=ALU.mult), r=TK(4) + TK(19), w=TK(20))
                p.op("scalar", lambda e: e.activation(
                    out=FA[:, FT0:FT0 + 1024], in_=PSA[:, 2048:3072],
                    func=AF.Identity, scale=TB[:, 19:20], bias=TB[:, 20:21]),
                    r=PK(4, 2) + TK(19) + TK(20), w=FK(FT0, 1024))
                p.op("gpsimd", lambda e, qi=qi: e.tensor_tensor(
                    out=X[:, qi * 1024:(qi + 1) * 1024],
                    in0=FA[:, FT0:FT0 + 1024],
                    in1=X[:, qi * 1024:(qi + 1) * 1024], op=ALU.add),
                    r=FK(FT0, 1024) + [("x", qi)], w=[("x", qi)])

            def ffn_prep_group(layer, g):
                tiles = [4 * g + tt for tt in range(4)]
                ln_stats_batch2(
                    [X[:, t * 1024:(t + 1) * 1024] for t in tiles],
                    [[("x", t)] for t in tiles],
                    QM[:, 4 * g:], QR[:, 4 * g:], ("qm", g), ("qr", g))
                for pair in range(2):
                    sts = []
                    for tt2 in range(2):
                        t = tiles[pair * 2 + tt2]
                        xst = A[:, XST0 + tt2 * 1024: XST0 + tt2 * 1024 + 1024]
                        stkey = AK(XST0 + tt2 * 1024, 1024)
                        sts.append((t, xst, stkey))
                        p.op("scalar", lambda e, t=t, g=g, xst=xst: e.activation(
                            out=xst, in_=X[:, t * 1024:(t + 1) * 1024], func=AF.Identity,
                            scale=QR[:, t:t + 1], bias=QM[:, t:t + 1]),
                            r=[("x", t), ("qm", g), ("qr", g)], w=stkey)
                    for t, xst, stkey in sts:
                        xl = A[:, FXLT:FXLT + 16384].rearrange("p (c w) -> p c w", w=2048)
                        d3 = xl[:, :, t * 128:t * 128 + 128]
                        dk = [k for cc in range(8) for k in AK(FXLT + cc * 2048 + t * 128, 128)]
                        transpose_tile_128(xst, stkey, d3, dk, 0)

            def ffn_whalf_load(layer, half):
                sgp = s_ffn["g%d" % layer]
                sup = s_ffn["u%d" % layer]
                gk, uk = ("w", "g%d" % layer), ("w", "u%d" % layer)
                slg = next_wslot()
                slu = next_wslot()
                if s == 0:
                    for cc in range(8):
                        dsl = A[:, FWGH + cc * 1536: FWGH + (cc + 1) * 1536]
                        srcq = sgp[half * 1536:(half + 1) * 1536, cc * 128:(cc + 1) * 128]
                        p.dma("sync", lambda e, d=dsl, sr=srcq: e.dma_start(
                            out=d, in_=sr, transpose=True),
                            r=[gk], w=AK(FWGH + cc * 1536, 1536), slot=slg)
                        dsl2 = A[:, FWUH + cc * 1536: FWUH + (cc + 1) * 1536]
                        srcq2 = sup[half * 1536:(half + 1) * 1536, cc * 128:(cc + 1) * 128]
                        p.dma("sync", lambda e, d=dsl2, sr=srcq2: e.dma_start(
                            out=d, in_=sr, transpose=True),
                            r=[uk], w=AK(FWUH + cc * 1536, 1536), slot=slu)
                    p.dma("gpsimd", lambda e, layer=layer, half=half: e.dma_start(
                        out=wgt[layer][:, half * 12288:(half + 1) * 12288],
                        in_=A[:, FWGH:FWGH + 12288]),
                        r=AK(FWGH, 12288), w=[("wgt", layer, half)], slot="wds")
                    p.dma("gpsimd", lambda e, layer=layer, half=half: e.dma_start(
                        out=wut[layer][:, half * 12288:(half + 1) * 12288],
                        in_=A[:, FWUH:FWUH + 12288]),
                        r=AK(FWUH, 12288), w=[("wut", layer, half)], slot="wds")
                else:
                    p.dma("sync", lambda e, layer=layer, half=half: e.dma_start(
                        out=A[:, FWGH:FWGH + 12288],
                        in_=wgt[layer][:, half * 12288:(half + 1) * 12288]),
                        r=[("wgt", layer, half)], w=AK(FWGH, 12288), slot=slg)
                    p.dma("sync", lambda e, layer=layer, half=half: e.dma_start(
                        out=A[:, FWUH:FWUH + 12288],
                        in_=wut[layer][:, half * 12288:(half + 1) * 12288]),
                        r=[("wut", layer, half)], w=AK(FWUH, 12288), slot=slu)

            def ffn_pass1_group(layer, half, g):
                if True:
                    if True:
                        hbase = FHTH + (g % 2) * 6144
                        for hc in range(12):
                            i_ = half * 48 + g * 12 + hc
                            pg = (2 * i_) % 3
                            pu = (2 * i_ + 1) % 3
                            for wsel, wcol, pp in ((0, FWGH, pg), (1, FWUH, pu)):
                                for cc in range(8):
                                    lhs = A[:, wcol + cc * 1536 + hc * 128: wcol + cc * 1536 + hc * 128 + 128]
                                    rhs = A[:, FXLT + cc * 2048 + g * 512: FXLT + cc * 2048 + g * 512 + 512]
                                    p.op("tensor", lambda e, lhs=lhs, rhs=rhs, pp=pp, cc=cc: e.matmul(
                                        PSA[:, pp * 1024: pp * 1024 + 512], lhs, rhs,
                                        start=(cc == 0), stop=(cc == 7)),
                                        r=AK(wcol + cc * 1536 + hc * 128, 128) +
                                          AK(FXLT + cc * 2048 + g * 512, 512) +
                                          ([] if cc else PK(2 * pp)),
                                        w=PK(2 * pp))
                            sgo = FT3 + (hc % 2) * 512
                            p.op("scalar", lambda e, pg=pg, sgo=sgo: e.activation(
                                out=FA[:, sgo:sgo + 512], in_=PSA[:, pg * 1024: pg * 1024 + 512],
                                func=AF.Silu, scale=DQ),
                                r=PK(2 * pg), w=FK(sgo, 512))
                            hdst = A[:, hbase + hc * 512: hbase + (hc + 1) * 512]
                            p.op("vector", lambda e, pu=pu, d=hdst, sgo=sgo: e.scalar_tensor_tensor(
                                out=d, in0=PSA[:, pu * 1024: pu * 1024 + 512],
                                scalar=DQ, in1=FA[:, sgo:sgo + 512],
                                op0=ALU.mult, op1=ALU.mult),
                                r=PK(2 * pu) + FK(sgo, 512),
                                w=AK(hbase + hc * 512, 512))
                        hsrc = A[:, hbase:hbase + 6144]
                        hdst2 = hs[g][:, half * 6144:(half + 1) * 6144]
                        p.dma("gpsimd", lambda e, sr=hsrc, d=hdst2: e.dma_start(out=d, in_=sr),
                              r=AK(hbase, 6144), w=[("hs", g, half)], slot="ho")

            def ffn_layer2(layer, prep_done=False, half0_done=False,
                           post_group=None):
                sdp = s_ffn["d%d" % layer]
                dk_ = ("w", "d%d" % layer)
                if not prep_done:
                    for g in range(4):
                        ffn_prep_group(layer, g)
                if not prep_done:
                    ffn_whalf_load(layer, 0)
                if not half0_done:
                    for g in range(4):
                        ffn_pass1_group(layer, 0, g)
                ffn_whalf_load(layer, 1)
                for g in range(4):
                    ffn_pass1_group(layer, 1, g)
                for g in range(4):
                    hgbase = FHG + (g % 2) * 12288
                    for hv in range(2):
                        sld = ("hiA0", "hiA1", "hiB0", "hiB1")[(g % 2) * 2 + hv]
                        p.dma("sync", lambda e, g=g, hb=hgbase, hv=hv: e.dma_start(
                            out=A[:, hb + hv * 6144: hb + (hv + 1) * 6144],
                            in_=hs[g][:, hv * 6144:(hv + 1) * 6144]),
                            r=[("hs", g, hv)],
                            w=AK(hgbase + hv * 6144, 6144), slot=sld)
                    if g == 0:
                        slw = next_wslot()
                        slw2 = next_wslot()
                        if s == 0:
                            for hc in range(24):
                                dsl = A[:, FWD + hc * 1024: FWD + (hc + 1) * 1024]
                                srcw = sdp[0:1024, hc * 128:(hc + 1) * 128]
                                p.dma("sync", lambda e, d=dsl, sr=srcw: e.dma_start(
                                    out=d, in_=sr, transpose=True),
                                    r=[dk_], w=AK(FWD + hc * 1024, 1024),
                                    slot=(slw if hc % 2 == 0 else slw2))
                            p.dma("gpsimd", lambda e, layer=layer: e.dma_start(
                                out=wdt[layer][:, :], in_=A[:, FWD:FWD + 24576]),
                                r=AK(FWD, 24576), w=[("wdt", layer)], slot="wds")
                        else:
                            for q in range(4):
                                eng_ = "sync" if q < 2 else "gpsimd"
                                sl_ = (slw, slw2, "wdg", "wdg2")[q]
                                p.dma(eng_, lambda e, layer=layer, q=q: e.dma_start(
                                    out=A[:, FWD + q * 6144:FWD + (q + 1) * 6144],
                                    in_=wdt[layer][:, q * 6144:(q + 1) * 6144]),
                                    r=[("wdt", layer)], w=AK(FWD + q * 6144, 6144),
                                    slot=sl_)
                    for tsub in range(4):
                        t = g * 4 + tsub
                        pp = tsub % 3
                        FTd = FT0 if tsub % 2 == 0 else FT1
                        tbb_ = 4 if tsub % 2 == 0 else 12
                        bnn = BN1 if tsub % 2 == 0 else BN2
                        bnk = ("bn", tsub % 2)
                        for dh in range(2):
                            for hc in range(24):
                                lhs = A[:, hgbase + hc * 512 + tsub * 128: hgbase + hc * 512 + tsub * 128 + 128]
                                rhs = A[:, FWD + hc * 1024 + dh * 512: FWD + hc * 1024 + (dh + 1) * 512]
                                p.op("tensor", lambda e, lhs=lhs, rhs=rhs, pp=pp, dh=dh, hc=hc: e.matmul(
                                    PSA[:, pp * 1024 + dh * 512: pp * 1024 + (dh + 1) * 512],
                                    lhs, rhs, start=(hc == 0), stop=(hc == 23)),
                                    r=AK(hgbase + hc * 512, 512) +
                                      AK(FWD + hc * 1024 + dh * 512, 512) +
                                      ([] if hc else PK(2 * pp + dh)),
                                    w=PK(2 * pp + dh))
                        p.op("vector", lambda e, pp=pp, bnn=bnn: e.bn_stats(
                            bnn[:, 0:6], PSA[:, pp * 1024: pp * 1024 + 512]),
                            r=PK(2 * pp), w=[bnk])
                        p.op("vector", lambda e, pp=pp, bnn=bnn: e.bn_stats(
                            bnn[:, 6:12], PSA[:, pp * 1024 + 512: pp * 1024 + 1024]),
                            r=PK(2 * pp + 1), w=[bnk])
                        p.op("vector", lambda e, b=tbb_, bnn=bnn: e.bn_aggr(
                            TB[:, b:b + 2], bnn), r=[bnk], w=TK(tbb_, 2))
                        p.op("vector", lambda e, b=tbb_: e.tensor_scalar(
                            out=TB[:, b + 3:b + 4], in0=TB[:, b + 1:b + 2], scalar1=DQ * DQ,
                            scalar2=EPS_LN, op0=ALU.mult, op1=ALU.add),
                            r=TK(tbb_ + 1), w=TK(tbb_ + 3))
                        p.op("scalar", lambda e, b=tbb_: e.activation(
                            out=TB[:, b + 4:b + 5], in_=TB[:, b + 3:b + 4], func=AF.Ln),
                            r=TK(tbb_ + 3), w=TK(tbb_ + 4))
                        p.op("scalar", lambda e, b=tbb_: e.activation(
                            out=TB[:, b + 5:b + 6], in_=TB[:, b + 4:b + 5], func=AF.Exp, scale=-0.5),
                            r=TK(tbb_ + 4), w=TK(tbb_ + 5))
                        p.op("vector", lambda e, b=tbb_: e.tensor_scalar(
                            out=TB[:, b + 6:b + 7], in0=TB[:, b + 5:b + 6], scalar1=DQ,
                            scalar2=None, op0=ALU.mult), r=TK(tbb_ + 5), w=TK(tbb_ + 6))
                        p.op("vector", lambda e, pp=pp, b=tbb_, FTd=FTd: e.tensor_scalar(
                            out=FA[:, FTd:FTd + 1024], in0=PSA[:, pp * 1024: pp * 1024 + 1024],
                            scalar1=TB[:, b:b + 1], scalar2=TB[:, b + 6:b + 7],
                            op0=ALU.subtract, op1=ALU.mult),
                            r=PK(2 * pp, 2) + TK(tbb_) + TK(tbb_ + 6), w=FK(FTd, 1024))
                        p.op("gpsimd", lambda e, t=t, FTd=FTd: e.tensor_tensor(
                            out=X[:, t * 1024:(t + 1) * 1024],
                            in0=FA[:, FTd:FTd + 1024],
                            in1=X[:, t * 1024:(t + 1) * 1024], op=ALU.add),
                            r=FK(FTd, 1024) + [("x", t)], w=[("x", t)])
                    if post_group is not None:
                        post_group(g)

            def l1a_stats_group(g_):
                tiles = [4 * g_ + tt for tt in range(4)]
                ln_stats_batch2(
                    [X[:, t * 1024:(t + 1) * 1024] for t in tiles],
                    [[("x", t)] for t in tiles],
                    QM[:, 4 * g_:], QR[:, 4 * g_:], ("qm", g_), ("qr", g_),
                    rho=(QM2[:, 4 * g_:], QR2[:, 4 * g_:], ("qm2", g_), ("qr2", g_)))

            ffn_layer2(0, post_group=l1a_stats_group)

            p.op("scalar", lambda e: e.activation(
                out=FA[:, FT2:FT2 + 1024], in_=X[:, 15 * 1024:16 * 1024],
                func=AF.Identity, scale=QR[:, 15:16], bias=QM[:, 15:16]),
                r=[("x", 15), ("qm", 3), ("qr", 3)], w=FK(FT2, 1024))
            p.op("scalar", lambda e: e.activation(
                out=A[96:128, LXQB:LXQB + 1024], in_=FA[96:128, FT2:FT2 + 1024],
                func=AF.Copy), r=FK(FT2, 1024), w=AK(LXQB, 1024))
            for cc in range(8):
                p.op("tensor", lambda e, cc=cc: e.transpose(
                    PST[:, cc * 128:cc * 128 + 32],
                    A[96:128, LXQB + cc * 128:LXQB + (cc + 1) * 128],
                    IDN[96:128, 96:128], tile_position=(96, 0)),
                    r=AK(LXQB, 1024) + [("c", "idn")], w=[("PT", 0)])
            src3q = PST[:, 0:1024].rearrange("p (c t) -> p c t", t=128)[:, :, 0:32]
            dst3q = A[:, LXQT:LXQT + 256].rearrange("p (c t) -> p c t", t=32)
            p.op("scalar", lambda e, s3=src3q, d3=dst3q: e.activation(
                out=d3, in_=s3, func=AF.Copy),
                r=[("PT", 0)], w=AK(LXQT, 256))
            xqtk = AK(LXQT, 256)

            def l1_proj(scr, wkey, wsl_off, dst_off, dst_key_off, l2n):
                slw = next_wslot()
                for cc in range(8):
                    dsl = A[:, wsl_off + cc * 1024: wsl_off + (cc + 1) * 1024]
                    srcw = scr[cc * 128:(cc + 1) * 128, :]
                    p.dma("sync", lambda e, d=dsl, sr=srcw: e.dma_start(out=d, in_=sr),
                          r=[wkey], w=AK(wsl_off + cc * 1024, 1024), slot=slw)
                wk_ = AK(wsl_off, 8192)
                for dh in range(2):
                    for cc in range(8):
                        lhs = A[:, LXQT + cc * 32: LXQT + (cc + 1) * 32]
                        rhs = A[:, wsl_off + cc * 1024 + dh * 512: wsl_off + cc * 1024 + (dh + 1) * 512]
                        p.op("tensor", lambda e, lhs=lhs, rhs=rhs, dh=dh, cc=cc: e.matmul(
                            PSA[0:32, dh * 512:(dh + 1) * 512], lhs, rhs,
                            start=(cc == 0), stop=(cc == 7)),
                            r=wk_ + xqtk + ([] if cc else PK(dh)), w=PK(dh))
                if not l2n:
                    p.op("scalar", lambda e, d=dst_off: e.activation(
                        out=A[0:32, d:d + 1024], in_=PSA[0:32, 0:1024], func=AF.Copy),
                        r=PK(0, 2), w=AK(dst_off, 1024))
                    return
                p.op("vector", lambda e: e.tensor_copy(FA[0:32, FT0:FT0 + 1024],
                                                       PSA[0:32, 0:1024]),
                     r=PK(0, 2), w=FK(FT0, 1024))
                p.op("scalar", lambda e: e.activation(
                    out=FA[0:32, FT1:FT1 + 1024], in_=FA[0:32, FT0:FT0 + 1024],
                    func=AF.Square), r=FK(FT0, 1024), w=FK(FT1, 1024))
                sq3 = FA[0:32, FT1:FT1 + 1024].rearrange("p (h d) -> p h d", d=64)
                p.op("vector", lambda e, s3=sq3: e.reduce_sum(
                    NRM[0:32, 0:16], s3, axis=AX.X), r=FK(FT1, 1024), w=[("nrm", 0)])
                p.op("vector", lambda e: e.tensor_scalar(
                    out=NRM[0:32, 0:16], in0=NRM[0:32, 0:16], scalar1=1e-30,
                    scalar2=None, op0=ALU.add), r=[("nrm", 0)], w=[("nrm", 0)])
                p.op("scalar", lambda e: e.activation(
                    out=NRM[0:32, 16:32], in_=NRM[0:32, 0:16], func=AF.Ln),
                    r=[("nrm", 0)], w=[("nrm", 1)])
                p.op("scalar", lambda e: e.activation(
                    out=NRM[0:32, 16:32], in_=NRM[0:32, 16:32], func=AF.Exp, scale=0.5),
                    r=[("nrm", 1)], w=[("nrm", 1)])
                p.op("vector", lambda e: e.tensor_scalar(
                    out=NRM[0:32, 16:32], in0=NRM[0:32, 16:32], scalar1=EPS_NORM,
                    scalar2=None, op0=ALU.add), r=[("nrm", 1)], w=[("nrm", 1)])
                p.op("vector", lambda e: e.reciprocal(NRM[0:32, 32:48], NRM[0:32, 16:32]),
                     r=[("nrm", 1)], w=[("nrm", 2)])
                for hh in range(16):
                    p.op("vector", lambda e, hh=hh, d=dst_off: e.tensor_scalar(
                        out=A[0:32, d + hh * 64: d + (hh + 1) * 64],
                        in0=FA[0:32, FT0 + hh * 64: FT0 + (hh + 1) * 64],
                        scalar1=NRM[0:32, 32 + hh:33 + hh], scalar2=None,
                        op0=ALU.mult),
                        r=FK(FT0, 1024) + [("nrm", 2)], w=AK(dst_off + hh * 64, 64))

            l1_proj(s_att["wq1"], ("w", "wq1"), LW0, LQH, 0, True)
            l1_proj(s_att["wk1"], ("w", "wk1"), LW1, LKH, 0, True)
            l1_proj(s_att["wkp1"], ("w", "wkp1"), LW0, LKPH, 0, True)
            l1_proj(s_att["wv1"], ("w", "wv1"), LW1, LVH, 0, False)

            for cc in range(8):
                p.op("tensor", lambda e, cc=cc: e.transpose(
                    PST[:, cc * 128:cc * 128 + 32],
                    A[0:32, LQH + cc * 128: LQH + (cc + 1) * 128],
                    IDN[0:32, 0:32]),
                    r=AK(LQH, 1024) + [("c", "idn")], w=[("PT", 0)])
            src3q = PST[:, 0:1024].rearrange("p (c t) -> p c t", t=128)[:, :, 0:32]
            dst3q = A[:, LQHT:LQHT + 256].rearrange("p (c t) -> p c t", t=32)
            p.op("scalar", lambda e, s3=src3q, d3=dst3q: e.activation(
                out=d3, in_=s3, func=AF.Copy),
                r=[("PT", 0)], w=AK(LQHT, 256))

            for cc in range(8):
                lhs1 = A[0:32, LKH + cc * 128: LKH + (cc + 1) * 128]
                for dh in range(2):
                    p.op("tensor", lambda e, lhs=lhs1, dh=dh: e.matmul(
                        PSA[:, dh * 512:(dh + 1) * 512], lhs,
                        E1[:, dh * 512:(dh + 1) * 512], start=True, stop=True),
                        r=AK(LKH, 1024) + [("c", "e1")] + PK(dh), w=PK(dh))
                lhs2 = A[0:32, LKPH + cc * 128: LKPH + (cc + 1) * 128]
                for dh in range(2):
                    p.op("tensor", lambda e, lhs=lhs2, dh=dh: e.matmul(
                        PSA[:, 1024 + dh * 512: 1024 + (dh + 1) * 512], lhs,
                        E2[:, dh * 512:(dh + 1) * 512], start=True, stop=True),
                        r=AK(LKPH, 1024) + [("c", "e2")] + PK(2 + dh), w=PK(2 + dh))
                rep1s = A[:, XST0:XST0 + 1024]
                p.op("scalar", lambda e, d=rep1s: e.activation(
                    out=d, in_=PSA[:, 0:1024], func=AF.Copy),
                    r=PK(0, 2), w=AK(XST0, 1024))
                dst = A[:, LW2K + cc * 1024: LW2K + (cc + 1) * 1024]
                p.op("vector", lambda e, d=dst, r1=rep1s: e.tensor_tensor(
                    out=d, in0=PSA[:, 1024:2048], in1=r1, op=ALU.mult),
                    r=PK(2, 2) + AK(XST0, 1024), w=AK(LW2K + cc * 1024, 1024))
            for cc in range(8):
                lhs1 = E1[:, cc * 128:(cc + 1) * 128]
                for dh in range(2):
                    p.op("tensor", lambda e, lhs=lhs1, dh=dh: e.matmul(
                        PSA[:, dh * 512:(dh + 1) * 512], lhs,
                        A[0:32, LVH:LVH + 1024][:, dh * 512:(dh + 1) * 512],
                        start=True, stop=True),
                        r=AK(LVH, 1024) + [("c", "e1")] + PK(dh), w=PK(dh))
                lhs2 = E2[:, cc * 128:(cc + 1) * 128]
                for dh in range(2):
                    p.op("tensor", lambda e, lhs=lhs2, dh=dh: e.matmul(
                        PSA[:, 1024 + dh * 512: 1024 + (dh + 1) * 512], lhs,
                        A[0:32, LVH:LVH + 1024][:, dh * 512:(dh + 1) * 512],
                        start=True, stop=True),
                        r=AK(LVH, 1024) + [("c", "e2")] + PK(2 + dh), w=PK(2 + dh))
                rep1s = A[:, XST0:XST0 + 1024]
                p.op("scalar", lambda e, d=rep1s: e.activation(
                    out=d, in_=PSA[:, 0:1024], func=AF.Copy),
                    r=PK(0, 2), w=AK(XST0, 1024))
                dst = A[:, LW2V + cc * 1024: LW2V + (cc + 1) * 1024]
                p.op("vector", lambda e, d=dst, r1=rep1s: e.tensor_tensor(
                    out=d, in0=PSA[:, 1024:2048], in1=r1, op=ALU.mult),
                    r=PK(2, 2) + AK(XST0, 1024), w=AK(LW2V + cc * 1024, 1024))

            LP4S = 40960
            for b in range(4):
                for j in range(4):
                    hh = 4 * b + j
                    cc = hh // 2
                    po = (hh % 2) * 64
                    lhs = A[:, LQHT + cc * 32: LQHT + (cc + 1) * 32][po:po + 64, :]
                    for dh in range(2):
                        rhs = A[:, LW2K + cc * 1024 + dh * 512: LW2K + cc * 1024 + (dh + 1) * 512][po:po + 64, :]
                        p.op("tensor", lambda e, lhs=lhs, rhs=rhs, dh=dh, j=j, po=po: e.matmul(
                            PSA[32 * j:32 * j + 32, dh * 512:(dh + 1) * 512], lhs, rhs,
                            start=True, stop=True, tile_position=(po, 32 * j)),
                            r=AK(LQHT, 256) + AK(LW2K, 8192) + PK(dh), w=PK(dh))
                p.op("vector", lambda e: e.tensor_tensor(
                    out=PSA[:, 1024:2048], in0=PSA[:, 0:1024], in1=MSK4,
                    op=ALU.add), r=PK(0, 2) + [("c", "msk")], w=PK(2, 2))
                p.op("vector", lambda e: e.reduce_max(
                    TB[:, 0:1], PSA[:, 1024:2048], axis=AX.X),
                    r=PK(2, 2), w=TK(0))
                p.op("vector", lambda e: e.tensor_scalar(
                    out=TB[:, 1:2], in0=TB[:, 0:1], scalar1=-A_SCALE,
                    scalar2=None, op0=ALU.mult), r=TK(0), w=TK(1))
                p.op("scalar", lambda e: e.activation(
                    out=A[:, LPH:LPH + 1024], in_=PSA[:, 1024:2048],
                    func=AF.Exp, scale=A_SCALE, bias=TB[:, 1:2],
                    accum_out=TB[:, 2:3]),
                    r=PK(2, 2) + TK(1), w=AK(LPH, 1024) + TK(2))
                p.op("vector", lambda e: e.reciprocal(
                    TB[:, 3:4], TB[:, 2:3]), r=TK(2), w=TK(3))
                p.op("vector", lambda e: e.tensor_scalar(
                    out=A[:, LP4S:LP4S + 1024], in0=A[:, LPH:LPH + 1024],
                    scalar1=TB[:, 3:4], scalar2=None, op0=ALU.mult),
                    r=AK(LPH, 1024) + TK(3), w=AK(LP4S, 1024))
                for j in range(4):
                    hh = 4 * b + j
                    for c2 in range(8):
                        p.op("tensor", lambda e, c2=c2, j=j: e.transpose(
                            PST[:, c2 * 128:c2 * 128 + 32],
                            A[32 * j:32 * j + 32, LP4S + c2 * 128: LP4S + (c2 + 1) * 128],
                            IDN[32 * j:32 * j + 32, 32 * j:32 * j + 32],
                            tile_position=(32 * j, 0)),
                            r=AK(LP4S, 1024) + [("c", "idn")], w=[("PT", 0)])
                    src3p = PST[:, 0:1024].rearrange("p (c t) -> p c t", t=128)[:, :, 0:32]
                    dst3p = A[:, LPHT:LPHT + 256].rearrange("p (c t) -> p c t", t=32)
                    p.op("scalar", lambda e, s3=src3p, d3=dst3p: e.activation(
                        out=d3, in_=s3, func=AF.Copy),
                        r=[("PT", 0)], w=AK(LPHT, 256))
                    zc = (hh % 8) * 64
                    for c2 in range(8):
                        lhs = A[:, LPHT + c2 * 32: LPHT + (c2 + 1) * 32]
                        rhs = A[:, LW2V + c2 * 1024 + hh * 64: LW2V + c2 * 1024 + (hh + 1) * 64]
                        p.op("tensor", lambda e, lhs=lhs, rhs=rhs, zc=zc, c2=c2: e.matmul(
                            PSB[0:32, zc:zc + 64], lhs, rhs,
                            start=(c2 == 0), stop=(c2 == 7)),
                            r=AK(LPHT, 256) + AK(LW2V, 8192) + ([] if c2 else [("PB", 0)]),
                            w=[("PB", 0)])
                if b % 2 == 1:
                    zb = (b // 2) * 512
                    p.op("scalar", lambda e, zb=zb: e.activation(
                        out=A[0:32, LZH + zb: LZH + zb + 512], in_=PSB[0:32, 0:512],
                        func=AF.Copy),
                        r=[("PB", 0)], w=AK(LZH + zb, 512))

            for cc in range(8):
                p.op("tensor", lambda e, cc=cc: e.transpose(
                    PST[:, cc * 128:cc * 128 + 32],
                    A[0:32, LZH + cc * 128: LZH + (cc + 1) * 128],
                    IDN[0:32, 0:32]),
                    r=AK(LZH, 1024) + [("c", "idn")], w=[("PT", 0)])
            src3z = PST[:, 0:1024].rearrange("p (c t) -> p c t", t=128)[:, :, 0:32]
            dst3z = A[:, LZT:LZT + 256].rearrange("p (c t) -> p c t", t=32)
            p.op("scalar", lambda e, s3=src3z, d3=dst3z: e.activation(
                out=d3, in_=s3, func=AF.Copy),
                r=[("PT", 0)], w=AK(LZT, 256))
            slw = next_wslot()
            for cc in range(8):
                dsl = A[:, LW0 + cc * 1024: LW0 + (cc + 1) * 1024]
                srcw = s_att["wo1"][cc * 128:(cc + 1) * 128, :]
                p.dma("sync", lambda e, d=dsl, sr=srcw: e.dma_start(out=d, in_=sr),
                      r=[("w", "wo1")], w=AK(LW0 + cc * 1024, 1024), slot=slw)
            for dh in range(2):
                for cc in range(8):
                    lhs = A[:, LZT + cc * 32: LZT + (cc + 1) * 32]
                    rhs = A[:, LW0 + cc * 1024 + dh * 512: LW0 + cc * 1024 + (dh + 1) * 512]
                    p.op("tensor", lambda e, lhs=lhs, rhs=rhs, dh=dh, cc=cc: e.matmul(
                        PSA[96:128, 2048 + dh * 512: 2048 + (dh + 1) * 512], lhs, rhs,
                        start=(cc == 0), stop=(cc == 7), tile_position=(0, 96)),
                        r=AK(LZT, 256) + AK(LW0, 8192) + ([] if cc else PK(4 + dh)),
                        w=PK(4 + dh))
            p.op("vector", lambda e: e.tensor_tensor(
                out=FA[96:128, FT2:FT2 + 1024], in0=PSA[96:128, 2048:3072],
                in1=FA[96:128, FT2:FT2 + 1024], op=ALU.add),
                r=PK(4, 2) + FK(FT2, 1024), w=FK(FT2, 1024))
            ffn_whalf_load(1, 0)

            chain_tiles = [15] + list(range(15))

            def chain_bufs(pos):
                par = pos % 2
                S = FT2 if par == 0 else FT3
                W = FT0 if par == 0 else FT1
                tb0 = 4 if par == 0 else 12
                bn = BN1 if par == 0 else BN2
                return par, S, W, tb0, bn

            def chain_rstd(tb0):
                tbv, tbl, tbr, tbb = tb0 + 3, tb0 + 4, tb0 + 5, tb0 + 7
                p.op("vector", lambda e: e.tensor_scalar(
                    out=TB[:, tbv:tbv + 1], in0=TB[:, tb0 + 1:tb0 + 2],
                    scalar1=1.0, scalar2=EPS_LN,
                    op0=ALU.mult, op1=ALU.add), r=TK(tb0 + 1), w=TK(tbv))
                p.op("scalar", lambda e: e.activation(
                    out=TB[:, tbl:tbl + 1], in_=TB[:, tbv:tbv + 1], func=AF.Ln),
                    r=TK(tbv), w=TK(tbl))
                p.op("scalar", lambda e: e.activation(
                    out=TB[:, tbr:tbr + 1], in_=TB[:, tbl:tbl + 1], func=AF.Exp,
                    scale=-0.5), r=TK(tbl), w=TK(tbr))
                p.op("vector", lambda e: e.scalar_tensor_tensor(
                    out=TB[:, tbb:tbb + 1], in0=TB[:, tb0:tb0 + 1], scalar=-1.0,
                    in1=TB[:, tbr:tbr + 1],
                    op0=ALU.mult, op1=ALU.mult), r=TK(tb0) + TK(tbr), w=TK(tbb))

            def chain_stageA(pos):
                t = chain_tiles[pos]
                par, S, W, tb0, bn = chain_bufs(pos)
                tbr, tbb = tb0 + 5, tb0 + 7
                Sk, Wk_ = FK(S, 1024), FK(W, 1024)
                bkey = ("bn", par)
                if t == 15:
                    p.op("vector", lambda e, S=S, b=bn: e.bn_stats(
                        b[:, 0:6], FA[:, S:S + 512]), r=Sk, w=[bkey])
                    p.op("vector", lambda e, S=S, b=bn: e.bn_stats(
                        b[:, 6:12], FA[:, S + 512:S + 1024]), r=Sk, w=[bkey])
                    p.op("vector", lambda e, b=bn, tb0=tb0: e.bn_aggr(
                        TB[:, tb0:tb0 + 2], b), r=[bkey], w=TK(tb0, 2))
                    chain_rstd(tb0)
                    p.op("scalar", lambda e, S=S, W=W, tbr=tbr, tbb=tbb: e.activation(
                        out=FA[:, W:W + 1024], in_=FA[:, S:S + 1024],
                        func=AF.Identity, scale=TB[:, tbr:tbr + 1],
                        bias=TB[:, tbb:tbb + 1]),
                        r=Sk + TK(tbr) + TK(tbb), w=Wk_)
                else:
                    p.op("scalar", lambda e, t=t, W=W: e.activation(
                        out=FA[:, W:W + 1024], in_=X[:, t * 1024:(t + 1) * 1024],
                        func=AF.Identity, scale=QR2[:, t:t + 1], bias=QM2[:, t:t + 1]),
                        r=[("x", t), ("qm2", t // 4), ("qr2", t // 4)], w=Wk_)
                p.op("vector", lambda e, S=S, W=W: e.tensor_mul(
                    FA[:, S:S + 1024], FA[:, W:W + 1024], GAM),
                    r=Wk_ + [("c", "gam")], w=Sk)
                p.op("vector", lambda e, S=S, W=W: e.tensor_tensor(
                    out=FA[:, W:W + 1024], in0=FA[:, S:S + 1024],
                    in1=BET, op=ALU.add),
                    r=Sk + [("c", "bet")], w=Wk_)
                p.op("vector", lambda e, W=W, b=bn: e.bn_stats(
                    b[:, 0:6], FA[:, W:W + 512]), r=Wk_, w=[bkey])
                p.op("vector", lambda e, W=W, b=bn: e.bn_stats(
                    b[:, 6:12], FA[:, W + 512:W + 1024]), r=Wk_, w=[bkey])
                p.op("vector", lambda e, b=bn, tb0=tb0: e.bn_aggr(
                    TB[:, tb0:tb0 + 2], b), r=[bkey], w=TK(tb0, 2))
                chain_rstd(tb0)

            def chain_stageB(pos):
                t = chain_tiles[pos]
                par, S, W, tb0, bn = chain_bufs(pos)
                tbr, tbb = tb0 + 5, tb0 + 7
                Sk, Wk_ = FK(S, 1024), FK(W, 1024)
                p.op("scalar", lambda e, S=S, W=W, tbr=tbr, tbb=tbb: e.activation(
                    out=FA[:, S:S + 1024], in_=FA[:, W:W + 1024],
                    func=AF.Identity, scale=TB[:, tbr:tbr + 1], bias=TB[:, tbb:tbb + 1]),
                    r=Wk_ + TK(tbr) + TK(tbb), w=Sk)
                p.op("vector", lambda e, t=t, S=S: e.tensor_tensor(
                    out=X[:, t * 1024:(t + 1) * 1024], in0=FA[:, S:S + 1024],
                    in1=X[:, t * 1024:(t + 1) * 1024], op=ALU.add),
                    r=Sk + [("x", t)], w=[("x", t)])
                if t in (3, 7, 11, 14):
                    g_done = 3 if t == 14 else t // 4
                    ffn_prep_group(1, g_done)
                p.op("tensor", lambda e: e.matmul(
                    PSB[:, 0:128], IDN, IDN, start=True, stop=True),
                    r=[("c", "idn")], w=[("PB", 0)])

            CHAIN_PIPE = True
            if CHAIN_PIPE:
                chain_stageA(0)
                for pos in range(16):
                    if pos + 1 < 16:
                        chain_stageA(pos + 1)
                    chain_stageB(pos)
            else:
                for pos in range(16):
                    chain_stageA(pos)
                    chain_stageB(pos)

            def store_group(g_):
                for t in range(4 * g_, 4 * g_ + 4):
                    dst = out[xoff + t * 128: xoff + (t + 1) * 128, :]
                    p.dma("gpsimd", lambda e, t=t, dst=dst: e.dma_start(
                        out=dst, in_=X[:, t * 1024:(t + 1) * 1024]),
                        r=[("x", t)], slot="out")

            ffn_layer2(1, prep_done=True, post_group=store_group)

        if OP_LIMIT is not None:
            p.truncate(OP_LIMIT)
            for t in range(16):
                dst = out[t * 128:(t + 1) * 128, :]
                p.dma("gpsimd", lambda e, t=t, dst=dst: e.dma_start(
                    out=dst, in_=X[:, t * 1024:(t + 1) * 1024]), r=(), slot="out")
        p.build(nc, es, final_fences=[("gpsimd", "out")])
    return nc


_CACHE = {}


def kernel(**inputs):
    if "nc" not in _CACHE:
        _CACHE["nc"] = build_program()
    nc = _CACHE["nc"]
    x = np.ascontiguousarray(np.asarray(inputs["L0_attn_x"], dtype=np.float32))
    gam = np.ascontiguousarray(np.broadcast_to(
        np.asarray(inputs["L1_attn_ln_gamma"], dtype=np.float32), (128, 1024)))
    bet = np.ascontiguousarray(np.broadcast_to(
        np.asarray(inputs["L1_attn_ln_beta"], dtype=np.float32), (128, 1024)))
    ident = np.eye(128, dtype=ml_dtypes.bfloat16)
    causal = (np.triu(np.ones((128, 128), np.float32), k=1) * NEG).astype(np.float32)
    jj = np.arange(1024) // 32
    kk = np.arange(1024) % 32
    ss = np.arange(32)[:, None]
    maska = np.where((jj[None, :] > ss) | (kk[None, :] > ss), NEG, 0.0).astype(np.float32)
    maska = np.ascontiguousarray(np.tile(maska, (4, 1)))
    e1 = (jj[None, :] == np.arange(32)[:, None]).astype(ml_dtypes.bfloat16)
    e2 = (kk[None, :] == np.arange(32)[:, None]).astype(ml_dtypes.bfloat16)

    common = {
        "wq0": np.asarray(inputs["L0_attn_Wq"], np.float32),
        "wk0": np.asarray(inputs["L0_attn_Wk"], np.float32),
        "wv0": np.asarray(inputs["L0_attn_Wv"], np.float32),
        "g0": np.asarray(inputs["L0_ffn_w_gate"], np.int32),
        "u0": np.asarray(inputs["L0_ffn_w_up"], np.int32),
        "d0": np.asarray(inputs["L0_ffn_w_down"], np.int32),
        "wq1": np.asarray(inputs["L1_attn_W_Q"], np.float32),
        "wk1": np.asarray(inputs["L1_attn_W_K"], np.float32),
        "wv1": np.asarray(inputs["L1_attn_W_V"], np.float32),
        "wkp1": np.asarray(inputs["L1_attn_W_Kp"], np.float32),
        "wo1": np.asarray(inputs["L1_attn_W_O"], np.float32),
        "g1": np.asarray(inputs["L1_ffn_w_gate"], np.int32),
        "u1": np.asarray(inputs["L1_ffn_w_up"], np.int32),
        "d1": np.asarray(inputs["L1_ffn_w_down"], np.int32),
        "gamma": gam, "beta": bet, "identd": ident, "causald": causal,
        "e1d": e1, "e2d": e2, "maskad": maska,
    }
    in_maps = []
    for c in range(8):
        m = dict(common)
        m["x_in"] = np.ascontiguousarray(x[2 * c:2 * c + 2].reshape(4096, 1024))
        in_maps.append(m)
    res = run_bass_kernel_spmd(nc, in_maps, core_ids=list(range(8)))
    outs = [res.results[c]["out"].reshape(2, 2048, 1024) for c in range(8)]
    return np.concatenate(outs, axis=0)
